# version 01
# baseline (speedup 1.0000x reference)
import numpy as np
import ml_dtypes
from contextlib import ExitStack
import concourse.bass as bass
import concourse.mybir as mybir
from concourse.bass_utils import run_bass_kernel_spmd

F32 = mybir.dt.float32
BF16 = mybir.dt.bfloat16
I32 = mybir.dt.int32
AF = mybir.ActivationFunctionType
ALU = mybir.AluOpType
AX = mybir.AxisListType
NPBF16 = ml_dtypes.bfloat16

NCORE = 8
TC = 2048
D = 1024
DI = 2048
NS = 16
CAP = 4096
EXP = 8
LN_EPS = 1e-5


ENGINES = ("sync", "gpsimd", "vector", "scalar", "tensor")


class Buf:
    __slots__ = ("h", "writer", "readers", "name", "kb", "cm")

    def __init__(self, kb, h, name):
        self.kb = kb
        self.h = h
        self.name = name
        self.cm = None
        self.writer = None
        self.readers = []

    def __getitem__(self, idx):
        return self.h[idx]

    def ap(self, offset, pattern):
        return bass.AP(self.h, offset, pattern)


class KB:
    def __init__(self, name="k"):
        self.nc = bass.Bass("TRN2", target_bir_lowering=False)
        self.name = name
        self.ctx = ExitStack()
        self.thunks = {e: [] for e in ENGINES}
        self.semh = {}
        self.ctr = {}
        self.waited = {e: {} for e in ENGINES}
        self._constbuf = None
        self._consts = {}
        self._open = []
        self._freed_tokens = []

    def sem(self, name):
        if name not in self.semh:
            self.semh[name] = self.ctx.enter_context(self.nc.semaphore(name))
            self.ctr[name] = 0
        return self.semh[name]

    def sbuf(self, name, shape, dtype=F32, side=None):
        if side is None:
            cm = self.nc.sbuf_tensor(name, list(shape), dtype)
        else:
            cm = self.nc.sbuf_tensor(name, list(shape), dtype, side=side)
        h = cm.__enter__()
        b = Buf(self, h, name)
        b.cm = cm
        b.readers = list(self._freed_tokens)
        self._open.append(b)
        return b

    def free(self, *bufs):
        todo = [b for b in bufs if b.cm is not None]
        todo.sort(key=lambda b: self._open.index(b), reverse=True)
        for b in todo:
            if b.writer:
                self._freed_tokens.append(b.writer)
            self._freed_tokens.extend(b.readers)
            b.cm.__exit__(None, None, None)
            b.cm = None
            self._open.remove(b)

    def psum(self, name, shape, dtype=F32):
        h = self.ctx.enter_context(self.nc.psum_tensor(name, list(shape), dtype))
        return Buf(self, h, name)

    def dram(self, name, shape, dtype=F32, kind=None):
        if kind is None:
            t = self.nc.dram_tensor(name, list(shape), dtype)
        else:
            t = self.nc.dram_tensor(name, list(shape), dtype, kind=kind)
        return Buf(self, t, name)

    def inp(self, name, shape, dtype=F32):
        return self.dram(name, shape, dtype, kind="ExternalInput")

    def out(self, name, shape, dtype=F32):
        return self.dram(name, shape, dtype, kind="ExternalOutput")

    def constcol(self, value):
        if self._constbuf is None:
            self._constbuf = self.sbuf("_consts", [128, 16])
        if value not in self._consts:
            ci = len(self._consts)
            assert ci < 16
            self._consts[value] = ci
            cb = self._constbuf
            self.op("vector", lambda q, ci=ci, value=value: q.memset(
                cb.h[:, ci:ci + 1], value), reads=[], writes=[cb])
        ci = self._consts[value]
        return self._constbuf.h[:, ci:ci + 1], self._constbuf

    def _deps(self, reads, writes, extra):
        deps = []
        for b in reads:
            if b.writer:
                deps.append(b.writer)
        for b in writes:
            if b.writer:
                deps.append(b.writer)
            deps.extend(b.readers)
        deps.extend(extra)
        return deps

    def _emit(self, eng, fn, deps, sem, inc):
        self.sem(sem)
        needed = []
        for (s, v) in deps:
            if v > self.waited[eng].get(s, 0):
                needed.append((s, v))
                self.waited[eng][s] = v
        self.ctr[sem] += inc
        val = self.ctr[sem]
        semh = self.semh

        def thunk(q, needed=tuple(needed), fn=fn, sem=sem, inc=inc):
            for (s, v) in needed:
                q.wait_ge(semh[s], v)
            inst = fn(q)
            if inst is not None:
                inst.then_inc(semh[sem], inc)
        self.thunks[eng].append(thunk)
        return (sem, val)

    def _commit(self, tok, reads, writes):
        for b in writes:
            b.writer = tok
            b.readers = []
        for b in reads:
            b.readers.append(tok)

    def op(self, eng, fn, reads=(), writes=(), after=()):
        tok = self._emit(eng, fn, self._deps(reads, writes, after), "_" + eng, 1)
        self._commit(tok, reads, writes)
        return tok

    def dma(self, q, out_ap, in_ap, reads=(), writes=(), after=(), stream=None, **kw):
        assert writes, "dma needs writes for stream naming"
        sname = stream or ("d_" + writes[0].name)

        def fn(qq, out_ap=out_ap, in_ap=in_ap, kw=kw):
            return qq.dma_start(out=out_ap, in_=in_ap, **kw)
        tok = self._emit(q, fn, self._deps(reads, writes, after), sname, 16)
        self._commit(tok, reads, writes)
        return tok

    def idma(self, out_ap, in_ap, out_off=None, in_off=None, bounds=None,
             reads=(), writes=(), after=(), stream=None):
        sname = stream or ("d_" + writes[0].name)

        def fn(qq):
            return qq.indirect_dma_start(
                out_ap, out_off, in_ap, in_off,
                bounds_check=bounds, oob_is_err=False)
        tok = self._emit("gpsimd", fn, self._deps(reads, writes, after), sname, 16)
        self._commit(tok, reads, writes)
        return tok

    def finalize(self):
        final_waits = [(s, v) for s, v in self.ctr.items()
                       if s.startswith("d_") and v > 0]
        semh = self.semh

        def thunk(q, fw=tuple(final_waits)):
            for (s, v) in fw:
                q.wait_ge(semh[s], v)
        self.thunks["gpsimd"].append(thunk)

        for b in reversed(list(self._open)):
            self.free(b)
        with self.nc.Block() as block:
            for ename in ENGINES:
                ths = self.thunks[ename]
                if not ths:
                    continue

                def make(ths):
                    def section(q):
                        for t in ths:
                            t(q)
                    return section
                getattr(block, ename)(make(ths))
        self.ctx.close()
        return self.nc



def e_layernorm(kb, x_buf, x_ap, stat, sq, dmodel=D):
    st = stat.h
    kb.op("scalar", lambda q: q.activation(out=sq[0][:, :], in_=x_ap, func=AF.Square),
          reads=[x_buf], writes=[sq[0]])
    kb.op("vector", lambda q: q.reduce_sum(out=st[:, 0:1], in_=x_ap, axis=AX.X),
          reads=[x_buf], writes=[stat])
    kb.op("vector", lambda q: q.reduce_sum(out=st[:, 1:2], in_=sq[0][:, :], axis=AX.X),
          reads=[sq[0]], writes=[stat])
    kb.op("vector", lambda q: q.tensor_scalar(
        out=st[:, 2:4], in0=st[:, 0:2], scalar1=1.0 / dmodel, scalar2=None,
        op0=ALU.mult), reads=[stat], writes=[stat])
    kb.op("vector", lambda q: q.tensor_tensor(
        out=st[:, 4:5], in0=st[:, 2:3], in1=st[:, 2:3], op=ALU.mult),
        reads=[stat], writes=[stat])
    kb.op("vector", lambda q: q.tensor_tensor(
        out=st[:, 5:6], in0=st[:, 3:4], in1=st[:, 4:5], op=ALU.subtract),
        reads=[stat], writes=[stat])
    eps_ap, cb0 = kb.constcol(LN_EPS)
    mh_ap, _ = kb.constcol(-0.5)
    kb.op("scalar", lambda q: q.activation(
        out=st[:, 6:7], in_=st[:, 5:6], func=AF.Ln, bias=eps_ap),
        reads=[stat, cb0], writes=[stat])
    kb.op("scalar", lambda q: q.activation(
        out=st[:, 7:8], in_=st[:, 6:7], func=AF.Exp, scale=mh_ap),
        reads=[stat, cb0], writes=[stat])
    return st[:, 2:3], st[:, 7:8]


def matmul_group(kb, ps, ps_ap, pairs, reads):
    def fn(q):
        last = None
        n = len(pairs)
        for i, (lt, rt) in enumerate(pairs):
            last = q.matmul(ps_ap, lt, rt, start=(i == 0), stop=(i == n - 1))
        return last
    return kb.op("tensor", fn, reads=list(reads), writes=[ps])



TP = 2176
NT = TP // 128
CHUNKS5 = [(0, 512), (512, 512), (1024, 512), (1536, 512), (2048, 128)]
CHUNKS4 = [(0, 512), (512, 512), (1024, 512), (1536, 512)]


def build_l0a_pre():
    kb = KB("l0a_pre")
    x0p = kb.inp("x0p", [TP, D])
    W_inT = kb.inp("W_inT", [D, 2 * DI])
    cw4 = kb.inp("cw4", [128, 64])
    cbv = kb.inp("cbv", [128, 16])
    WxCT = kb.inp("WxCT", [128, 16 * NS])
    BlogT = kb.inp("BlogT", [NS, DI])

    o_xln = kb.out("o_xln", [TC, D])
    o_xc = kb.out("o_xc", [DI, TC], BF16)
    o_cb = kb.out("o_cb", [DI, TC], BF16)
    o_sz = kb.out("o_sz", [DI, TC], BF16)

    xlnb_dram = kb.dram("xlnb_dram", [TP, D], BF16)

    xin = [kb.sbuf(f"xin{i}", [128, D]) for i in range(3)]
    xlnf = [kb.sbuf(f"xlnf{i}", [128, D]) for i in range(3)]
    sq = [kb.sbuf("sq", [128, D])]
    stat = kb.sbuf("stat", [128, 8])
    for j in range(NT):
        bi = xin[j % 3]
        bo = xlnf[j % 3]
        kb.dma("sync", bi[:, :], x0p.ap(j * 128 * D, [[D, 128], [1, D]]),
               reads=[x0p], writes=[bi], stream=f"d_xin{j % 3}")
        mu_ap, rstd_ap = e_layernorm(kb, bi, bi[:, :], stat, sq)
        kb.op("vector", lambda q, bi=bi, bo=bo, mu_ap=mu_ap, rstd_ap=rstd_ap:
              q.tensor_scalar(out=bo[:, :], in0=bi[:, :], scalar1=mu_ap,
                              scalar2=rstd_ap, op0=ALU.subtract, op1=ALU.mult),
              reads=[bi, stat], writes=[bo])
        if j >= 1:
            kb.dma("sync", o_xln.ap((j - 1) * 128 * D, [[D, 128], [1, D]]),
                   bo[:, :], reads=[bo], writes=[o_xln], stream=f"d_oxln{j % 3}")
        kb.dma("gpsimd", xlnb_dram.ap(j * 128 * D, [[D, 128], [1, D]]),
               bo[:, :], reads=[bo], writes=[xlnb_dram], stream=f"d_xlnb{j % 3}")

    xlnT = [kb.sbuf(f"xlnT{k}", [128, TP], BF16) for k in range(8)]
    for kd in range(8):
        kb.dma("sync", xlnT[kd][:, :],
               xlnb_dram.ap(kd * 128, [[D, TP], [1, 128]]),
               reads=[xlnb_dram], writes=[xlnT[kd]], transpose=True)

    wbuf = kb.sbuf("wbuf", [128, 8 * DI], BF16)
    kb.dma("gpsimd",
           bass.AP(wbuf.h, 0, [[8 * DI, 128], [DI, 8], [1, DI]]),
           W_inT.ap(0, [[2 * DI, 128], [128 * 2 * DI, 8], [1, DI]]),
           reads=[W_inT], writes=[wbuf])

    cwb = kb.sbuf("cwb", [128, 16 * 4])
    kb.dma("sync", cwb[:, :], cw4.ap(0, [[64, 128], [1, 64]]),
           reads=[cw4], writes=[cwb])
    cbb = kb.sbuf("cbb", [128, 16])
    kb.dma("sync", cbb[:, :], cbv.ap(0, [[16, 128], [1, 16]]),
           reads=[cbv], writes=[cbb])

    xc = [kb.sbuf(f"xct{m}", [128, TC], BF16) for m in range(16)]
    ps = [kb.psum(f"ps{i}", [128, 512]) for i in range(4)]
    xi_m = [kb.sbuf(f"xi_m{i}", [128, TP], BF16) for i in range(2)]
    convb = [kb.sbuf(f"convb{i}", [128, TC]) for i in range(2)]
    for m in range(16):
        xim = xi_m[m % 2]
        cvb = convb[m % 2]
        for ci, (t0, nt) in enumerate(CHUNKS5):
            p = ps[ci % 4]
            pairs = [(wbuf.ap(kd * DI + m * 128, [[8 * DI, 128], [1, 128]]),
                      xlnT[kd][:, t0:t0 + nt]) for kd in range(8)]
            matmul_group(kb, p, p.h[:, 0:nt], pairs, [wbuf] + xlnT)
            kb.op("vector", lambda q, p=p, xim=xim, t0=t0, nt=nt:
                  q.tensor_copy(out=xim[:, t0:t0 + nt], in_=p.h[:, 0:nt]),
                  reads=[p], writes=[xim])
        kb.op("vector", lambda q, xim=xim, cvb=cvb, m=m: q.tensor_scalar(
            out=cvb[:, :], in0=xim[:, 125:125 + TC],
            scalar1=cwb.h[:, m * 4:m * 4 + 1], scalar2=None, op0=ALU.mult),
            reads=[xim, cwb], writes=[cvb])
        for tap in range(1, 4):
            kb.op("vector", lambda q, xim=xim, cvb=cvb, m=m, tap=tap:
                  q.scalar_tensor_tensor(
                      out=cvb[:, :], in0=xim[:, 125 + tap:125 + tap + TC],
                      scalar=cwb.h[:, m * 4 + tap:m * 4 + tap + 1],
                      in1=cvb[:, :], op0=ALU.mult, op1=ALU.add),
                  reads=[xim, cwb, cvb], writes=[cvb])
        kb.op("scalar", lambda q, cvb=cvb, m=m: q.activation(
            out=xc[m][:, :], in_=cvb[:, :], func=AF.Silu,
            bias=cbb.h[:, m:m + 1]), reads=[cvb, cbb], writes=[xc[m]])
        kb.dma("sync", o_xc.ap(m * 128 * TC, [[TC, 128], [1, TC]]),
               xc[m][:, :], reads=[xc[m]], writes=[o_xc], stream="d_oxc")

    kb.dma("gpsimd",
           bass.AP(wbuf.h, 0, [[8 * DI, 128], [DI, 8], [1, DI]]),
           W_inT.ap(DI, [[2 * DI, 128], [128 * 2 * DI, 8], [1, DI]]),
           reads=[W_inT], writes=[wbuf])
    szt = [kb.sbuf(f"szt{i}", [128, 512], BF16) for i in range(2)]
    for m in range(16):
        for ci, (t0, nt) in enumerate(CHUNKS4):
            p = ps[ci % 4]
            s = szt[ci % 2]
            pairs = [(wbuf.ap(kd * DI + m * 128, [[8 * DI, 128], [1, 128]]),
                      xlnT[kd][:, 128 + t0:128 + t0 + nt]) for kd in range(8)]
            matmul_group(kb, p, p.h[:, 0:nt], pairs, [wbuf] + xlnT)
            kb.op("scalar", lambda q, p=p, s=s, nt=nt: q.activation(
                out=s[:, 0:nt], in_=p.h[:, 0:nt], func=AF.Silu),
                reads=[p], writes=[s])
            kb.dma("sync", o_sz.ap(m * 128 * TC + t0, [[TC, 128], [1, nt]]),
                   s[:, 0:nt], reads=[s], writes=[o_sz], stream=f"d_osz{ci % 2}")

    wxb = kb.sbuf("wxb", [128, 16 * NS], BF16)
    kb.dma("gpsimd", wxb[:, :], WxCT.ap(0, [[16 * NS, 128], [1, 16 * NS]]),
           reads=[WxCT], writes=[wxb])
    sC = kb.sbuf("sC", [16, TC], BF16)
    ps16 = kb.psum("ps16", [16, 512])
    for ci, (t0, nt) in enumerate(CHUNKS4):
        pairs = [(wxb.ap(kd * NS, [[16 * NS, 128], [1, NS]]),
                  xc[kd][:, t0:t0 + nt]) for kd in range(16)]
        matmul_group(kb, ps16, ps16.h[:, 0:nt], pairs, [wxb] + xc)
        kb.op("vector", lambda q, t0=t0, nt=nt: q.tensor_copy(
            out=sC[:, t0:t0 + nt], in_=ps16.h[:, 0:nt]),
            reads=[ps16], writes=[sC])

    blg = kb.sbuf("blg", [16, DI])
    kb.dma("sync", blg[:, :], BlogT.ap(0, [[DI, 16], [1, DI]]),
           reads=[BlogT], writes=[blg])
    expBT = kb.sbuf("expBT", [16, DI], BF16)
    kb.op("scalar", lambda q: q.activation(out=expBT[:, :], in_=blg[:, :], func=AF.Exp),
          reads=[blg], writes=[expBT])

    cbt = [kb.sbuf(f"cbt{i}", [128, 512], BF16) for i in range(2)]
    for m in range(16):
        for ci, (t0, nt) in enumerate(CHUNKS4):
            p = ps[ci % 4]
            cbo = cbt[ci % 2]
            matmul_group(kb, p, p.h[:, 0:nt],
                         [(expBT[:, m * 128:(m + 1) * 128], sC[:, t0:t0 + nt])],
                         [expBT, sC])
            kb.op("vector", lambda q, p=p, cbo=cbo, nt=nt: q.tensor_copy(
                out=cbo[:, 0:nt], in_=p.h[:, 0:nt]), reads=[p], writes=[cbo])
            kb.dma("sync", o_cb.ap(m * 128 * TC + t0, [[TC, 128], [1, nt]]),
                   cbo[:, 0:nt], reads=[cbo], writes=[o_cb], stream=f"d_ocb{ci % 2}")

    return kb.finalize()


def stage_l0a_pre(inputs):
    xf = np.ascontiguousarray(inputs["L0_attn_x"].reshape(16384, D))
    W_inT = np.ascontiguousarray(inputs["L0_attn_W_in"].T)
    cw4 = np.ascontiguousarray(
        inputs["L0_attn_conv_w"].reshape(16, 128, 4).transpose(1, 0, 2).reshape(128, 64))
    cbv = np.ascontiguousarray(
        inputs["L0_attn_conv_b"].reshape(16, 128).T)
    WxCT = np.ascontiguousarray(
        inputs["L0_attn_W_x"][NS:].T.reshape(16, 128, NS).transpose(1, 0, 2).reshape(128, 16 * NS))
    BlogT = np.ascontiguousarray(inputs["L0_attn_B_log"].T)
    maps = []
    for c in range(NCORE):
        b, h = divmod(c, 2)
        x0p = np.zeros((TP, D), np.float32)
        if h == 1:
            x0p[125:128] = xf[c * TC - 3: c * TC]
        x0p[128:] = xf[c * TC:(c + 1) * TC]
        maps.append(dict(x0p=x0p, W_inT=W_inT, cw4=cw4, cbv=cbv,
                         WxCT=WxCT, BlogT=BlogT))
    return maps





TFULL = 4096


def build_l0a_scan():
    kb = KB("l0a_scan")
    xc2 = kb.inp("xc2", [D, TFULL], BF16)
    CB2 = kb.inp("CB2", [D, TFULL], BF16)
    sz2 = kb.inp("sz2", [D, TFULL], BF16)
    o_yg = kb.out("o_yg", [D, TFULL], BF16)

    dec = kb.sbuf("dec", [128, TFULL])
    kb.op("vector", lambda q: q.memset(dec.h[:, :], 0.95), reads=[], writes=[dec])
    nslot = 2
    xs = [kb.sbuf(f"xs{i}", [128, TFULL], BF16) for i in range(nslot)]
    cs = [kb.sbuf(f"cs{i}", [128, TFULL], BF16) for i in range(nslot)]
    zs = [kb.sbuf(f"zs{i}", [128, TFULL], BF16) for i in range(nslot)]
    hs = [kb.sbuf(f"hs{i}", [128, TFULL], BF16) for i in range(nslot)]
    for j in range(8):
        x, c, z, hb = xs[j % nslot], cs[j % nslot], zs[j % nslot], hs[j % nslot]
        kb.dma("sync", x[:, :], xc2.ap(j * 128 * TFULL, [[TFULL, 128], [1, TFULL]]),
               reads=[xc2], writes=[x])
        kb.dma("sync", c[:, :], CB2.ap(j * 128 * TFULL, [[TFULL, 128], [1, TFULL]]),
               reads=[CB2], writes=[c])
        kb.dma("sync", z[:, :], sz2.ap(j * 128 * TFULL, [[TFULL, 128], [1, TFULL]]),
               reads=[sz2], writes=[z])
        kb.op("vector", lambda q, x=x, hb=hb: q.tensor_tensor_scan(
            out=hb[:, :], data0=dec.h[:, :], data1=x[:, :], initial=0.0,
            op0=ALU.mult, op1=ALU.add), reads=[dec, x], writes=[hb])
        kb.op("vector", lambda q, hb=hb, c=c: q.tensor_tensor(
            out=hb[:, :], in0=hb[:, :], in1=c[:, :], op=ALU.mult),
            reads=[hb, c], writes=[hb])
        kb.op("vector", lambda q, hb=hb, z=z: q.tensor_tensor(
            out=hb[:, :], in0=hb[:, :], in1=z[:, :], op=ALU.mult),
            reads=[hb, z], writes=[hb])
        kb.dma("sync", o_yg.ap(j * 128 * TFULL, [[TFULL, 128], [1, TFULL]]),
               hb[:, :], reads=[hb], writes=[o_yg], stream=f"d_oyg{j % nslot}")
    return kb.finalize()


def stage_l0a_scan(prev):
    maps = []
    for c in range(NCORE):
        b, h = divmod(c, 2)
        hs = slice(h * D, (h + 1) * D)
        maps.append(dict(
            xc2=np.concatenate([prev["o_xc"][2 * b][hs], prev["o_xc"][2 * b + 1][hs]], 1),
            CB2=np.concatenate([prev["o_cb"][2 * b][hs], prev["o_cb"][2 * b + 1][hs]], 1),
            sz2=np.concatenate([prev["o_sz"][2 * b][hs], prev["o_sz"][2 * b + 1][hs]], 1),
        ))
    return maps



def build_l0a_post():
    kb = KB("l0a_post")
    yg2 = kb.inp("yg2", [DI, TC], BF16)
    W_outT = kb.inp("W_outT", [DI, D])
    x_ln = kb.inp("x_ln", [TC, D])
    x0s = kb.inp("x0s", [TC, D])
    lngb = kb.inp("lngb", [128, D])
    lnbb = kb.inp("lnbb", [128, D])
    wrt = kb.inp("wrt", [128, 64])
    iota8b = kb.inp("iota8b", [128, 8])
    L128 = kb.inp("L128", [128, 128])
    L16s = kb.inp("L16s", [16, 16])
    SEL16 = kb.inp("SEL16", [16, 16 * 128])
    COLSEL = kb.inp("COLSEL", [128, 16 * 16])

    o_x1 = kb.out("o_x1", [TC, D])
    o_yln2 = kb.out("o_yln2", [TC, D], BF16)
    o_misc = kb.out("o_misc", [128, 96])
    o_cnt = kb.out("o_cnt", [1, 8])

    ylnlo = kb.dram("ylnlo", [TC, D], BF16)

    yg = [kb.sbuf(f"ygs{k}", [128, TC], BF16, side="right") for k in range(16)]
    for k in range(16):
        kb.dma("sync", yg[k][:, :], yg2.ap(k * 128 * TC, [[TC, 128], [1, TC]]),
               reads=[yg2], writes=[yg[k]])
    wo = kb.sbuf("wo", [128, 16 * D], BF16, side="right")
    kb.dma("gpsimd", bass.AP(wo.h, 0, [[16 * D, 128], [D, 16], [1, D]]),
           W_outT.ap(0, [[D, 128], [128 * D, 16], [1, D]]),
           reads=[W_outT], writes=[wo])

    ps = [kb.psum(f"ps{i}", [128, 512]) for i in range(4)]
    sq = [kb.sbuf("sq", [128, D])]
    stat = kb.sbuf("stat", [128, 8])
    stat2 = kb.sbuf("stat2", [128, 8])
    xlt = [kb.sbuf(f"xlt{i}", [128, D]) for i in range(2)]
    x0t = [kb.sbuf(f"x0t{i}", [128, D]) for i in range(2)]
    rsd = [kb.sbuf(f"rsd{i}", [128, D]) for i in range(2)]
    x1t = [kb.sbuf(f"x1t{i}", [128, D]) for i in range(2)]
    y2t = [kb.sbuf(f"y2t{i}", [128, D]) for i in range(2)]
    hit = kb.sbuf("hit", [128, D], BF16)
    lot = kb.sbuf("lot", [128, D], BF16)
    gb = kb.sbuf("gb", [128, D])
    bb = kb.sbuf("bb", [128, D])
    kb.dma("sync", gb[:, :], lngb.ap(0, [[D, 128], [1, D]]), reads=[lngb], writes=[gb])
    kb.dma("sync", bb[:, :], lnbb.ap(0, [[D, 128], [1, D]]), reads=[lnbb], writes=[bb])

    for j in range(16):
        xl, x0b, rs, x1b, y2 = (xlt[j % 2], x0t[j % 2], rsd[j % 2],
                                x1t[j % 2], y2t[j % 2])
        kb.dma("sync", xl[:, :], x_ln.ap(j * 128 * D, [[D, 128], [1, D]]),
               reads=[x_ln], writes=[xl], stream=f"d_xl{j % 2}")
        kb.dma("sync", x0b[:, :], x0s.ap(j * 128 * D, [[D, 128], [1, D]]),
               reads=[x0s], writes=[x0b], stream=f"d_x0{j % 2}")
        for ci in range(2):
            p = ps[ci % 2]
            pairs = [(yg[k][:, j * 128:(j + 1) * 128],
                      wo.ap(k * D + ci * 512, [[16 * D, 128], [1, 512]]))
                     for k in range(16)]
            matmul_group(kb, p, p.h[:, 0:512], pairs, yg + [wo])
            kb.op("vector", lambda q, p=p, rs=rs, xl=xl, ci=ci: q.tensor_tensor(
                out=rs[:, ci * 512:(ci + 1) * 512], in0=xl[:, ci * 512:(ci + 1) * 512],
                in1=p.h[:, 0:512], op=ALU.add), reads=[p, xl], writes=[rs])
        mu_ap, rstd_ap = e_layernorm(kb, rs, rs[:, :], stat, sq)
        kb.op("vector", lambda q, rs=rs, y2=y2, mu_ap=mu_ap, rstd_ap=rstd_ap:
              q.tensor_scalar(out=y2[:, :], in0=rs[:, :], scalar1=mu_ap,
                              scalar2=rstd_ap, op0=ALU.subtract, op1=ALU.mult),
              reads=[rs, stat], writes=[y2])
        kb.op("vector", lambda q, y2=y2: q.tensor_tensor(
            out=y2[:, :], in0=y2[:, :], in1=gb[:, :], op=ALU.mult),
            reads=[y2, gb], writes=[y2])
        kb.op("vector", lambda q, y2=y2: q.tensor_tensor(
            out=y2[:, :], in0=y2[:, :], in1=bb[:, :], op=ALU.add),
            reads=[y2, bb], writes=[y2])
        kb.op("vector", lambda q, y2=y2, x1b=x1b, x0b=x0b: q.tensor_tensor(
            out=x1b[:, :], in0=y2[:, :], in1=x0b[:, :], op=ALU.add),
            reads=[y2, x0b], writes=[x1b])
        kb.dma("sync", o_x1.ap(j * 128 * D, [[D, 128], [1, D]]), x1b[:, :],
               reads=[x1b], writes=[o_x1], stream=f"d_ox1{j % 2}")
        mu2, rstd2 = e_layernorm(kb, x1b, x1b[:, :], stat2, sq)
        kb.op("vector", lambda q, x1b=x1b, rs=rs, mu2=mu2, rstd2=rstd2:
              q.tensor_scalar(out=rs[:, :], in0=x1b[:, :], scalar1=mu2,
                              scalar2=rstd2, op0=ALU.subtract, op1=ALU.mult),
              reads=[x1b, stat2], writes=[rs])
        kb.op("vector", lambda q, rs=rs: q.tensor_copy(out=hit[:, :], in_=rs[:, :]),
              reads=[rs], writes=[hit])
        kb.op("vector", lambda q, rs=rs: q.tensor_tensor(
            out=lot[:, :], in0=rs[:, :], in1=hit[:, :], op=ALU.subtract),
            reads=[rs, hit], writes=[lot])
        kb.dma("sync", o_yln2.ap(j * 128 * D, [[D, 128], [1, D]]), hit[:, :],
               reads=[hit], writes=[o_yln2], stream="d_hi")
        kb.dma("sync", ylnlo.ap(j * 128 * D, [[D, 128], [1, D]]), lot[:, :],
               reads=[lot], writes=[ylnlo], stream="d_lo")

    kb.free(*yg, wo)
    hiT = [kb.sbuf(f"hiT{k}", [128, TC], BF16, side="right") for k in range(8)]
    loT = [kb.sbuf(f"loT{k}", [128, TC], BF16, side="right") for k in range(8)]
    for kd in range(8):
        kb.dma("sync", hiT[kd][:, :], o_yln2.ap(kd * 128, [[D, TC], [1, 128]]),
               reads=[o_yln2], writes=[hiT[kd]], transpose=True)
        kb.dma("sync", loT[kd][:, :], ylnlo.ap(kd * 128, [[D, TC], [1, 128]]),
               reads=[ylnlo], writes=[loT[kd]], transpose=True)
    wrf = kb.sbuf("wrf", [128, 64])
    kb.dma("sync", wrf[:, :], wrt.ap(0, [[64, 128], [1, 64]]), reads=[wrt], writes=[wrf])
    whi = kb.sbuf("whi", [128, 64], BF16)
    wlo = kb.sbuf("wlo", [128, 64], BF16)
    kb.op("vector", lambda q: q.tensor_copy(out=whi[:, :], in_=wrf[:, :]),
          reads=[wrf], writes=[whi])
    kb.op("vector", lambda q: q.tensor_tensor(
        out=wlo[:, :], in0=wrf[:, :], in1=whi[:, :], op=ALU.subtract),
        reads=[wrf, whi], writes=[wlo])
    io8 = kb.sbuf("io8", [128, 8])
    kb.dma("sync", io8[:, :], iota8b.ap(0, [[8, 128], [1, 8]]), reads=[iota8b], writes=[io8])
    l128 = kb.sbuf("l128", [128, 128])
    kb.dma("sync", l128[:, :], L128.ap(0, [[128, 128], [1, 128]]), reads=[L128], writes=[l128])
    l16 = kb.sbuf("l16", [16, 16])
    kb.dma("sync", l16[:, :], L16s.ap(0, [[16, 16], [1, 16]]), reads=[L16s], writes=[l16])
    sel16 = kb.sbuf("sel16", [16, 16 * 128])
    kb.dma("sync", sel16[:, :], SEL16.ap(0, [[16 * 128, 16], [1, 16 * 128]]),
           reads=[SEL16], writes=[sel16])
    colsel = kb.sbuf("colsel", [128, 256])
    kb.dma("sync", colsel[:, :], COLSEL.ap(0, [[256, 128], [1, 256]]),
           reads=[COLSEL], writes=[colsel])

    OH1a = kb.sbuf("OH1a", [128, 16 * 8])
    OH2a = kb.sbuf("OH2a", [128, 16 * 8])
    OHa = kb.sbuf("OHa", [128, 16 * 8])
    misc = kb.sbuf("misc", [128, 96])
    rst = kb.sbuf("rst", [128, 16])
    Sx = kb.sbuf("Sx", [128, 8])
    S2 = kb.sbuf("S2", [128, 8])
    t8 = kb.sbuf("t8", [128, 8])
    psr = ps[2]
    for j in range(16):
        pairs = []
        for kd in range(8):
            pairs.append((hiT[kd][:, j * 128:(j + 1) * 128],
                          whi.ap(kd * 8, [[64, 128], [1, 8]])))
        for kd in range(8):
            pairs.append((hiT[kd][:, j * 128:(j + 1) * 128],
                          wlo.ap(kd * 8, [[64, 128], [1, 8]])))
        for kd in range(8):
            pairs.append((loT[kd][:, j * 128:(j + 1) * 128],
                          whi.ap(kd * 8, [[64, 128], [1, 8]])))
        matmul_group(kb, psr, psr.h[:, 0:8], pairs, hiT + loT + [whi, wlo])
        kb.op("vector", lambda q: q.reduce_max(
            out=rst.h[:, 0:1], in_=psr.h[:, 0:8], axis=AX.X, negate=True),
            reads=[psr], writes=[rst])
        kb.op("scalar", lambda q: q.activation(
            out=Sx[:, :], in_=psr.h[:, 0:8], func=AF.Exp, bias=rst.h[:, 0:1]),
            reads=[psr, rst], writes=[Sx])
        kb.op("vector", lambda q: q.reduce_sum(
            out=rst.h[:, 1:2], in_=Sx[:, :], axis=AX.X), reads=[Sx], writes=[rst])
        kb.op("vector", lambda q: q.reduce_max(
            out=rst.h[:, 2:3], in_=Sx[:, :], axis=AX.X), reads=[Sx], writes=[rst])
        kb.op("vector", lambda q, j=j: q.tensor_scalar(
            out=OH1a[:, j * 8:(j + 1) * 8], in0=Sx[:, :], scalar1=rst.h[:, 2:3],
            scalar2=None, op0=ALU.is_ge), reads=[Sx, rst], writes=[OH1a])
        kb.op("vector", lambda q, j=j: q.tensor_scalar(
            out=t8[:, :], in0=OH1a[:, j * 8:(j + 1) * 8], scalar1=-1.0, scalar2=1.0,
            op0=ALU.mult, op1=ALU.add), reads=[OH1a], writes=[t8])
        kb.op("vector", lambda q: q.tensor_tensor(
            out=S2[:, :], in0=Sx[:, :], in1=t8[:, :], op=ALU.mult),
            reads=[Sx, t8], writes=[S2])
        kb.op("vector", lambda q: q.reduce_max(
            out=rst.h[:, 3:4], in_=S2[:, :], axis=AX.X), reads=[S2], writes=[rst])
        kb.op("vector", lambda q, j=j: q.tensor_scalar(
            out=OH2a[:, j * 8:(j + 1) * 8], in0=S2[:, :], scalar1=rst.h[:, 3:4],
            scalar2=None, op0=ALU.is_ge), reads=[S2, rst], writes=[OH2a])
        kb.op("vector", lambda q, j=j: q.tensor_tensor(
            out=OHa[:, j * 8:(j + 1) * 8], in0=OH1a[:, j * 8:(j + 1) * 8],
            in1=OH2a[:, j * 8:(j + 1) * 8], op=ALU.add),
            reads=[OH1a, OH2a], writes=[OHa])
        kb.op("vector", lambda q, j=j: q.tensor_tensor(
            out=t8[:, :], in0=OH1a[:, j * 8:(j + 1) * 8], in1=io8[:, :], op=ALU.mult),
            reads=[OH1a, io8], writes=[t8])
        kb.op("vector", lambda q, j=j: q.reduce_sum(
            out=misc.h[:, 0 * 16 + j:0 * 16 + j + 1], in_=t8[:, :], axis=AX.X),
            reads=[t8], writes=[misc])
        kb.op("vector", lambda q, j=j: q.tensor_tensor(
            out=t8[:, :], in0=OH2a[:, j * 8:(j + 1) * 8], in1=io8[:, :], op=ALU.mult),
            reads=[OH2a, io8], writes=[t8])
        kb.op("vector", lambda q, j=j: q.reduce_sum(
            out=misc.h[:, 1 * 16 + j:1 * 16 + j + 1], in_=t8[:, :], axis=AX.X),
            reads=[t8], writes=[misc])
        kb.op("vector", lambda q: q.reciprocal(
            out=rst.h[:, 4:5], in_=rst.h[:, 1:2]), reads=[rst], writes=[rst])
        kb.op("vector", lambda q, j=j: q.tensor_tensor(
            out=misc.h[:, 2 * 16 + j:2 * 16 + j + 1], in0=rst.h[:, 2:3],
            in1=rst.h[:, 4:5], op=ALU.mult), reads=[rst], writes=[misc])
        kb.op("vector", lambda q, j=j: q.tensor_tensor(
            out=misc.h[:, 3 * 16 + j:3 * 16 + j + 1], in0=rst.h[:, 3:4],
            in1=rst.h[:, 4:5], op=ALU.mult), reads=[rst], writes=[misc])

    psT = ps[3]
    pairsT = [(colsel[:, j * 16:(j + 1) * 16], OHa[:, j * 8:(j + 1) * 8])
              for j in range(16)]
    matmul_group(kb, psT, psT.h[0:16, 0:8], pairsT, [colsel, OHa])
    tot16 = kb.sbuf("tot16", [16, 8])
    kb.op("vector", lambda q: q.tensor_copy(out=tot16[:, :], in_=psT.h[0:16, 0:8]),
          reads=[psT], writes=[tot16])
    one16 = kb.sbuf("one16", [16, 1])
    kb.op("vector", lambda q: q.memset(one16.h[:, :], 1.0), reads=[], writes=[one16])
    psC = ps[3]
    matmul_group(kb, psC, psC.h[0:1, 0:8], [(one16[:, :], tot16[:, :])], [one16, tot16])
    cnt1 = kb.sbuf("cnt1", [1, 8])
    kb.op("vector", lambda q: q.tensor_copy(out=cnt1[:, :], in_=psC.h[0:1, 0:8]),
          reads=[psC], writes=[cnt1])
    kb.dma("sync", o_cnt.ap(0, [[8, 1], [1, 8]]), cnt1[:, :],
           reads=[cnt1], writes=[o_cnt])
    psB = ps[2]
    matmul_group(kb, psB, psB.h[0:16, 0:8], [(l16[:, :], tot16[:, :])], [l16, tot16])
    tb16 = kb.sbuf("tb16", [16, 8])
    kb.op("vector", lambda q: q.tensor_copy(out=tb16[:, :], in_=psB.h[0:16, 0:8]),
          reads=[psB], writes=[tb16])

    psc2 = [ps[0], ps[1]]
    cum = [kb.sbuf(f"cum{i}", [128, 8]) for i in range(2)]
    for j in range(16):
        p = psc2[j % 2]
        cu = cum[j % 2]

        def fn(q, j=j, p=p):
            q.matmul(p.h[:, 0:8], l128[:, :], OHa[:, j * 8:(j + 1) * 8],
                     start=True, stop=False)
            return q.matmul(p.h[:, 0:8], sel16[:, j * 128:(j + 1) * 128],
                            tb16[:, :], start=False, stop=True)
        kb.op("tensor", fn, reads=[l128, OHa, sel16, tb16], writes=[p])
        kb.op("vector", lambda q, p=p, cu=cu: q.tensor_copy(
            out=cu[:, :], in_=p.h[:, 0:8]), reads=[p], writes=[cu])
        for oh, col in ((OH1a, 4), (OH2a, 5)):
            kb.op("vector", lambda q, j=j, oh=oh, cu=cu: q.tensor_tensor(
                out=t8[:, :], in0=oh[:, j * 8:(j + 1) * 8], in1=cu[:, :],
                op=ALU.mult), reads=[oh, cu], writes=[t8])
            kb.op("vector", lambda q, j=j, col=col: q.reduce_sum(
                out=rst.h[:, 5:6], in_=t8[:, :], axis=AX.X),
                reads=[t8], writes=[rst])
            kb.op("vector", lambda q, j=j, col=col: q.tensor_scalar(
                out=misc.h[:, col * 16 + j:col * 16 + j + 1], in0=rst.h[:, 5:6],
                scalar1=-1.0, scalar2=None, op0=ALU.add),
                reads=[rst], writes=[misc])
    kb.dma("sync", o_misc.ap(0, [[96, 128], [1, 96]]), misc.h[:, :],
           reads=[misc], writes=[o_misc])
    return kb.finalize()


def stage_l0a_post(inputs, prev):
    xf = np.ascontiguousarray(inputs["L0_attn_x"].reshape(16384, D))
    W_outT = np.ascontiguousarray(inputs["L0_attn_W_out"].T)
    lngb = np.ascontiguousarray(np.broadcast_to(inputs["L0_attn_ln_g"], (128, D)))
    lnbb = np.ascontiguousarray(np.broadcast_to(inputs["L0_attn_ln_b"], (128, D)))
    wrt = np.ascontiguousarray(
        inputs["L0_ffn_w_router"].reshape(8, 128, 8).transpose(1, 0, 2).reshape(128, 64))
    iota8b = np.ascontiguousarray(
        np.broadcast_to(np.arange(8, dtype=np.float32), (128, 8)))
    L128 = np.tril(np.ones((128, 128), np.float32)).T.copy()
    L16s = np.triu(np.ones((16, 16), np.float32), 1)
    SEL16 = np.zeros((16, 16 * 128), np.float32)
    for j in range(16):
        SEL16[j, j * 128:(j + 1) * 128] = 1.0
    COLSEL = np.zeros((128, 16 * 16), np.float32)
    for j in range(16):
        COLSEL[:, j * 16 + j] = 1.0
    maps = []
    for c in range(NCORE):
        b, h = divmod(c, 2)
        yg2 = np.concatenate([prev["o_yg"][2 * b][:, h * TC:(h + 1) * TC],
                              prev["o_yg"][2 * b + 1][:, h * TC:(h + 1) * TC]], 0)
        maps.append(dict(
            yg2=np.ascontiguousarray(yg2), W_outT=W_outT,
            x_ln=prev["o_xln"][c], x0s=np.ascontiguousarray(xf[c * TC:(c + 1) * TC]),
            lngb=lngb, lnbb=lnbb, wrt=wrt, iota8b=iota8b, L128=L128, L16s=L16s,
            SEL16=SEL16, COLSEL=COLSEL))
    return maps





BIGSLOT = 1000000.0


def build_moe_route2():
    kb = KB("moe_route2")
    cnt_all = kb.inp("cnt_all", [8, 8])
    maskprev = kb.inp("maskprev", [8, 1])
    misc_in = kb.inp("misc_in", [128, 96])
    tidc = kb.inp("tidc", [128, 16])
    iota8b = kb.inp("iota8b", [128, 8])

    o_stg = kb.out("o_stg", [EXP * CAP, 1])
    o_weff = kb.out("o_weff", [128, 32])
    o_gidx = kb.out("o_gidx", [128, 32], I32)

    cbd = kb.dram("cbd", [1, 8])

    ca = kb.sbuf("ca", [8, 8])
    kb.dma("sync", ca[:, :], cnt_all.ap(0, [[8, 8], [1, 8]]), reads=[cnt_all], writes=[ca])
    mp = kb.sbuf("mp", [8, 1])
    kb.dma("sync", mp[:, :], maskprev.ap(0, [[1, 8], [1, 1]]), reads=[maskprev], writes=[mp])
    mi = kb.sbuf("mi", [128, 96])
    kb.dma("sync", mi[:, :], misc_in.ap(0, [[96, 128], [1, 96]]), reads=[misc_in], writes=[mi])
    tid = kb.sbuf("tid", [128, 16])
    kb.dma("sync", tid[:, :], tidc.ap(0, [[16, 128], [1, 16]]), reads=[tidc], writes=[tid])
    io8 = kb.sbuf("io8", [128, 8])
    kb.dma("sync", io8[:, :], iota8b.ap(0, [[8, 128], [1, 8]]), reads=[io8, iota8b], writes=[io8])

    psc = kb.psum("psc", [1, 8])
    matmul_group(kb, psc, psc.h[0:1, 0:8], [(mp[:, :], ca[:, :])], [mp, ca])
    cb1 = kb.sbuf("cb1", [1, 8])
    kb.op("vector", lambda q: q.tensor_copy(out=cb1[:, :], in_=psc.h[0:1, 0:8]),
          reads=[psc], writes=[cb1])
    kb.dma("sync", cbd.ap(0, [[8, 1], [1, 8]]), cb1[:, :], reads=[cb1], writes=[cbd])
    cbbc = kb.sbuf("cbbc", [128, 8])
    kb.dma("sync", cbbc[:, :], cbd.ap(0, [[0, 128], [1, 8]]), reads=[cbd], writes=[cbbc])

    zt = kb.sbuf("zt", [128, 256])
    kb.op("vector", lambda q: q.memset(zt.h[:, :], 0.0), reads=[], writes=[zt])
    kb.dma("gpsimd", o_stg.ap(0, [[256, 128], [1, 256]]), zt[:, :],
           reads=[zt], writes=[o_stg])

    oh = kb.sbuf("oh", [128, 8])
    t8 = kb.sbuf("t8", [128, 8])
    sc = kb.sbuf("sc", [128, 8])
    weff = kb.sbuf("weff", [128, 32])
    gidx = kb.sbuf("gidx", [128, 32], I32)
    gfl = kb.sbuf("gfl", [128, 1])
    for k in range(2):
        for j in range(16):
            ecol = mi.h[:, k * 16 + j:k * 16 + j + 1]
            wcol = mi.h[:, (2 + k) * 16 + j:(2 + k) * 16 + j + 1]
            lrcol = mi.h[:, (4 + k) * 16 + j:(4 + k) * 16 + j + 1]
            kb.op("vector", lambda q, ecol=ecol: q.tensor_scalar(
                out=oh[:, :], in0=io8[:, :], scalar1=ecol, scalar2=None,
                op0=ALU.is_equal), reads=[mi, io8], writes=[oh])
            kb.op("vector", lambda q: q.tensor_tensor(
                out=t8[:, :], in0=oh[:, :], in1=cbbc[:, :], op=ALU.mult),
                reads=[oh, cbbc], writes=[t8])
            kb.op("vector", lambda q: q.reduce_sum(
                out=sc.h[:, 0:1], in_=t8[:, :], axis=AX.X), reads=[t8], writes=[sc])
            kb.op("vector", lambda q, lrcol=lrcol: q.tensor_tensor(
                out=sc.h[:, 1:2], in0=sc.h[:, 0:1], in1=lrcol, op=ALU.add),
                reads=[sc, mi], writes=[sc])
            kb.op("vector", lambda q: q.tensor_scalar(
                out=sc.h[:, 2:3], in0=sc.h[:, 1:2], scalar1=float(CAP), scalar2=None,
                op0=ALU.is_lt), reads=[sc], writes=[sc])
            kb.op("vector", lambda q, wcol=wcol, k=k, j=j: q.tensor_tensor(
                out=weff.h[:, k * 16 + j:k * 16 + j + 1], in0=sc.h[:, 2:3],
                in1=wcol, op=ALU.mult), reads=[sc, mi], writes=[weff])
            kb.op("vector", lambda q, ecol=ecol: q.tensor_scalar(
                out=sc.h[:, 3:4], in0=ecol, scalar1=float(CAP), scalar2=None,
                op0=ALU.mult), reads=[mi], writes=[sc])
            kb.op("vector", lambda q: q.tensor_tensor(
                out=sc.h[:, 4:5], in0=sc.h[:, 3:4], in1=sc.h[:, 1:2], op=ALU.add),
                reads=[sc], writes=[sc])
            kb.op("vector", lambda q: q.tensor_scalar(
                out=sc.h[:, 5:6], in0=sc.h[:, 2:3], scalar1=-BIGSLOT, scalar2=BIGSLOT,
                op0=ALU.mult, op1=ALU.add), reads=[sc], writes=[sc])
            kb.op("vector", lambda q, k=k, j=j: q.tensor_tensor(
                out=sc.h[:, 6:7], in0=sc.h[:, 4:5], in1=sc.h[:, 5:6], op=ALU.add),
                reads=[sc], writes=[sc])
            kb.op("vector", lambda q, k=k, j=j: q.tensor_copy(
                out=gidx.h[:, k * 16 + j:k * 16 + j + 1], in_=sc.h[:, 6:7]),
                reads=[sc], writes=[gidx])
            kb.op("vector", lambda q, j=j: q.tensor_copy(
                out=gfl[:, :], in_=tid.h[:, j:j + 1]), reads=[tid], writes=[gfl])
            kb.idma(o_stg.ap(0, [[1, EXP * CAP], [1, 1]]), gfl[:, :],
                    out_off=bass.IndirectOffsetOnAxis(
                        gidx.h[:, k * 16 + j:k * 16 + j + 1], 0),
                    bounds=EXP * CAP - 1,
                    reads=[gfl, gidx], writes=[o_stg], stream="d_scat")
    kb.dma("sync", o_weff.ap(0, [[32, 128], [1, 32]]), weff[:, :],
           reads=[weff], writes=[o_weff])
    kb.dma("sync", o_gidx.ap(0, [[32, 128], [1, 32]]), gidx[:, :],
           reads=[gidx], writes=[o_gidx])
    return kb.finalize()


def stage_moe_route2(prev):
    cnt_all = np.concatenate([prev["o_cnt"][c] for c in range(8)], 0)
    maps = []
    for c in range(NCORE):
        mp = np.zeros((8, 1), np.float32)
        mp[:c] = 1.0
        tid = np.arange(c * TC, (c + 1) * TC, dtype=np.float32)
        tidc = np.ascontiguousarray(tid.reshape(16, 128).T)
        iota8b = np.ascontiguousarray(
            np.broadcast_to(np.arange(8, dtype=np.float32), (128, 8)))
        maps.append(dict(cnt_all=np.ascontiguousarray(cnt_all), maskprev=mp,
                         misc_in=prev["o_misc"][c], tidc=tidc, iota8b=iota8b))
    return maps



def build_moe_expert():
    kb = KB("moe_expert")
    stg_all = kb.inp("stg_all", [8, EXP * CAP])
    yfull = kb.inp("yfull", [16384, D], BF16)
    w1e = kb.inp("w1e", [D, 4096])
    w2e = kb.inp("w2e", [4096, D])
    eid = kb.inp("eid", [1, 1])

    o_ye = kb.out("o_ye", [CAP, D], BF16)

    stgi = kb.dram("stgi", [EXP * CAP, 1], I32)
    xb_dram = kb.dram("xb_dram", [CAP, D], BF16)
    yT_dram = kb.dram("yT_dram", [D, CAP], BF16)

    acc = kb.sbuf("acc", [128, 256])
    tmp = kb.sbuf("tmp", [128, 256])
    kb.dma("sync", acc[:, :], stg_all.ap(0, [[256, 128], [1, 256]]),
           reads=[stg_all], writes=[acc])
    for r in range(1, 8):
        kb.dma("sync", tmp[:, :], stg_all.ap(r * EXP * CAP, [[256, 128], [1, 256]]),
               reads=[stg_all], writes=[tmp], stream="d_tmp")
        kb.op("vector", lambda q: q.tensor_tensor(
            out=acc[:, :], in0=acc[:, :], in1=tmp[:, :], op=ALU.add),
            reads=[acc, tmp], writes=[acc])
    acci = kb.sbuf("acci", [128, 256], I32)
    kb.op("vector", lambda q: q.tensor_copy(out=acci[:, :], in_=acc[:, :]),
          reads=[acc], writes=[acci])
    kb.dma("sync", stgi.ap(0, [[256, 128], [1, 256]]), acci[:, :],
           reads=[acci], writes=[stgi])

    idxt = [kb.sbuf(f"idxt{i}", [128, 1], I32) for i in range(4)]
    gt = [kb.sbuf(f"gt{i}", [128, D], BF16) for i in range(2)]
    eidv = kb.sbuf("eidv", [1, 1])
    kb.dma("sync", eidv[:, :], eid.ap(0, [[1, 1], [1, 1]]), reads=[eid], writes=[eidv])
    for js in range(32):
        it = idxt[js % 4]
        g = gt[js % 2]
        kb.dma("sync", it[:, :], stgi.ap(js * 128, [[1, 128], [1, 1]]),
               reads=[stgi], writes=[it], stream=f"d_idx{js % 4}")
        kb.idma(g[:, :], yfull.ap(0, [[D, 16384], [1, D]]),
                in_off=bass.IndirectOffsetOnAxis(it[:, :], 0),
                bounds=16383, reads=[yfull, it], writes=[g])
        kb.dma("gpsimd", xb_dram.ap(js * 128 * D, [[D, 128], [1, D]]), g[:, :],
               reads=[g], writes=[xb_dram], stream=f"d_xbd{js % 2}")

    w1b = kb.sbuf("w1b", [128, 8 * 4096], BF16)
    kb.dma("gpsimd", bass.AP(w1b.h, 0, [[8 * 4096, 128], [4096, 8], [1, 4096]]),
           w1e.ap(0, [[4096, 128], [128 * 4096, 8], [1, 4096]]),
           reads=[w1e], writes=[w1b])
    w2b = kb.sbuf("w2b", [128, 32 * D], BF16)
    kb.dma("gpsimd", bass.AP(w2b.h, 0, [[32 * D, 128], [D, 32], [1, D]]),
           w2e.ap(0, [[D, 128], [128 * D, 32], [1, D]]),
           reads=[w2e], writes=[w2b])

    ps = [kb.psum(f"ps{i}", [128, 512]) for i in range(4)]
    xbT = [kb.sbuf(f"xbT{kd}", [128, 512], BF16) for kd in range(8)]
    hT = kb.sbuf("hT", [128, 32 * 512], BF16)
    yto = [kb.sbuf(f"yto{i}", [128, 512], BF16) for i in range(2)]
    for s0 in range(8):
        for kd in range(8):
            kb.dma("sync", xbT[kd][:, :],
                   xb_dram.ap(s0 * 512 * D + kd * 128, [[D, 512], [1, 128]]),
                   reads=[xb_dram], writes=[xbT[kd]], transpose=True)
        for mf in range(32):
            p = ps[mf % 4]
            pairs = [(w1b.ap(kd * 4096 + mf * 128, [[8 * 4096, 128], [1, 128]]),
                      xbT[kd][:, :]) for kd in range(8)]
            matmul_group(kb, p, p.h[:, 0:512], pairs, [w1b] + xbT)
            kb.op("scalar", lambda q, p=p, mf=mf: q.activation(
                out=hT[:, mf * 512:(mf + 1) * 512], in_=p.h[:, 0:512], func=AF.Gelu),
                reads=[p], writes=[hT])
        for mo in range(8):
            p = ps[mo % 4]
            y = yto[mo % 2]
            pairs = [(w2b.ap(kf * D + mo * 128, [[32 * D, 128], [1, 128]]),
                      hT[:, kf * 512:(kf + 1) * 512]) for kf in range(32)]
            matmul_group(kb, p, p.h[:, 0:512], pairs, [w2b, hT])
            kb.op("vector", lambda q, p=p, y=y: q.tensor_copy(
                out=y[:, :], in_=p.h[:, 0:512]), reads=[p], writes=[y])
            kb.dma("sync", yT_dram.ap(mo * 128 * CAP + s0 * 512, [[CAP, 128], [1, 512]]),
                   y[:, :], reads=[y], writes=[yT_dram], stream=f"d_yT{mo % 2}")

    ytk = [kb.sbuf(f"ytk{i}", [128, D], BF16) for i in range(2)]
    for js in range(32):
        y = ytk[js % 2]
        kb.dma("sync", y[:, :], yT_dram.ap(js * 128, [[CAP, D], [1, 128]]),
               reads=[yT_dram], writes=[y], transpose=True)
        kb.dma("gpsimd", o_ye.ap(js * 128 * D, [[D, 128], [1, D]]), y[:, :],
               reads=[y], writes=[o_ye], stream=f"d_oye{js % 2}")
    return kb.finalize()


def stage_moe_expert(inputs, prev):
    stg = np.stack([np.asarray(prev["o_stg"][c]).reshape(-1) for c in range(8)], 0)
    yfull = np.concatenate([np.asarray(prev["o_yln2"][c]) for c in range(8)], 0)
    w1 = inputs["L0_ffn_w1"]
    w2 = inputs["L0_ffn_w2"]
    maps = []
    for e in range(NCORE):
        stg_rot = np.ascontiguousarray(
            np.roll(stg, -e * CAP, axis=1))
        maps.append(dict(
            stg_all=stg_rot, yfull=np.ascontiguousarray(yfull),
            w1e=np.ascontiguousarray(w1[e]), w2e=np.ascontiguousarray(w2[e]),
            eid=np.array([[float(e)]], np.float32)))
    return maps



def build_moe_combine():
    kb = KB("moe_combine")
    yall = kb.inp("yall", [EXP * CAP, D], BF16)
    x1s = kb.inp("x1s", [TC, D])
    weff_in = kb.inp("weff_in", [128, 32])
    gidx_in = kb.inp("gidx_in", [128, 32], I32)
    biasb = kb.inp("biasb", [128, D])

    o_x2 = kb.out("o_x2", [TC, D])
    o_yln3 = kb.out("o_yln3", [TC, D], BF16)

    wf = kb.sbuf("wf", [128, 32])
    kb.dma("sync", wf[:, :], weff_in.ap(0, [[32, 128], [1, 32]]),
           reads=[weff_in], writes=[wf])
    gx = kb.sbuf("gx", [128, 32], I32)
    kb.dma("sync", gx[:, :], gidx_in.ap(0, [[32, 128], [1, 32]]),
           reads=[gidx_in], writes=[gx])
    bb = kb.sbuf("bb", [128, D])
    kb.dma("sync", bb[:, :], biasb.ap(0, [[D, 128], [1, D]]), reads=[biasb], writes=[bb])

    g1 = [kb.sbuf(f"g1_{i}", [128, D], BF16) for i in range(2)]
    g2 = [kb.sbuf(f"g2_{i}", [128, D], BF16) for i in range(2)]
    x1t = [kb.sbuf(f"x1t{i}", [128, D]) for i in range(2)]
    acc = [kb.sbuf(f"acc{i}", [128, D]) for i in range(2)]
    yl = [kb.sbuf(f"yl{i}", [128, D], BF16) for i in range(2)]
    sq = [kb.sbuf("sq", [128, D])]
    stat = kb.sbuf("stat", [128, 8])
    for j in range(16):
        ga, gb2, x1b, ac, ylb = g1[j % 2], g2[j % 2], x1t[j % 2], acc[j % 2], yl[j % 2]
        kb.dma("sync", x1b[:, :], x1s.ap(j * 128 * D, [[D, 128], [1, D]]),
               reads=[x1s], writes=[x1b], stream=f"d_x1{j % 2}")
        kb.op("vector", lambda q, ga=ga: q.memset(ga.h[:, :], 0.0),
              reads=[], writes=[ga])
        kb.op("vector", lambda q, gb2=gb2: q.memset(gb2.h[:, :], 0.0),
              reads=[], writes=[gb2])
        kb.idma(ga[:, :], yall.ap(0, [[D, EXP * CAP], [1, D]]),
                in_off=bass.IndirectOffsetOnAxis(gx.h[:, j:j + 1], 0),
                bounds=EXP * CAP - 1, reads=[yall, gx], writes=[ga])
        kb.idma(gb2[:, :], yall.ap(0, [[D, EXP * CAP], [1, D]]),
                in_off=bass.IndirectOffsetOnAxis(gx.h[:, 16 + j:16 + j + 1], 0),
                bounds=EXP * CAP - 1, reads=[yall, gx], writes=[gb2])
        kb.op("vector", lambda q, ga=ga, ac=ac, j=j: q.tensor_scalar(
            out=ac[:, :], in0=ga[:, :], scalar1=wf.h[:, j:j + 1], scalar2=None,
            op0=ALU.mult), reads=[ga, wf], writes=[ac])
        kb.op("vector", lambda q, gb2=gb2, ac=ac, j=j: q.scalar_tensor_tensor(
            out=ac[:, :], in0=gb2[:, :], scalar=wf.h[:, 16 + j:16 + j + 1],
            in1=ac[:, :], op0=ALU.mult, op1=ALU.add),
            reads=[gb2, wf, ac], writes=[ac])
        kb.op("vector", lambda q, x1b=x1b, ac=ac: q.tensor_tensor(
            out=ac[:, :], in0=ac[:, :], in1=x1b[:, :], op=ALU.add),
            reads=[ac, x1b], writes=[ac])
        kb.op("vector", lambda q, ac=ac: q.tensor_tensor(
            out=ac[:, :], in0=ac[:, :], in1=bb[:, :], op=ALU.add),
            reads=[ac, bb], writes=[ac])
        kb.dma("sync", o_x2.ap(j * 128 * D, [[D, 128], [1, D]]), ac[:, :],
               reads=[ac], writes=[o_x2], stream=f"d_ox2{j % 2}")
        mu_ap, rstd_ap = e_layernorm(kb, ac, ac[:, :], stat, sq)
        kb.op("vector", lambda q, ac=ac, ylb=ylb, mu_ap=mu_ap, rstd_ap=rstd_ap:
              q.tensor_scalar(out=ylb[:, :], in0=ac[:, :], scalar1=mu_ap,
                              scalar2=rstd_ap, op0=ALU.subtract, op1=ALU.mult),
              reads=[ac, stat], writes=[ylb])
        kb.dma("sync", o_yln3.ap(j * 128 * D, [[D, 128], [1, D]]), ylb[:, :],
               reads=[ylb], writes=[o_yln3], stream=f"d_oyl{j % 2}")
    return kb.finalize()


def stage_moe_combine(inputs, prev):
    yall = np.concatenate([np.asarray(prev["o_ye"][c]) for c in range(8)], 0)
    biasb = np.ascontiguousarray(np.broadcast_to(
        np.asarray(inputs["L0_ffn_bias"]).reshape(D), (128, D)))
    maps = []
    for c in range(NCORE):
        maps.append(dict(yall=yall, x1s=prev["o_x1"][c],
                         weff_in=prev["o_weff"][c], gidx_in=prev["o_gidx"][c],
                         biasb=biasb))
    return maps





FP4_MID = (0.25, 0.75, 1.25, 1.75, 2.5, 3.5, 5.0)
FP4_DG = (0.5, 0.5, 0.5, 0.5, 1.0, 1.0, 2.0)


def e_fp4(kb, x_buf, x_ap, P, W, qs, out_buf, out_ap):
    nb = W // 16
    ax, qa, tt, rs = qs["ax"], qs["qa"], qs["tt"], qs["rs"]
    kb.op("scalar", lambda q: q.activation(out=ax.h[0:P, 0:W], in_=x_ap, func=AF.Abs),
          reads=[x_buf], writes=[ax])
    ax3 = bass.AP(ax.h, 0, [[ax.h.shape[1], P], [16, nb], [1, 16]])
    kb.op("vector", lambda q: q.tensor_reduce(
        out=rs.h[0:P, 0:nb], in_=ax3, axis=AX.X, op=ALU.max),
        reads=[ax], writes=[rs])
    kb.op("vector", lambda q: q.tensor_scalar(
        out=rs.h[0:P, nb:2 * nb], in0=rs.h[0:P, 0:nb], scalar1=1.0 / 6.0,
        scalar2=1e-12, op0=ALU.mult, op1=ALU.max), reads=[rs], writes=[rs])
    kb.op("vector", lambda q: q.reciprocal(
        out=rs.h[0:P, 2 * nb:3 * nb], in_=rs.h[0:P, nb:2 * nb]),
        reads=[rs], writes=[rs])
    rsw = rs.h.shape[1]
    rrep = bass.AP(rs.h, 2 * nb, [[rsw, P], [1, nb], [0, 16]])
    kb.op("vector", lambda q: q.tensor_tensor(
        out=ax3, in0=ax3, in1=rrep, op=ALU.mult), reads=[ax, rs], writes=[ax])
    kb.op("vector", lambda q: q.tensor_scalar(
        out=qa.h[0:P, 0:W], in0=ax.h[0:P, 0:W], scalar1=FP4_MID[0],
        scalar2=FP4_DG[0], op0=ALU.is_gt, op1=ALU.mult), reads=[ax], writes=[qa])
    for j in range(1, 7):
        kb.op("vector", lambda q, j=j: q.tensor_scalar(
            out=tt.h[0:P, 0:W], in0=ax.h[0:P, 0:W], scalar1=FP4_MID[j],
            scalar2=FP4_DG[j], op0=ALU.is_gt, op1=ALU.mult), reads=[ax], writes=[tt])
        kb.op("vector", lambda q: q.tensor_tensor(
            out=qa.h[0:P, 0:W], in0=qa.h[0:P, 0:W], in1=tt.h[0:P, 0:W],
            op=ALU.add), reads=[qa, tt], writes=[qa])
    kb.op("scalar", lambda q: q.activation(out=ax.h[0:P, 0:W], in_=x_ap, func=AF.Sign),
          reads=[x_buf], writes=[ax])
    srep = bass.AP(rs.h, nb, [[rsw, P], [1, nb], [0, 16]])
    kb.op("vector", lambda q: q.tensor_tensor(
        out=ax3, in0=ax3, in1=srep, op=ALU.mult), reads=[ax, rs], writes=[ax])
    kb.op("vector", lambda q: q.tensor_tensor(
        out=out_ap, in0=qa.h[0:P, 0:W], in1=ax.h[0:P, 0:W], op=ALU.mult),
        reads=[qa, ax], writes=[out_buf])



def build_l1a_pre():
    kb = KB("l1a_pre")
    yln3 = kb.inp("yln3", [TC, D], BF16)
    W_in1 = kb.inp("W_in1", [D, 2 * DI])
    o_xi = kb.out("o_xi", [DI, TC], BF16)
    o_sz1 = kb.out("o_sz1", [DI, TC], BF16)

    ylT = [kb.sbuf(f"ylT{k}", [128, TC], BF16) for k in range(8)]
    for kd in range(8):
        kb.dma("sync", ylT[kd][:, :], yln3.ap(kd * 128, [[D, TC], [1, 128]]),
               reads=[yln3], writes=[ylT[kd]], transpose=True)
    wbuf = kb.sbuf("wbuf", [128, 8 * DI], BF16)
    kb.dma("gpsimd", bass.AP(wbuf.h, 0, [[8 * DI, 128], [DI, 8], [1, DI]]),
           W_in1.ap(0, [[2 * DI, 128], [128 * 2 * DI, 8], [1, DI]]),
           reads=[W_in1], writes=[wbuf])
    ps = [kb.psum(f"ps{i}", [128, 512]) for i in range(4)]
    ot = [kb.sbuf(f"ot{i}", [128, 512], BF16) for i in range(2)]
    for m in range(16):
        for ci, (t0, nt) in enumerate(CHUNKS4):
            p = ps[ci % 4]
            o = ot[ci % 2]
            pairs = [(wbuf.ap(kd * DI + m * 128, [[8 * DI, 128], [1, 128]]),
                      ylT[kd][:, t0:t0 + nt]) for kd in range(8)]
            matmul_group(kb, p, p.h[:, 0:nt], pairs, [wbuf] + ylT)
            kb.op("vector", lambda q, p=p, o=o, nt=nt: q.tensor_copy(
                out=o[:, 0:nt], in_=p.h[:, 0:nt]), reads=[p], writes=[o])
            kb.dma("sync", o_xi.ap(m * 128 * TC + t0, [[TC, 128], [1, nt]]),
                   o[:, 0:nt], reads=[o], writes=[o_xi], stream=f"d_oxi{ci % 2}")
    kb.dma("gpsimd", bass.AP(wbuf.h, 0, [[8 * DI, 128], [DI, 8], [1, DI]]),
           W_in1.ap(DI, [[2 * DI, 128], [128 * 2 * DI, 8], [1, DI]]),
           reads=[W_in1], writes=[wbuf])
    for m in range(16):
        for ci, (t0, nt) in enumerate(CHUNKS4):
            p = ps[ci % 4]
            o = ot[ci % 2]
            pairs = [(wbuf.ap(kd * DI + m * 128, [[8 * DI, 128], [1, 128]]),
                      ylT[kd][:, t0:t0 + nt]) for kd in range(8)]
            matmul_group(kb, p, p.h[:, 0:nt], pairs, [wbuf] + ylT)
            kb.op("scalar", lambda q, p=p, o=o, nt=nt: q.activation(
                out=o[:, 0:nt], in_=p.h[:, 0:nt], func=AF.Silu),
                reads=[p], writes=[o])
            kb.dma("sync", o_sz1.ap(m * 128 * TC + t0, [[TC, 128], [1, nt]]),
                   o[:, 0:nt], reads=[o], writes=[o_sz1], stream=f"d_osz{ci % 2}")
    return kb.finalize()


def stage_l1a_pre(inputs, prev):
    W_in1 = np.ascontiguousarray(inputs["L1_attn_W_in"])
    maps = []
    for c in range(NCORE):
        maps.append(dict(yln3=prev["o_yln3"][c], W_in1=W_in1))
    return maps



def build_l1a_mid():
    kb = KB("l1a_mid")
    xih = kb.inp("xih", [DI, TC + 3], BF16)
    cw4 = kb.inp("cw4", [128, 64])
    cbv = kb.inp("cbv", [128, 16])
    W_xproj = kb.inp("W_xproj", [DI, 2 * NS])
    W_dt = kb.inp("W_dt", [DI, DI])
    bdt = kb.inp("bdt", [128, 16])

    o_xc1 = kb.out("o_xc1", [DI, TC], BF16)
    o_bt = kb.out("o_bt", [NS, TC])
    o_delta = kb.out("o_delta", [DI, TC])

    cwb = kb.sbuf("cwb", [128, 64])
    kb.dma("sync", cwb[:, :], cw4.ap(0, [[64, 128], [1, 64]]), reads=[cw4], writes=[cwb])
    cbb = kb.sbuf("cbb", [128, 16])
    kb.dma("sync", cbb[:, :], cbv.ap(0, [[16, 128], [1, 16]]), reads=[cbv], writes=[cbb])
    bdtb = kb.sbuf("bdtb", [128, 16])
    kb.dma("sync", bdtb[:, :], bdt.ap(0, [[16, 128], [1, 16]]), reads=[bdt], writes=[bdtb])

    xc = [kb.sbuf(f"xct{m}", [128, TC], BF16) for m in range(16)]
    xihb = [kb.sbuf(f"xihb{m}", [128, TC + 3], BF16, side="right") for m in range(16)]
    convb = [kb.sbuf(f"convb{i}", [128, TC]) for i in range(2)]
    for m in range(16):
        kb.dma("sync", xihb[m][:, :],
               xih.ap(m * 128 * (TC + 3), [[TC + 3, 128], [1, TC + 3]]),
               reads=[xih], writes=[xihb[m]])
        cvb = convb[m % 2]
        kb.op("vector", lambda q, m=m, cvb=cvb: q.tensor_scalar(
            out=cvb[:, :], in0=xihb[m][:, 0:TC],
            scalar1=cwb.h[:, m * 4:m * 4 + 1], scalar2=None, op0=ALU.mult),
            reads=[xihb[m], cwb], writes=[cvb])
        for tap in range(1, 4):
            kb.op("vector", lambda q, m=m, cvb=cvb, tap=tap: q.scalar_tensor_tensor(
                out=cvb[:, :], in0=xihb[m][:, tap:tap + TC],
                scalar=cwb.h[:, m * 4 + tap:m * 4 + tap + 1],
                in1=cvb[:, :], op0=ALU.mult, op1=ALU.add),
                reads=[xihb[m], cwb, cvb], writes=[cvb])
        kb.op("scalar", lambda q, m=m, cvb=cvb: q.activation(
            out=xc[m][:, :], in_=cvb[:, :], func=AF.Silu,
            bias=cbb.h[:, m:m + 1]), reads=[cvb, cbb], writes=[xc[m]])
        kb.dma("sync", o_xc1.ap(m * 128 * TC, [[TC, 128], [1, TC]]),
               xc[m][:, :], reads=[xc[m]], writes=[o_xc1], stream="d_oxc")
    kb.free(*xihb)

    wxp = kb.sbuf("wxp", [128, 16 * 2 * NS], BF16)
    kb.dma("gpsimd", bass.AP(wxp.h, 0, [[16 * 2 * NS, 128], [2 * NS, 16], [1, 2 * NS]]),
           W_xproj.ap(0, [[2 * NS, 128], [128 * 2 * NS, 16], [1, 2 * NS]]),
           reads=[W_xproj], writes=[wxp])
    ps32 = kb.psum("ps32", [32, 512])
    spb = [kb.sbuf(f"spb{i}", [32, 512]) for i in range(2)]
    for ci, (t0, nt) in enumerate(CHUNKS4):
        pairs = [(wxp.ap(kd * 2 * NS, [[16 * 2 * NS, 128], [1, 2 * NS]]),
                  xc[kd][:, t0:t0 + nt]) for kd in range(16)]
        matmul_group(kb, ps32, ps32.h[:, 0:nt], pairs, [wxp] + xc)
        s = spb[ci % 2]
        kb.op("vector", lambda q, s=s, nt=nt: q.tensor_copy(
            out=s[:, 0:nt], in_=ps32.h[:, 0:nt]), reads=[ps32], writes=[s])
        kb.dma("sync", o_bt.ap(t0, [[TC, NS], [1, nt]]),
               s.h[NS:2 * NS, 0:nt], reads=[s], writes=[o_bt], stream=f"d_obt{ci % 2}")

    wdt = kb.sbuf("wdt", [128, 16 * DI], BF16)
    kb.dma("gpsimd", bass.AP(wdt.h, 0, [[16 * DI, 128], [DI, 16], [1, DI]]),
           W_dt.ap(0, [[DI, 128], [128 * DI, 16], [1, DI]]),
           reads=[W_dt], writes=[wdt])
    ps = [kb.psum(f"ps{i}", [128, 512]) for i in range(2)]
    t1 = kb.sbuf("t1", [128, 512])
    t2 = kb.sbuf("t2", [128, 512])
    t3 = kb.sbuf("t3", [128, 512])
    dto = [kb.sbuf(f"dto{i}", [128, 512]) for i in range(2)]
    for m in range(16):
        for ci, (t0, nt) in enumerate(CHUNKS4):
            p = ps[ci % 2]
            d = dto[ci % 2]
            pairs = [(wdt.ap(kd * DI + m * 128, [[16 * DI, 128], [1, 128]]),
                      xc[kd][:, t0:t0 + nt]) for kd in range(16)]
            matmul_group(kb, p, p.h[:, 0:nt], pairs, [wdt] + xc)
            kb.op("scalar", lambda q, p=p, m=m: q.activation(
                out=t1[:, :], in_=p.h[:, 0:512], func=AF.Abs,
                bias=bdtb.h[:, m:m + 1]), reads=[p, bdtb], writes=[t1])
            kb.op("scalar", lambda q: q.activation(
                out=t2[:, :], in_=t1[:, :], func=AF.Exp, scale=-1.0),
                reads=[t1], writes=[t2])
            kb.op("scalar", lambda q: q.activation(
                out=t1[:, :], in_=t2[:, :], func=AF.Ln, bias=1.0),
                reads=[t2], writes=[t1])
            kb.op("scalar", lambda q, p=p, m=m: q.activation(
                out=t3[:, :], in_=p.h[:, 0:512], func=AF.Relu,
                bias=bdtb.h[:, m:m + 1]), reads=[p, bdtb], writes=[t3])
            kb.op("vector", lambda q, d=d: q.tensor_tensor(
                out=d[:, :], in0=t1[:, :], in1=t3[:, :], op=ALU.add),
                reads=[t1, t3], writes=[d])
            kb.dma("sync", o_delta.ap(m * 128 * TC + t0, [[TC, 128], [1, nt]]),
                   d[:, :], reads=[d], writes=[o_delta], stream=f"d_odel{ci % 2}")
    return kb.finalize()


def stage_l1a_mid(inputs, prev):
    cw4 = np.ascontiguousarray(
        inputs["L1_attn_conv_w"].reshape(16, 128, 4).transpose(1, 0, 2).reshape(128, 64))
    cbv = np.ascontiguousarray(inputs["L1_attn_conv_b"].reshape(16, 128).T)
    bdt = np.ascontiguousarray(inputs["L1_attn_b_dt"].reshape(16, 128).T)
    W_xproj = np.ascontiguousarray(inputs["L1_attn_W_xproj"])
    W_dt = np.ascontiguousarray(inputs["L1_attn_W_dt"])
    maps = []
    for c in range(NCORE):
        b, h = divmod(c, 2)
        xi = np.asarray(prev["o_xi"][c])
        if h == 1:
            halo = np.asarray(prev["o_xi"][c - 1])[:, -3:]
        else:
            halo = np.zeros((DI, 3), xi.dtype)
        xih = np.ascontiguousarray(np.concatenate([halo, xi], axis=1))
        maps.append(dict(xih=xih, cw4=cw4, cbv=cbv, W_xproj=W_xproj,
                         W_dt=W_dt, bdt=bdt))
    return maps



def build_l1a_scan():
    kb = KB("l1a_scan")
    delta2 = kb.inp("delta2", [D, TFULL])
    xc2 = kb.inp("xc2", [D, TFULL], BF16)
    BT2 = kb.inp("BT2", [NS, TFULL])
    Alog = kb.inp("Alog", [128, 8 * NS])
    Dps = kb.inp("Dps", [128, 8])
    o_y = kb.out("o_y", [D, TFULL], BF16)

    alg = kb.sbuf("alg", [128, 8 * NS])
    kb.dma("sync", alg[:, :], Alog.ap(0, [[8 * NS, 128], [1, 8 * NS]]),
           reads=[Alog], writes=[alg])
    dpb = kb.sbuf("dpb", [128, 8])
    kb.dma("sync", dpb[:, :], Dps.ap(0, [[8, 128], [1, 8]]), reads=[Dps], writes=[dpb])
    aa = kb.sbuf("aa", [128, 8 * NS])
    kb.op("scalar", lambda q: q.activation(out=aa[:, :], in_=alg[:, :], func=AF.Exp),
          reads=[alg], writes=[aa])
    kb.op("vector", lambda q: q.tensor_scalar(
        out=aa[:, :], in0=aa[:, :], scalar1=-1.0, scalar2=None, op0=ALU.mult),
        reads=[aa], writes=[aa])

    TCH = 2048
    bbc = [kb.sbuf(f"bbc{n}", [128, TCH], BF16) for n in range(NS)]
    scar = [kb.sbuf(f"scar{db}", [128, NS]) for db in range(8)]
    for db in range(8):
        kb.op("vector", lambda q, db=db: q.memset(scar[db].h[:, :], 0.0),
              reads=[], writes=[scar[db]])

    dl = [kb.sbuf(f"dl{i}", [128, TCH]) for i in range(2)]
    xcb = [kb.sbuf(f"xcb{i}", [128, TCH], BF16) for i in range(2)]
    dx = [kb.sbuf(f"dx{i}", [128, TCH], BF16) for i in range(2)]
    ab = [kb.sbuf(f"ab{i}", [128, TCH]) for i in range(2)]
    ub = [kb.sbuf(f"ub{i}", [128, TCH], BF16) for i in range(2)]
    sb = [kb.sbuf(f"sb{i}", [128, TCH], BF16) for i in range(2)]
    sm = kb.sbuf("sm", [128, TCH], BF16)
    ya = [kb.sbuf(f"ya{i}", [128, TCH]) for i in range(2)]
    for tc in range(2):
        for n in range(NS):
            kb.dma("gpsimd", bbc[n][:, :],
                   BT2.ap(n * TFULL + tc * TCH, [[0, 128], [1, TCH]]),
                   reads=[BT2], writes=[bbc[n]])
        for db in range(8):
            d = dl[db % 2]
            x = xcb[db % 2]
            dxb = dx[db % 2]
            y = ya[db % 2]
            kb.dma("sync", d[:, :],
                   delta2.ap(db * 128 * TFULL + tc * TCH, [[TFULL, 128], [1, TCH]]),
                   reads=[delta2], writes=[d], stream=f"d_dl{db % 2}")
            kb.dma("sync", x[:, :],
                   xc2.ap(db * 128 * TFULL + tc * TCH, [[TFULL, 128], [1, TCH]]),
                   reads=[xc2], writes=[x], stream=f"d_xc{db % 2}")
            kb.op("vector", lambda q, d=d, x=x, dxb=dxb: q.tensor_tensor(
                out=dxb[:, :], in0=d[:, :], in1=x[:, :], op=ALU.mult),
                reads=[d, x], writes=[dxb])
            kb.op("vector", lambda q, x=x, y=y, db=db: q.tensor_scalar(
                out=y[:, :], in0=x[:, :], scalar1=dpb.h[:, db:db + 1], scalar2=None,
                op0=ALU.mult), reads=[x, dpb], writes=[y])
            for n in range(NS):
                a = ab[n % 2]
                u = ub[n % 2]
                s = sb[n % 2]
                kb.op("scalar", lambda q, d=d, a=a, db=db, n=n: q.activation(
                    out=a[:, :], in_=d[:, :], func=AF.Exp,
                    scale=aa.h[:, db * NS + n:db * NS + n + 1]),
                    reads=[d, aa], writes=[a])
                kb.op("vector", lambda q, dxb=dxb, u=u, n=n: q.tensor_tensor(
                    out=u[:, :], in0=dxb[:, :], in1=bbc[n][:, :], op=ALU.mult),
                    reads=[dxb, bbc[n]], writes=[u])
                if tc == 0:
                    kb.op("vector", lambda q, a=a, u=u, s=s: q.tensor_tensor_scan(
                        out=s[:, :], data0=a[:, :], data1=u[:, :], initial=0.0,
                        op0=ALU.mult, op1=ALU.add), reads=[a, u], writes=[s])
                else:
                    kb.op("vector", lambda q, a=a, u=u, s=s, db=db, n=n:
                          q.tensor_tensor_scan(
                              out=s[:, :], data0=a[:, :], data1=u[:, :],
                              initial=scar[db].h[:, n:n + 1],
                              op0=ALU.mult, op1=ALU.add),
                          reads=[a, u, scar[db]], writes=[s])
                if tc == 0:
                    kb.op("vector", lambda q, s=s, db=db, n=n: q.tensor_copy(
                        out=scar[db].h[:, n:n + 1], in_=s[:, TCH - 1:TCH]),
                        reads=[s], writes=[scar[db]])
                kb.op("vector", lambda q, s=s, n=n: q.tensor_tensor(
                    out=sm[:, :], in0=s[:, :], in1=bbc[n][:, :], op=ALU.mult),
                    reads=[s, bbc[n]], writes=[sm])
                kb.op("vector", lambda q, y=y: q.tensor_tensor(
                    out=y[:, :], in0=y[:, :], in1=sm[:, :], op=ALU.add),
                    reads=[y, sm], writes=[y])
            kb.dma("gpsimd", o_y.ap(db * 128 * TFULL + tc * TCH, [[TFULL, 128], [1, TCH]]),
                   y[:, :], reads=[y], writes=[o_y], stream=f"d_oy{db % 2}")
    return kb.finalize()


def stage_l1a_scan(inputs, prev):
    A_log = np.asarray(inputs["L1_attn_A_log"])
    Dp = np.asarray(inputs["L1_attn_Dp"])
    maps = []
    for c in range(NCORE):
        b, h = divmod(c, 2)
        hs = slice(h * D, (h + 1) * D)
        delta2 = np.concatenate([np.asarray(prev["o_delta"][2 * b])[hs],
                                 np.asarray(prev["o_delta"][2 * b + 1])[hs]], 1)
        xc2 = np.concatenate([np.asarray(prev["o_xc1"][2 * b])[hs],
                              np.asarray(prev["o_xc1"][2 * b + 1])[hs]], 1)
        BT2 = np.concatenate([np.asarray(prev["o_bt"][2 * b]),
                              np.asarray(prev["o_bt"][2 * b + 1])], 1)
        Al = A_log[hs].reshape(8, 128, NS).transpose(1, 0, 2).reshape(128, 8 * NS)
        Dps = Dp[hs].reshape(8, 128).T
        maps.append(dict(delta2=np.ascontiguousarray(delta2),
                         xc2=np.ascontiguousarray(xc2),
                         BT2=np.ascontiguousarray(BT2),
                         Alog=np.ascontiguousarray(Al),
                         Dps=np.ascontiguousarray(Dps)))
    return maps



def build_l1a_post():
    kb = KB("l1a_post")
    y2 = kb.inp("y2", [DI, TC], BF16)
    sz1 = kb.inp("sz1", [DI, TC], BF16)
    W_out1 = kb.inp("W_out1", [DI, D])
    x2s = kb.inp("x2s", [TC, D])
    wg_s = kb.inp("wg_s", [352, D])
    wu_s = kb.inp("wu_s", [352, D])
    wd_s = kb.inp("wd_s", [128, 2816])

    o_x3 = kb.out("o_x3", [TC, D])
    o_xq = kb.out("o_xq", [TC, D], BF16)
    o_wgq = kb.out("o_wgq", [352, D], BF16)
    o_wuq = kb.out("o_wuq", [352, D], BF16)
    o_wdq = kb.out("o_wdq", [128, 2816], BF16)

    ygt = [kb.sbuf(f"ygt{k}", [128, TC], BF16) for k in range(16)]
    szb = [kb.sbuf(f"szb{i}", [128, TC], BF16) for i in range(2)]
    for k in range(16):
        s = szb[k % 2]
        kb.dma("sync", ygt[k][:, :], y2.ap(k * 128 * TC, [[TC, 128], [1, TC]]),
               reads=[y2], writes=[ygt[k]])
        kb.dma("sync", s[:, :], sz1.ap(k * 128 * TC, [[TC, 128], [1, TC]]),
               reads=[sz1], writes=[s], stream=f"d_sz{k % 2}")
        kb.op("vector", lambda q, k=k, s=s: q.tensor_tensor(
            out=ygt[k][:, :], in0=ygt[k][:, :], in1=s[:, :], op=ALU.mult),
            reads=[ygt[k], s], writes=[ygt[k]])
    wo1 = kb.sbuf("wo1", [128, 16 * D], BF16)
    kb.dma("gpsimd", bass.AP(wo1.h, 0, [[16 * D, 128], [D, 16], [1, D]]),
           W_out1.ap(0, [[D, 128], [128 * D, 16], [1, D]]),
           reads=[W_out1], writes=[wo1])

    ps = [kb.psum(f"ps{i}", [128, 512]) for i in range(2)]
    x2t = [kb.sbuf(f"x2t{i}", [128, D]) for i in range(2)]
    x3t = [kb.sbuf(f"x3t{i}", [128, D]) for i in range(2)]
    xl4 = [kb.sbuf(f"xl4{i}", [128, D]) for i in range(2)]
    xqt = [kb.sbuf(f"xqt{i}", [128, D], BF16) for i in range(2)]
    sq = [kb.sbuf("sq", [128, D])]
    stat = kb.sbuf("stat", [128, 8])
    qs = dict(ax=kb.sbuf("q_ax", [128, 2816]), qa=kb.sbuf("q_qa", [128, 2816]),
              tt=kb.sbuf("q_tt", [128, 2816]), rs=kb.sbuf("q_rs", [128, 3 * 176]))
    for j in range(16):
        x2b, x3b, xlb, xqb = x2t[j % 2], x3t[j % 2], xl4[j % 2], xqt[j % 2]
        kb.dma("sync", x2b[:, :], x2s.ap(j * 128 * D, [[D, 128], [1, D]]),
               reads=[x2s], writes=[x2b], stream=f"d_x2{j % 2}")
        for ci in range(2):
            p = ps[ci]
            pairs = [(ygt[k][:, j * 128:(j + 1) * 128],
                      wo1.ap(k * D + ci * 512, [[16 * D, 128], [1, 512]]))
                     for k in range(16)]
            matmul_group(kb, p, p.h[:, 0:512], pairs, ygt + [wo1])
            kb.op("vector", lambda q, p=p, x3b=x3b, x2b=x2b, ci=ci: q.tensor_tensor(
                out=x3b[:, ci * 512:(ci + 1) * 512],
                in0=x2b[:, ci * 512:(ci + 1) * 512],
                in1=p.h[:, 0:512], op=ALU.add), reads=[p, x2b], writes=[x3b])
        kb.dma("sync", o_x3.ap(j * 128 * D, [[D, 128], [1, D]]), x3b[:, :],
               reads=[x3b], writes=[o_x3], stream=f"d_ox3{j % 2}")
        mu_ap, rstd_ap = e_layernorm(kb, x3b, x3b[:, :], stat, sq)
        kb.op("vector", lambda q, x3b=x3b, xlb=xlb, mu_ap=mu_ap, rstd_ap=rstd_ap:
              q.tensor_scalar(out=xlb[:, :], in0=x3b[:, :], scalar1=mu_ap,
                              scalar2=rstd_ap, op0=ALU.subtract, op1=ALU.mult),
              reads=[x3b, stat], writes=[xlb])
        e_fp4(kb, xlb, xlb[:, :], 128, D, qs, xqb, xqb[:, :])
        kb.dma("sync", o_xq.ap(j * 128 * D, [[D, 128], [1, D]]), xqb[:, :],
               reads=[xqb], writes=[o_xq], stream=f"d_oxq{j % 2}")

    wqt = kb.sbuf("wqt", [128, 2816])
    wqo = kb.sbuf("wqo", [128, 2816], BF16)
    for (win, wout, rows, wdth) in ((wg_s, o_wgq, 352, D), (wu_s, o_wuq, 352, D),
                                    (wd_s, o_wdq, 128, 2816)):
        r0 = 0
        while r0 < rows:
            pr = min(128, rows - r0)
            kb.dma("sync", wqt.h[0:pr, 0:wdth],
                   win.ap(r0 * wdth, [[wdth, pr], [1, wdth]]),
                   reads=[win], writes=[wqt], stream="d_wq")
            e_fp4(kb, wqt, wqt.h[0:pr, 0:wdth], pr, wdth, qs, wqo,
                  wqo.h[0:pr, 0:wdth])
            kb.dma("sync", wout.ap(r0 * wdth, [[wdth, pr], [1, wdth]]),
                   wqo.h[0:pr, 0:wdth], reads=[wqo], writes=[wout], stream="d_wqo")
            r0 += pr
    return kb.finalize()


def stage_l1a_post(inputs, prev):
    W_out1 = np.ascontiguousarray(inputs["L1_attn_W_out"])
    wg = np.asarray(inputs["L1_ffn_w_gate"])
    wu = np.asarray(inputs["L1_ffn_w_up"])
    wd = np.asarray(inputs["L1_ffn_w_down"])
    maps = []
    for c in range(NCORE):
        b, h = divmod(c, 2)
        y2 = np.concatenate([np.asarray(prev["o_y"][2 * b])[:, h * TC:(h + 1) * TC],
                             np.asarray(prev["o_y"][2 * b + 1])[:, h * TC:(h + 1) * TC]], 0)
        maps.append(dict(
            y2=np.ascontiguousarray(y2), sz1=prev["o_sz1"][c], W_out1=W_out1,
            x2s=prev["o_x2"][c],
            wg_s=np.ascontiguousarray(wg[c * 352:(c + 1) * 352]),
            wu_s=np.ascontiguousarray(wu[c * 352:(c + 1) * 352]),
            wd_s=np.ascontiguousarray(wd[c * 128:(c + 1) * 128])))
    return maps



FFI = 2816


def build_l1_ffn():
    kb = KB("l1_ffn")
    xq = kb.inp("xq", [TC, D], BF16)
    wgq = kb.inp("wgq", [FFI, D], BF16)
    wuq = kb.inp("wuq", [FFI, D], BF16)
    wdq = kb.inp("wdq", [D, FFI], BF16)
    x3s = kb.inp("x3s", [TC, D])
    o_x4 = kb.out("o_x4", [TC, D])

    hq_dram = kb.dram("hq_dram", [TC, FFI], BF16)

    xqT = [kb.sbuf(f"xqT{k}", [128, TC], BF16, side="right") for k in range(8)]
    for kd in range(8):
        kb.dma("sync", xqT[kd][:, :], xq.ap(kd * 128, [[D, TC], [1, 128]]),
               reads=[xq], writes=[xqT[kd]], transpose=True)
    wgT = [kb.sbuf(f"wgT{k}", [128, FFI], BF16, side="right") for k in range(8)]
    wuT = [kb.sbuf(f"wuT{k}", [128, FFI], BF16, side="right") for k in range(8)]
    for kd in range(8):
        kb.dma("sync", wgT[kd][:, :], wgq.ap(kd * 128, [[D, FFI], [1, 128]]),
               reads=[wgq], writes=[wgT[kd]], transpose=True)
        kb.dma("sync", wuT[kd][:, :], wuq.ap(kd * 128, [[D, FFI], [1, 128]]),
               reads=[wuq], writes=[wuT[kd]], transpose=True)

    ps = [kb.psum(f"ps{i}", [128, 512]) for i in range(4)]
    sg = kb.sbuf("sg", [128, FFI], BF16)
    upb = kb.sbuf("upb", [128, FFI], BF16)
    ht = kb.sbuf("ht", [128, FFI])
    hqb = [kb.sbuf(f"hqb{i}", [128, FFI], BF16) for i in range(2)]
    qs = dict(ax=kb.sbuf("q_ax", [128, FFI]), qa=kb.sbuf("q_qa", [128, FFI]),
              tt=kb.sbuf("q_tt", [128, FFI]), rs=kb.sbuf("q_rs", [128, 3 * 176]))
    FCH = [(0, 512), (512, 512), (1024, 512), (1536, 512), (2048, 512), (2560, 256)]
    for j in range(16):
        hqo = hqb[j % 2]
        for (f0, nf) in FCH:
            p = ps[0]
            pairs = [(xqT[kd][:, j * 128:(j + 1) * 128], wgT[kd][:, f0:f0 + nf])
                     for kd in range(8)]
            matmul_group(kb, p, p.h[:, 0:nf], pairs, xqT + wgT)
            kb.op("scalar", lambda q, p=p, f0=f0, nf=nf: q.activation(
                out=sg[:, f0:f0 + nf], in_=p.h[:, 0:nf], func=AF.Silu),
                reads=[p], writes=[sg])
            p2 = ps[1]
            pairs = [(xqT[kd][:, j * 128:(j + 1) * 128], wuT[kd][:, f0:f0 + nf])
                     for kd in range(8)]
            matmul_group(kb, p2, p2.h[:, 0:nf], pairs, xqT + wuT)
            kb.op("vector", lambda q, p2=p2, f0=f0, nf=nf: q.tensor_copy(
                out=upb[:, f0:f0 + nf], in_=p2.h[:, 0:nf]), reads=[p2], writes=[upb])
        kb.op("vector", lambda q: q.tensor_tensor(
            out=ht[:, :], in0=sg[:, :], in1=upb[:, :], op=ALU.mult),
            reads=[sg, upb], writes=[ht])
        e_fp4(kb, ht, ht[:, :], 128, FFI, qs, hqo, hqo[:, :])
        kb.dma("sync", hq_dram.ap(j * 128 * FFI, [[FFI, 128], [1, FFI]]),
               hqo[:, :], reads=[hqo], writes=[hq_dram], stream=f"d_hq{j % 2}")

    kb.free(*xqT, *wgT, *wuT)
    kb.free(sg, upb, ht, hqb[0], hqb[1], qs["ax"], qs["qa"], qs["tt"], qs["rs"])
    hqT = [kb.sbuf(f"hqT{k}", [128, TC], BF16, side="right") for k in range(22)]
    wdT = [kb.sbuf(f"wdT{k}", [128, D], BF16, side="right") for k in range(22)]
    for kf in range(22):
        kb.dma("sync", hqT[kf][:, :], hq_dram.ap(kf * 128, [[FFI, TC], [1, 128]]),
               reads=[hq_dram], writes=[hqT[kf]], transpose=True)
        kb.dma("sync", wdT[kf][:, :], wdq.ap(kf * 128, [[FFI, D], [1, 128]]),
               reads=[wdq], writes=[wdT[kf]], transpose=True)
    x3t = [kb.sbuf(f"x3t{i}", [128, D]) for i in range(2)]
    for j in range(16):
        x3b = x3t[j % 2]
        kb.dma("sync", x3b[:, :], x3s.ap(j * 128 * D, [[D, 128], [1, D]]),
               reads=[x3s], writes=[x3b], stream=f"d_x3{j % 2}")
        for ci in range(2):
            p = ps[2 + ci]
            pairs = [(hqT[kf][:, j * 128:(j + 1) * 128],
                      wdT[kf][:, ci * 512:(ci + 1) * 512]) for kf in range(22)]
            matmul_group(kb, p, p.h[:, 0:512], pairs, hqT + wdT)
            kb.op("vector", lambda q, p=p, x3b=x3b, ci=ci: q.tensor_tensor(
                out=x3b[:, ci * 512:(ci + 1) * 512],
                in0=x3b[:, ci * 512:(ci + 1) * 512],
                in1=p.h[:, 0:512], op=ALU.add), reads=[p], writes=[x3b])
        kb.dma("sync", o_x4.ap(j * 128 * D, [[D, 128], [1, D]]), x3b[:, :],
               reads=[x3b], writes=[o_x4], stream=f"d_ox4{j % 2}")
    return kb.finalize()


def stage_l1_ffn(inputs, prev):
    wgq = np.concatenate([np.asarray(prev["o_wgq"][c]) for c in range(8)], 0)
    wuq = np.concatenate([np.asarray(prev["o_wuq"][c]) for c in range(8)], 0)
    wdq = np.concatenate([np.asarray(prev["o_wdq"][c]) for c in range(8)], 0)
    maps = []
    for c in range(NCORE):
        maps.append(dict(xq=prev["o_xq"][c], wgq=wgq, wuq=wuq, wdq=wdq,
                         x3s=prev["o_x3"][c]))
    return maps


def kernel(**inputs):
    inputs = {k: np.asarray(v) for k, v in inputs.items()}
    prev = {}

    def runl(build, maps):
        nc = build()
        res = run_bass_kernel_spmd(nc, maps, core_ids=list(range(NCORE)))
        for k in res.results[0]:
            prev[k] = [res.results[c][k] for c in range(NCORE)]

    runl(build_l0a_pre, stage_l0a_pre(inputs))
    runl(build_l0a_scan, stage_l0a_scan(prev))
    runl(build_l0a_post, stage_l0a_post(inputs, prev))
    runl(build_moe_route2, stage_moe_route2(prev))
    runl(build_moe_expert, stage_moe_expert(inputs, prev))
    runl(build_moe_combine, stage_moe_combine(inputs, prev))
    runl(build_l1a_pre, stage_l1a_pre(inputs, prev))
    runl(build_l1a_mid, stage_l1a_mid(inputs, prev))
    runl(build_l1a_scan, stage_l1a_scan(inputs, prev))
    runl(build_l1a_post, stage_l1a_post(inputs, prev))
    runl(build_l1_ffn, stage_l1_ffn(inputs, prev))
    out = np.concatenate([np.asarray(prev["o_x4"][c], np.float32) for c in range(NCORE)], 0)
    return np.ascontiguousarray(out.reshape(4, 4096, 1024))


# version 02
# speedup vs baseline: 1.0041x; 1.0041x over previous
import numpy as np
import ml_dtypes
from contextlib import ExitStack
import concourse.bass as bass
import concourse.mybir as mybir
from concourse.bass_utils import run_bass_kernel_spmd

F32 = mybir.dt.float32
BF16 = mybir.dt.bfloat16
I32 = mybir.dt.int32
AF = mybir.ActivationFunctionType
ALU = mybir.AluOpType
AX = mybir.AxisListType
NPBF16 = ml_dtypes.bfloat16

NCORE = 8
TC = 2048
D = 1024
DI = 2048
NS = 16
CAP = 4096
EXP = 8
LN_EPS = 1e-5


ENGINES = ("sync", "gpsimd", "vector", "scalar", "tensor")


class Buf:
    __slots__ = ("h", "writer", "readers", "name", "kb", "cm", "stream")

    def __init__(self, kb, h, name):
        self.kb = kb
        self.h = h
        self.name = name
        self.cm = None
        self.stream = None
        self.writer = None
        self.readers = []

    def __getitem__(self, idx):
        return self.h[idx]

    def ap(self, offset, pattern):
        return bass.AP(self.h, offset, pattern)


class KB:
    def __init__(self, name="k"):
        self.nc = bass.Bass("TRN2", target_bir_lowering=False)
        self.name = name
        self.ctx = ExitStack()
        self.thunks = {e: [] for e in ENGINES}
        self.semh = {}
        self.ctr = {}
        self.waited = {e: {} for e in ENGINES}
        self._constbuf = None
        self._consts = {}
        self._open = []
        self._freed_tokens = []
        self._spool = []
        self._snext = 0
        self.prefix = ""
        self._gpreg = None
        self._gpreg2 = None

    def sem(self, name):
        if name not in self.semh:
            self.semh[name] = self.ctx.enter_context(self.nc.semaphore(name))
            self.ctr[name] = 0
        return self.semh[name]

    def sbuf(self, name, shape, dtype=F32, side=None):
        name = self.prefix + name
        if side is None:
            cm = self.nc.sbuf_tensor(name, list(shape), dtype)
        else:
            cm = self.nc.sbuf_tensor(name, list(shape), dtype, side=side)
        h = cm.__enter__()
        b = Buf(self, h, name)
        b.cm = cm
        b.readers = list(self._freed_tokens)
        self._open.append(b)
        return b

    def free(self, *bufs):
        todo = [b for b in bufs if b.cm is not None]
        todo.sort(key=lambda b: self._open.index(b), reverse=True)
        for b in todo:
            if b.writer:
                self._freed_tokens.append(b.writer)
            self._freed_tokens.extend(b.readers)
            if b.stream is not None:
                self._spool.append(b.stream)
                b.stream = None
            b.cm.__exit__(None, None, None)
            b.cm = None
            self._open.remove(b)

    def psum(self, name, shape, dtype=F32):
        name = self.prefix + name
        cm = self.nc.psum_tensor(name, list(shape), dtype)
        h = cm.__enter__()
        b = Buf(self, h, name)
        b.cm = cm
        b.readers = list(self._freed_tokens)
        self._open.append(b)
        return b

    def mark(self):
        return len(self._open)

    def free_phase(self, mark):
        keep = []
        drop = []
        for i, b in enumerate(self._open):
            (drop if i >= mark and b is not self._constbuf else keep).append(b)
        self.free(*drop)

    def dram(self, name, shape, dtype=F32, kind=None):
        if kind is None:
            t = self.nc.dram_tensor(name, list(shape), dtype)
        else:
            t = self.nc.dram_tensor(name, list(shape), dtype, kind=kind)
        return Buf(self, t, name)

    def inp(self, name, shape, dtype=F32):
        return self.dram(name, shape, dtype, kind="ExternalInput")

    def out(self, name, shape, dtype=F32):
        return self.dram(name, shape, dtype, kind="ExternalOutput")

    def constcol(self, value):
        if self._constbuf is None:
            self._constbuf = self.sbuf("_consts", [128, 16])
        if value not in self._consts:
            ci = len(self._consts)
            assert ci < 16
            self._consts[value] = ci
            cb = self._constbuf
            self.op("vector", lambda q, ci=ci, value=value: q.memset(
                cb.h[:, ci:ci + 1], value), reads=[], writes=[cb])
        ci = self._consts[value]
        return self._constbuf.h[:, ci:ci + 1], self._constbuf

    def _deps(self, reads, writes, extra):
        deps = []
        for b in reads:
            if b.writer:
                deps.append(b.writer)
        for b in writes:
            if b.writer:
                deps.append(b.writer)
            deps.extend(b.readers)
        deps.extend(extra)
        return deps

    def _emit(self, eng, fn, deps, sem, inc):
        self.sem(sem)
        needed = []
        for (s, v) in deps:
            if v > self.waited[eng].get(s, 0):
                needed.append((s, v))
                self.waited[eng][s] = v
        self.ctr[sem] += inc
        val = self.ctr[sem]
        semh = self.semh

        def thunk(q, needed=tuple(needed), fn=fn, sem=sem, inc=inc):
            for (s, v) in needed:
                q.wait_ge(semh[s], v)
            inst = fn(q)
            if inst is not None:
                inst.then_inc(semh[sem], inc)
        self.thunks[eng].append(thunk)
        return (sem, val)

    def _commit(self, tok, reads, writes):
        for b in writes:
            b.writer = tok
            b.readers = []
        for b in reads:
            b.readers.append(tok)

    def op(self, eng, fn, reads=(), writes=(), after=()):
        tok = self._emit(eng, fn, self._deps(reads, writes, after), "_" + eng, 1)
        self._commit(tok, reads, writes)
        return tok

    def _stream_of(self, b):
        if b.stream is None:
            if self._spool:
                b.stream = self._spool.pop()
            else:
                b.stream = f"d_s{self._snext}"
                self._snext += 1
        return b.stream

    def gpregs(self):
        if self._gpreg is None:
            self._gpreg = self.nc.gpsimd.alloc_register("dynbase")
            self._gpreg2 = self.nc.gpsimd.alloc_register("dynoff")
        return self._gpreg, self._gpreg2

    def dyn_load(self, dst_ap, src_h, const_off, pattern,
                 reads=(), writes=(), after=(), stream=None):
        sname = stream or self._stream_of(writes[0])
        r1, r2 = self.gpregs()

        def fn(qq, const_off=const_off, pattern=pattern, dst_ap=dst_ap):
            qq.reg_add(r2, r1, const_off)
            return qq.dma_start(out=dst_ap, in_=bass.AP(src_h, r2, pattern))
        tok = self._emit("gpsimd", fn, self._deps(reads, writes, after), sname, 16)
        self._commit(tok, reads, writes)
        return tok

    def load_dynbase(self, src_buf, sb_scratch):
        r1, r2 = self.gpregs()
        self.dma("gpsimd", sb_scratch[0:1, 0:1],
                 src_buf.ap(0, [[1, 1], [1, 1]]), reads=[src_buf], writes=[sb_scratch])

        def fn(qq):
            qq.reg_load(r1, sb_scratch.h[0:1, 0:1])
            return None
        self.op("gpsimd", fn, reads=[sb_scratch], writes=[])

    def dma(self, q, out_ap, in_ap, reads=(), writes=(), after=(), stream=None, **kw):
        assert writes, "dma needs writes for stream naming"
        sname = (self.prefix + stream) if stream else self._stream_of(writes[0])

        def fn(qq, out_ap=out_ap, in_ap=in_ap, kw=kw):
            return qq.dma_start(out=out_ap, in_=in_ap, **kw)
        tok = self._emit(q, fn, self._deps(reads, writes, after), sname, 16)
        self._commit(tok, reads, writes)
        return tok

    def idma(self, out_ap, in_ap, out_off=None, in_off=None, bounds=None,
             reads=(), writes=(), after=(), stream=None):
        sname = (self.prefix + stream) if stream else self._stream_of(writes[0])

        def fn(qq):
            return qq.indirect_dma_start(
                out_ap, out_off, in_ap, in_off,
                bounds_check=bounds, oob_is_err=False)
        tok = self._emit("gpsimd", fn, self._deps(reads, writes, after), sname, 16)
        self._commit(tok, reads, writes)
        return tok

    def collective(self, kind, op, groups, in_buf, out_buf, extra_reads=()):
        def fn(qq):
            return qq.collective_compute(
                kind, op, replica_groups=groups,
                ins=[in_buf.h.ap().opt()], outs=[out_buf.h.ap().opt()])
        sname = self._stream_of(out_buf)
        deps = self._deps([in_buf] + list(extra_reads), [out_buf], ())
        tok = self._emit("gpsimd", fn, deps, sname, 16)
        self._commit(tok, [in_buf], [out_buf])
        return tok

    def finalize(self):
        final_waits = [(s, v) for s, v in self.ctr.items()
                       if s.startswith("d_") and v > 0]
        semh = self.semh

        def thunk(q, fw=tuple(final_waits)):
            for (s, v) in fw:
                q.wait_ge(semh[s], v)
        self.thunks["gpsimd"].append(thunk)

        for b in reversed(list(self._open)):
            self.free(b)
        with self.nc.Block() as block:
            for ename in ENGINES:
                ths = self.thunks[ename]
                if not ths:
                    continue

                def make(ths):
                    def section(q):
                        for t in ths:
                            t(q)
                    return section
                getattr(block, ename)(make(ths))
        self.ctx.close()
        return self.nc



def e_layernorm(kb, x_buf, x_ap, stat, sq, dmodel=D):
    st = stat.h
    kb.op("scalar", lambda q: q.activation(out=sq[0][:, :], in_=x_ap, func=AF.Square),
          reads=[x_buf], writes=[sq[0]])
    kb.op("vector", lambda q: q.reduce_sum(out=st[:, 0:1], in_=x_ap, axis=AX.X),
          reads=[x_buf], writes=[stat])
    kb.op("vector", lambda q: q.reduce_sum(out=st[:, 1:2], in_=sq[0][:, :], axis=AX.X),
          reads=[sq[0]], writes=[stat])
    kb.op("vector", lambda q: q.tensor_scalar(
        out=st[:, 2:4], in0=st[:, 0:2], scalar1=1.0 / dmodel, scalar2=None,
        op0=ALU.mult), reads=[stat], writes=[stat])
    kb.op("vector", lambda q: q.tensor_tensor(
        out=st[:, 4:5], in0=st[:, 2:3], in1=st[:, 2:3], op=ALU.mult),
        reads=[stat], writes=[stat])
    kb.op("vector", lambda q: q.tensor_tensor(
        out=st[:, 5:6], in0=st[:, 3:4], in1=st[:, 4:5], op=ALU.subtract),
        reads=[stat], writes=[stat])
    eps_ap, cb0 = kb.constcol(LN_EPS)
    mh_ap, _ = kb.constcol(-0.5)
    kb.op("scalar", lambda q: q.activation(
        out=st[:, 6:7], in_=st[:, 5:6], func=AF.Ln, bias=eps_ap),
        reads=[stat, cb0], writes=[stat])
    kb.op("scalar", lambda q: q.activation(
        out=st[:, 7:8], in_=st[:, 6:7], func=AF.Exp, scale=mh_ap),
        reads=[stat, cb0], writes=[stat])
    return st[:, 2:3], st[:, 7:8]


def matmul_group(kb, ps, ps_ap, pairs, reads):
    def fn(q):
        last = None
        n = len(pairs)
        for i, (lt, rt) in enumerate(pairs):
            last = q.matmul(ps_ap, lt, rt, start=(i == 0), stop=(i == n - 1))
        return last
    return kb.op("tensor", fn, reads=list(reads), writes=[ps])



TP = 2176
NT = TP // 128
CHUNKS5 = [(0, 512), (512, 512), (1024, 512), (1536, 512), (2048, 128)]
CHUNKS4 = [(0, 512), (512, 512), (1024, 512), (1536, 512)]


def build_l0a_pre(kb=None, ext=None):
    standalone = kb is None
    if standalone:
        kb = KB("l0a_pre")

    def _inp(name, shape, dtype=F32):
        return ext[name] if (ext and name in ext) else kb.inp(name, shape, dtype)

    def _out(name, shape, dtype=F32):
        return ext[name] if (ext and name in ext) else kb.out(name, shape, dtype)
    x0p = _inp("x0p", [TP, D])
    W_inT = _inp("W_inT", [D, 2 * DI])
    cw4 = _inp("cw4", [128, 64])
    cbv = _inp("cbv", [128, 16])
    WxCT = _inp("WxCT", [128, 16 * NS])
    BlogT = _inp("BlogT", [NS, DI])

    o_xln = _out("o_xln", [TC, D])
    o_xc = _out("o_xc", [DI, TC], BF16)
    o_cb = _out("o_cb", [DI, TC], BF16)
    o_sz = _out("o_sz", [DI, TC], BF16)

    xlnb_dram = kb.dram("xlnb_dram", [TP, D], BF16)

    xin = [kb.sbuf(f"xin{i}", [128, D]) for i in range(3)]
    xlnf = [kb.sbuf(f"xlnf{i}", [128, D]) for i in range(3)]
    sq = [kb.sbuf("sq", [128, D])]
    stat = kb.sbuf("stat", [128, 8])
    for j in range(NT):
        bi = xin[j % 3]
        bo = xlnf[j % 3]
        kb.dma("sync", bi[:, :], x0p.ap(j * 128 * D, [[D, 128], [1, D]]),
               reads=[x0p], writes=[bi], stream=f"d_xin{j % 3}")
        mu_ap, rstd_ap = e_layernorm(kb, bi, bi[:, :], stat, sq)
        kb.op("vector", lambda q, bi=bi, bo=bo, mu_ap=mu_ap, rstd_ap=rstd_ap:
              q.tensor_scalar(out=bo[:, :], in0=bi[:, :], scalar1=mu_ap,
                              scalar2=rstd_ap, op0=ALU.subtract, op1=ALU.mult),
              reads=[bi, stat], writes=[bo])
        if j >= 1:
            kb.dma("sync", o_xln.ap((j - 1) * 128 * D, [[D, 128], [1, D]]),
                   bo[:, :], reads=[bo], writes=[o_xln], stream=f"d_oxln{j % 3}")
        kb.dma("gpsimd", xlnb_dram.ap(j * 128 * D, [[D, 128], [1, D]]),
               bo[:, :], reads=[bo], writes=[xlnb_dram], stream=f"d_xlnb{j % 3}")

    xlnT = [kb.sbuf(f"xlnT{k}", [128, TP], BF16) for k in range(8)]
    for kd in range(8):
        kb.dma("sync", xlnT[kd][:, :],
               xlnb_dram.ap(kd * 128, [[D, TP], [1, 128]]),
               reads=[xlnb_dram], writes=[xlnT[kd]], transpose=True)

    wbuf = kb.sbuf("wbuf", [128, 8 * DI], BF16)
    kb.dma("gpsimd",
           bass.AP(wbuf.h, 0, [[8 * DI, 128], [DI, 8], [1, DI]]),
           W_inT.ap(0, [[2 * DI, 128], [128 * 2 * DI, 8], [1, DI]]),
           reads=[W_inT], writes=[wbuf])

    cwb = kb.sbuf("cwb", [128, 16 * 4])
    kb.dma("sync", cwb[:, :], cw4.ap(0, [[64, 128], [1, 64]]),
           reads=[cw4], writes=[cwb])
    cbb = kb.sbuf("cbb", [128, 16])
    kb.dma("sync", cbb[:, :], cbv.ap(0, [[16, 128], [1, 16]]),
           reads=[cbv], writes=[cbb])

    xc = [kb.sbuf(f"xct{m}", [128, TC], BF16) for m in range(16)]
    ps = [kb.psum(f"ps{i}", [128, 512]) for i in range(4)]
    xi_m = [kb.sbuf(f"xi_m{i}", [128, TP], BF16) for i in range(2)]
    convb = [kb.sbuf(f"convb{i}", [128, TC]) for i in range(2)]
    for m in range(16):
        xim = xi_m[m % 2]
        cvb = convb[m % 2]
        for ci, (t0, nt) in enumerate(CHUNKS5):
            p = ps[ci % 4]
            pairs = [(wbuf.ap(kd * DI + m * 128, [[8 * DI, 128], [1, 128]]),
                      xlnT[kd][:, t0:t0 + nt]) for kd in range(8)]
            matmul_group(kb, p, p.h[:, 0:nt], pairs, [wbuf] + xlnT)
            kb.op("vector", lambda q, p=p, xim=xim, t0=t0, nt=nt:
                  q.tensor_copy(out=xim[:, t0:t0 + nt], in_=p.h[:, 0:nt]),
                  reads=[p], writes=[xim])
        kb.op("vector", lambda q, xim=xim, cvb=cvb, m=m: q.tensor_scalar(
            out=cvb[:, :], in0=xim[:, 125:125 + TC],
            scalar1=cwb.h[:, m * 4:m * 4 + 1], scalar2=None, op0=ALU.mult),
            reads=[xim, cwb], writes=[cvb])
        for tap in range(1, 4):
            kb.op("vector", lambda q, xim=xim, cvb=cvb, m=m, tap=tap:
                  q.scalar_tensor_tensor(
                      out=cvb[:, :], in0=xim[:, 125 + tap:125 + tap + TC],
                      scalar=cwb.h[:, m * 4 + tap:m * 4 + tap + 1],
                      in1=cvb[:, :], op0=ALU.mult, op1=ALU.add),
                  reads=[xim, cwb, cvb], writes=[cvb])
        kb.op("scalar", lambda q, cvb=cvb, m=m: q.activation(
            out=xc[m][:, :], in_=cvb[:, :], func=AF.Silu,
            bias=cbb.h[:, m:m + 1]), reads=[cvb, cbb], writes=[xc[m]])
        kb.dma("sync", o_xc.ap(m * 128 * TC, [[TC, 128], [1, TC]]),
               xc[m][:, :], reads=[xc[m]], writes=[o_xc], stream="d_oxc")

    kb.dma("gpsimd",
           bass.AP(wbuf.h, 0, [[8 * DI, 128], [DI, 8], [1, DI]]),
           W_inT.ap(DI, [[2 * DI, 128], [128 * 2 * DI, 8], [1, DI]]),
           reads=[W_inT], writes=[wbuf])
    szt = [kb.sbuf(f"szt{i}", [128, 512], BF16) for i in range(2)]
    for m in range(16):
        for ci, (t0, nt) in enumerate(CHUNKS4):
            p = ps[ci % 4]
            s = szt[ci % 2]
            pairs = [(wbuf.ap(kd * DI + m * 128, [[8 * DI, 128], [1, 128]]),
                      xlnT[kd][:, 128 + t0:128 + t0 + nt]) for kd in range(8)]
            matmul_group(kb, p, p.h[:, 0:nt], pairs, [wbuf] + xlnT)
            kb.op("scalar", lambda q, p=p, s=s, nt=nt: q.activation(
                out=s[:, 0:nt], in_=p.h[:, 0:nt], func=AF.Silu),
                reads=[p], writes=[s])
            kb.dma("sync", o_sz.ap(m * 128 * TC + t0, [[TC, 128], [1, nt]]),
                   s[:, 0:nt], reads=[s], writes=[o_sz], stream=f"d_osz{ci % 2}")

    wxb = kb.sbuf("wxb", [128, 16 * NS], BF16)
    kb.dma("gpsimd", wxb[:, :], WxCT.ap(0, [[16 * NS, 128], [1, 16 * NS]]),
           reads=[WxCT], writes=[wxb])
    sC = kb.sbuf("sC", [16, TC], BF16)
    ps16 = kb.psum("ps16", [16, 512])
    for ci, (t0, nt) in enumerate(CHUNKS4):
        pairs = [(wxb.ap(kd * NS, [[16 * NS, 128], [1, NS]]),
                  xc[kd][:, t0:t0 + nt]) for kd in range(16)]
        matmul_group(kb, ps16, ps16.h[:, 0:nt], pairs, [wxb] + xc)
        kb.op("vector", lambda q, t0=t0, nt=nt: q.tensor_copy(
            out=sC[:, t0:t0 + nt], in_=ps16.h[:, 0:nt]),
            reads=[ps16], writes=[sC])

    blg = kb.sbuf("blg", [16, DI])
    kb.dma("sync", blg[:, :], BlogT.ap(0, [[DI, 16], [1, DI]]),
           reads=[BlogT], writes=[blg])
    expBT = kb.sbuf("expBT", [16, DI], BF16)
    kb.op("scalar", lambda q: q.activation(out=expBT[:, :], in_=blg[:, :], func=AF.Exp),
          reads=[blg], writes=[expBT])

    cbt = [kb.sbuf(f"cbt{i}", [128, 512], BF16) for i in range(2)]
    for m in range(16):
        for ci, (t0, nt) in enumerate(CHUNKS4):
            p = ps[ci % 4]
            cbo = cbt[ci % 2]
            matmul_group(kb, p, p.h[:, 0:nt],
                         [(expBT[:, m * 128:(m + 1) * 128], sC[:, t0:t0 + nt])],
                         [expBT, sC])
            kb.op("vector", lambda q, p=p, cbo=cbo, nt=nt: q.tensor_copy(
                out=cbo[:, 0:nt], in_=p.h[:, 0:nt]), reads=[p], writes=[cbo])
            kb.dma("sync", o_cb.ap(m * 128 * TC + t0, [[TC, 128], [1, nt]]),
                   cbo[:, 0:nt], reads=[cbo], writes=[o_cb], stream=f"d_ocb{ci % 2}")

    if standalone:
        return kb.finalize()


def stage_l0a_pre(inputs):
    xf = np.ascontiguousarray(inputs["L0_attn_x"].reshape(16384, D))
    W_inT = np.ascontiguousarray(inputs["L0_attn_W_in"].T)
    cw4 = np.ascontiguousarray(
        inputs["L0_attn_conv_w"].reshape(16, 128, 4).transpose(1, 0, 2).reshape(128, 64))
    cbv = np.ascontiguousarray(
        inputs["L0_attn_conv_b"].reshape(16, 128).T)
    WxCT = np.ascontiguousarray(
        inputs["L0_attn_W_x"][NS:].T.reshape(16, 128, NS).transpose(1, 0, 2).reshape(128, 16 * NS))
    BlogT = np.ascontiguousarray(inputs["L0_attn_B_log"].T)
    maps = []
    for c in range(NCORE):
        b, h = divmod(c, 2)
        x0p = np.zeros((TP, D), np.float32)
        if h == 1:
            x0p[125:128] = xf[c * TC - 3: c * TC]
        x0p[128:] = xf[c * TC:(c + 1) * TC]
        maps.append(dict(x0p=x0p, W_inT=W_inT, cw4=cw4, cbv=cbv,
                         WxCT=WxCT, BlogT=BlogT))
    return maps





TFULL = 4096


def build_l0a_scan(kb=None, ext=None):
    standalone = kb is None
    if standalone:
        kb = KB("l0a_scan")

    def _inp(name, shape, dtype=F32):
        return ext[name] if (ext and name in ext) else kb.inp(name, shape, dtype)

    def _out(name, shape, dtype=F32):
        return ext[name] if (ext and name in ext) else kb.out(name, shape, dtype)
    xc2 = _inp("xc2", [D, TFULL], BF16)
    CB2 = _inp("CB2", [D, TFULL], BF16)
    sz2 = _inp("sz2", [D, TFULL], BF16)
    o_yg = _out("o_yg", [D, TFULL], BF16)

    dec = kb.sbuf("dec", [128, TFULL])
    kb.op("vector", lambda q: q.memset(dec.h[:, :], 0.95), reads=[], writes=[dec])
    nslot = 2
    xs = [kb.sbuf(f"xs{i}", [128, TFULL], BF16) for i in range(nslot)]
    cs = [kb.sbuf(f"cs{i}", [128, TFULL], BF16) for i in range(nslot)]
    zs = [kb.sbuf(f"zs{i}", [128, TFULL], BF16) for i in range(nslot)]
    hs = [kb.sbuf(f"hs{i}", [128, TFULL], BF16) for i in range(nslot)]
    for j in range(8):
        x, c, z, hb = xs[j % nslot], cs[j % nslot], zs[j % nslot], hs[j % nslot]
        kb.dma("sync", x[:, :], xc2.ap(j * 128 * TFULL, [[TFULL, 128], [1, TFULL]]),
               reads=[xc2], writes=[x])
        kb.dma("sync", c[:, :], CB2.ap(j * 128 * TFULL, [[TFULL, 128], [1, TFULL]]),
               reads=[CB2], writes=[c])
        kb.dma("sync", z[:, :], sz2.ap(j * 128 * TFULL, [[TFULL, 128], [1, TFULL]]),
               reads=[sz2], writes=[z])
        kb.op("vector", lambda q, x=x, hb=hb: q.tensor_tensor_scan(
            out=hb[:, :], data0=dec.h[:, :], data1=x[:, :], initial=0.0,
            op0=ALU.mult, op1=ALU.add), reads=[dec, x], writes=[hb])
        kb.op("vector", lambda q, hb=hb, c=c: q.tensor_tensor(
            out=hb[:, :], in0=hb[:, :], in1=c[:, :], op=ALU.mult),
            reads=[hb, c], writes=[hb])
        kb.op("vector", lambda q, hb=hb, z=z: q.tensor_tensor(
            out=hb[:, :], in0=hb[:, :], in1=z[:, :], op=ALU.mult),
            reads=[hb, z], writes=[hb])
        kb.dma("sync", o_yg.ap(j * 128 * TFULL, [[TFULL, 128], [1, TFULL]]),
               hb[:, :], reads=[hb], writes=[o_yg], stream=f"d_oyg{j % nslot}")
    if standalone:
        return kb.finalize()


def stage_l0a_scan(prev):
    maps = []
    for c in range(NCORE):
        b, h = divmod(c, 2)
        hs = slice(h * D, (h + 1) * D)
        maps.append(dict(
            xc2=np.concatenate([prev["o_xc"][2 * b][hs], prev["o_xc"][2 * b + 1][hs]], 1),
            CB2=np.concatenate([prev["o_cb"][2 * b][hs], prev["o_cb"][2 * b + 1][hs]], 1),
            sz2=np.concatenate([prev["o_sz"][2 * b][hs], prev["o_sz"][2 * b + 1][hs]], 1),
        ))
    return maps



def build_l0a_post(kb=None, ext=None):
    standalone = kb is None
    if standalone:
        kb = KB("l0a_post")

    def _inp(name, shape, dtype=F32):
        return ext[name] if (ext and name in ext) else kb.inp(name, shape, dtype)

    def _out(name, shape, dtype=F32):
        return ext[name] if (ext and name in ext) else kb.out(name, shape, dtype)
    yg2 = _inp("yg2", [DI, TC], BF16)
    W_outT = _inp("W_outT", [DI, D])
    x_ln = _inp("x_ln", [TC, D])
    x0s = _inp("x0s", [TC, D])
    lngb = _inp("lngb", [128, D])
    lnbb = _inp("lnbb", [128, D])
    wrt = _inp("wrt", [128, 64])
    iota8b = _inp("iota8b", [128, 8])
    L128 = _inp("L128", [128, 128])
    L16s = _inp("L16s", [16, 16])
    SEL16 = _inp("SEL16", [16, 16 * 128])
    COLSEL = _inp("COLSEL", [128, 16 * 16])

    o_x1 = _out("o_x1", [TC, D])
    o_yln2 = _out("o_yln2", [TC, D], BF16)
    o_misc = _out("o_misc", [128, 96])
    o_cnt = _out("o_cnt", [1, 8])

    ylnlo = kb.dram("ylnlo", [TC, D], BF16)

    yg = [kb.sbuf(f"ygs{k}", [128, TC], BF16, side="right") for k in range(16)]
    for k in range(16):
        kb.dma("sync", yg[k][:, :], yg2.ap(k * 128 * TC, [[TC, 128], [1, TC]]),
               reads=[yg2], writes=[yg[k]])
    wo = kb.sbuf("wo", [128, 16 * D], BF16, side="right")
    kb.dma("gpsimd", bass.AP(wo.h, 0, [[16 * D, 128], [D, 16], [1, D]]),
           W_outT.ap(0, [[D, 128], [128 * D, 16], [1, D]]),
           reads=[W_outT], writes=[wo])

    ps = [kb.psum(f"ps{i}", [128, 512]) for i in range(4)]
    sq = [kb.sbuf("sq", [128, D])]
    stat = kb.sbuf("stat", [128, 8])
    stat2 = kb.sbuf("stat2", [128, 8])
    xlt = [kb.sbuf(f"xlt{i}", [128, D]) for i in range(2)]
    x0t = [kb.sbuf(f"x0t{i}", [128, D]) for i in range(2)]
    rsd = [kb.sbuf(f"rsd{i}", [128, D]) for i in range(2)]
    x1t = [kb.sbuf(f"x1t{i}", [128, D]) for i in range(2)]
    y2t = [kb.sbuf(f"y2t{i}", [128, D]) for i in range(2)]
    hit = kb.sbuf("hit", [128, D], BF16)
    lot = kb.sbuf("lot", [128, D], BF16)
    gb = kb.sbuf("gb", [128, D])
    bb = kb.sbuf("bb", [128, D])
    kb.dma("sync", gb[:, :], lngb.ap(0, [[D, 128], [1, D]]), reads=[lngb], writes=[gb])
    kb.dma("sync", bb[:, :], lnbb.ap(0, [[D, 128], [1, D]]), reads=[lnbb], writes=[bb])

    for j in range(16):
        xl, x0b, rs, x1b, y2 = (xlt[j % 2], x0t[j % 2], rsd[j % 2],
                                x1t[j % 2], y2t[j % 2])
        kb.dma("sync", xl[:, :], x_ln.ap(j * 128 * D, [[D, 128], [1, D]]),
               reads=[x_ln], writes=[xl], stream=f"d_xl{j % 2}")
        kb.dma("sync", x0b[:, :], x0s.ap(j * 128 * D, [[D, 128], [1, D]]),
               reads=[x0s], writes=[x0b], stream=f"d_x0{j % 2}")
        for ci in range(2):
            p = ps[ci % 2]
            pairs = [(yg[k][:, j * 128:(j + 1) * 128],
                      wo.ap(k * D + ci * 512, [[16 * D, 128], [1, 512]]))
                     for k in range(16)]
            matmul_group(kb, p, p.h[:, 0:512], pairs, yg + [wo])
            kb.op("vector", lambda q, p=p, rs=rs, xl=xl, ci=ci: q.tensor_tensor(
                out=rs[:, ci * 512:(ci + 1) * 512], in0=xl[:, ci * 512:(ci + 1) * 512],
                in1=p.h[:, 0:512], op=ALU.add), reads=[p, xl], writes=[rs])
        mu_ap, rstd_ap = e_layernorm(kb, rs, rs[:, :], stat, sq)
        kb.op("vector", lambda q, rs=rs, y2=y2, mu_ap=mu_ap, rstd_ap=rstd_ap:
              q.tensor_scalar(out=y2[:, :], in0=rs[:, :], scalar1=mu_ap,
                              scalar2=rstd_ap, op0=ALU.subtract, op1=ALU.mult),
              reads=[rs, stat], writes=[y2])
        kb.op("vector", lambda q, y2=y2: q.tensor_tensor(
            out=y2[:, :], in0=y2[:, :], in1=gb[:, :], op=ALU.mult),
            reads=[y2, gb], writes=[y2])
        kb.op("vector", lambda q, y2=y2: q.tensor_tensor(
            out=y2[:, :], in0=y2[:, :], in1=bb[:, :], op=ALU.add),
            reads=[y2, bb], writes=[y2])
        kb.op("vector", lambda q, y2=y2, x1b=x1b, x0b=x0b: q.tensor_tensor(
            out=x1b[:, :], in0=y2[:, :], in1=x0b[:, :], op=ALU.add),
            reads=[y2, x0b], writes=[x1b])
        kb.dma("sync", o_x1.ap(j * 128 * D, [[D, 128], [1, D]]), x1b[:, :],
               reads=[x1b], writes=[o_x1], stream=f"d_ox1{j % 2}")
        mu2, rstd2 = e_layernorm(kb, x1b, x1b[:, :], stat2, sq)
        kb.op("vector", lambda q, x1b=x1b, rs=rs, mu2=mu2, rstd2=rstd2:
              q.tensor_scalar(out=rs[:, :], in0=x1b[:, :], scalar1=mu2,
                              scalar2=rstd2, op0=ALU.subtract, op1=ALU.mult),
              reads=[x1b, stat2], writes=[rs])
        kb.op("vector", lambda q, rs=rs: q.tensor_copy(out=hit[:, :], in_=rs[:, :]),
              reads=[rs], writes=[hit])
        kb.op("vector", lambda q, rs=rs: q.tensor_tensor(
            out=lot[:, :], in0=rs[:, :], in1=hit[:, :], op=ALU.subtract),
            reads=[rs, hit], writes=[lot])
        kb.dma("sync", o_yln2.ap(j * 128 * D, [[D, 128], [1, D]]), hit[:, :],
               reads=[hit], writes=[o_yln2], stream="d_hi")
        kb.dma("sync", ylnlo.ap(j * 128 * D, [[D, 128], [1, D]]), lot[:, :],
               reads=[lot], writes=[ylnlo], stream="d_lo")

    kb.free(*yg, wo)
    hiT = [kb.sbuf(f"hiT{k}", [128, TC], BF16, side="right") for k in range(8)]
    loT = [kb.sbuf(f"loT{k}", [128, TC], BF16, side="right") for k in range(8)]
    for kd in range(8):
        kb.dma("sync", hiT[kd][:, :], o_yln2.ap(kd * 128, [[D, TC], [1, 128]]),
               reads=[o_yln2], writes=[hiT[kd]], transpose=True)
        kb.dma("sync", loT[kd][:, :], ylnlo.ap(kd * 128, [[D, TC], [1, 128]]),
               reads=[ylnlo], writes=[loT[kd]], transpose=True)
    wrf = kb.sbuf("wrf", [128, 64])
    kb.dma("sync", wrf[:, :], wrt.ap(0, [[64, 128], [1, 64]]), reads=[wrt], writes=[wrf])
    whi = kb.sbuf("whi", [128, 64], BF16)
    wlo = kb.sbuf("wlo", [128, 64], BF16)
    kb.op("vector", lambda q: q.tensor_copy(out=whi[:, :], in_=wrf[:, :]),
          reads=[wrf], writes=[whi])
    kb.op("vector", lambda q: q.tensor_tensor(
        out=wlo[:, :], in0=wrf[:, :], in1=whi[:, :], op=ALU.subtract),
        reads=[wrf, whi], writes=[wlo])
    io8 = kb.sbuf("io8", [128, 8])
    kb.dma("sync", io8[:, :], iota8b.ap(0, [[8, 128], [1, 8]]), reads=[iota8b], writes=[io8])
    l128 = kb.sbuf("l128", [128, 128])
    kb.dma("sync", l128[:, :], L128.ap(0, [[128, 128], [1, 128]]), reads=[L128], writes=[l128])
    l16 = kb.sbuf("l16", [16, 16])
    kb.dma("sync", l16[:, :], L16s.ap(0, [[16, 16], [1, 16]]), reads=[L16s], writes=[l16])
    sel16 = kb.sbuf("sel16", [16, 16 * 128])
    kb.dma("sync", sel16[:, :], SEL16.ap(0, [[16 * 128, 16], [1, 16 * 128]]),
           reads=[SEL16], writes=[sel16])
    colsel = kb.sbuf("colsel", [128, 256])
    kb.dma("sync", colsel[:, :], COLSEL.ap(0, [[256, 128], [1, 256]]),
           reads=[COLSEL], writes=[colsel])

    OH1a = kb.sbuf("OH1a", [128, 16 * 8])
    OH2a = kb.sbuf("OH2a", [128, 16 * 8])
    OHa = kb.sbuf("OHa", [128, 16 * 8])
    misc = kb.sbuf("misc", [128, 96])
    rst = kb.sbuf("rst", [128, 16])
    Sx = kb.sbuf("Sx", [128, 8])
    S2 = kb.sbuf("S2", [128, 8])
    t8 = kb.sbuf("t8", [128, 8])
    psr = ps[2]
    for j in range(16):
        pairs = []
        for kd in range(8):
            pairs.append((hiT[kd][:, j * 128:(j + 1) * 128],
                          whi.ap(kd * 8, [[64, 128], [1, 8]])))
        for kd in range(8):
            pairs.append((hiT[kd][:, j * 128:(j + 1) * 128],
                          wlo.ap(kd * 8, [[64, 128], [1, 8]])))
        for kd in range(8):
            pairs.append((loT[kd][:, j * 128:(j + 1) * 128],
                          whi.ap(kd * 8, [[64, 128], [1, 8]])))
        matmul_group(kb, psr, psr.h[:, 0:8], pairs, hiT + loT + [whi, wlo])
        kb.op("vector", lambda q: q.reduce_max(
            out=rst.h[:, 0:1], in_=psr.h[:, 0:8], axis=AX.X, negate=True),
            reads=[psr], writes=[rst])
        kb.op("scalar", lambda q: q.activation(
            out=Sx[:, :], in_=psr.h[:, 0:8], func=AF.Exp, bias=rst.h[:, 0:1]),
            reads=[psr, rst], writes=[Sx])
        kb.op("vector", lambda q: q.reduce_sum(
            out=rst.h[:, 1:2], in_=Sx[:, :], axis=AX.X), reads=[Sx], writes=[rst])
        kb.op("vector", lambda q: q.reduce_max(
            out=rst.h[:, 2:3], in_=Sx[:, :], axis=AX.X), reads=[Sx], writes=[rst])
        kb.op("vector", lambda q, j=j: q.tensor_scalar(
            out=OH1a[:, j * 8:(j + 1) * 8], in0=Sx[:, :], scalar1=rst.h[:, 2:3],
            scalar2=None, op0=ALU.is_ge), reads=[Sx, rst], writes=[OH1a])
        kb.op("vector", lambda q, j=j: q.tensor_scalar(
            out=t8[:, :], in0=OH1a[:, j * 8:(j + 1) * 8], scalar1=-1.0, scalar2=1.0,
            op0=ALU.mult, op1=ALU.add), reads=[OH1a], writes=[t8])
        kb.op("vector", lambda q: q.tensor_tensor(
            out=S2[:, :], in0=Sx[:, :], in1=t8[:, :], op=ALU.mult),
            reads=[Sx, t8], writes=[S2])
        kb.op("vector", lambda q: q.reduce_max(
            out=rst.h[:, 3:4], in_=S2[:, :], axis=AX.X), reads=[S2], writes=[rst])
        kb.op("vector", lambda q, j=j: q.tensor_scalar(
            out=OH2a[:, j * 8:(j + 1) * 8], in0=S2[:, :], scalar1=rst.h[:, 3:4],
            scalar2=None, op0=ALU.is_ge), reads=[S2, rst], writes=[OH2a])
        kb.op("vector", lambda q, j=j: q.tensor_tensor(
            out=OHa[:, j * 8:(j + 1) * 8], in0=OH1a[:, j * 8:(j + 1) * 8],
            in1=OH2a[:, j * 8:(j + 1) * 8], op=ALU.add),
            reads=[OH1a, OH2a], writes=[OHa])
        kb.op("vector", lambda q, j=j: q.tensor_tensor(
            out=t8[:, :], in0=OH1a[:, j * 8:(j + 1) * 8], in1=io8[:, :], op=ALU.mult),
            reads=[OH1a, io8], writes=[t8])
        kb.op("vector", lambda q, j=j: q.reduce_sum(
            out=misc.h[:, 0 * 16 + j:0 * 16 + j + 1], in_=t8[:, :], axis=AX.X),
            reads=[t8], writes=[misc])
        kb.op("vector", lambda q, j=j: q.tensor_tensor(
            out=t8[:, :], in0=OH2a[:, j * 8:(j + 1) * 8], in1=io8[:, :], op=ALU.mult),
            reads=[OH2a, io8], writes=[t8])
        kb.op("vector", lambda q, j=j: q.reduce_sum(
            out=misc.h[:, 1 * 16 + j:1 * 16 + j + 1], in_=t8[:, :], axis=AX.X),
            reads=[t8], writes=[misc])
        kb.op("vector", lambda q: q.reciprocal(
            out=rst.h[:, 4:5], in_=rst.h[:, 1:2]), reads=[rst], writes=[rst])
        kb.op("vector", lambda q, j=j: q.tensor_tensor(
            out=misc.h[:, 2 * 16 + j:2 * 16 + j + 1], in0=rst.h[:, 2:3],
            in1=rst.h[:, 4:5], op=ALU.mult), reads=[rst], writes=[misc])
        kb.op("vector", lambda q, j=j: q.tensor_tensor(
            out=misc.h[:, 3 * 16 + j:3 * 16 + j + 1], in0=rst.h[:, 3:4],
            in1=rst.h[:, 4:5], op=ALU.mult), reads=[rst], writes=[misc])

    psT = ps[3]
    pairsT = [(colsel[:, j * 16:(j + 1) * 16], OHa[:, j * 8:(j + 1) * 8])
              for j in range(16)]
    matmul_group(kb, psT, psT.h[0:16, 0:8], pairsT, [colsel, OHa])
    tot16 = kb.sbuf("tot16", [16, 8])
    kb.op("vector", lambda q: q.tensor_copy(out=tot16[:, :], in_=psT.h[0:16, 0:8]),
          reads=[psT], writes=[tot16])
    one16 = kb.sbuf("one16", [16, 1])
    kb.op("vector", lambda q: q.memset(one16.h[:, :], 1.0), reads=[], writes=[one16])
    psC = ps[3]
    matmul_group(kb, psC, psC.h[0:1, 0:8], [(one16[:, :], tot16[:, :])], [one16, tot16])
    cnt1 = kb.sbuf("cnt1", [1, 8])
    kb.op("vector", lambda q: q.tensor_copy(out=cnt1[:, :], in_=psC.h[0:1, 0:8]),
          reads=[psC], writes=[cnt1])
    kb.dma("sync", o_cnt.ap(0, [[8, 1], [1, 8]]), cnt1[:, :],
           reads=[cnt1], writes=[o_cnt])
    psB = ps[2]
    matmul_group(kb, psB, psB.h[0:16, 0:8], [(l16[:, :], tot16[:, :])], [l16, tot16])
    tb16 = kb.sbuf("tb16", [16, 8])
    kb.op("vector", lambda q: q.tensor_copy(out=tb16[:, :], in_=psB.h[0:16, 0:8]),
          reads=[psB], writes=[tb16])

    psc2 = [ps[0], ps[1]]
    cum = [kb.sbuf(f"cum{i}", [128, 8]) for i in range(2)]
    for j in range(16):
        p = psc2[j % 2]
        cu = cum[j % 2]

        def fn(q, j=j, p=p):
            q.matmul(p.h[:, 0:8], l128[:, :], OHa[:, j * 8:(j + 1) * 8],
                     start=True, stop=False)
            return q.matmul(p.h[:, 0:8], sel16[:, j * 128:(j + 1) * 128],
                            tb16[:, :], start=False, stop=True)
        kb.op("tensor", fn, reads=[l128, OHa, sel16, tb16], writes=[p])
        kb.op("vector", lambda q, p=p, cu=cu: q.tensor_copy(
            out=cu[:, :], in_=p.h[:, 0:8]), reads=[p], writes=[cu])
        for oh, col in ((OH1a, 4), (OH2a, 5)):
            kb.op("vector", lambda q, j=j, oh=oh, cu=cu: q.tensor_tensor(
                out=t8[:, :], in0=oh[:, j * 8:(j + 1) * 8], in1=cu[:, :],
                op=ALU.mult), reads=[oh, cu], writes=[t8])
            kb.op("vector", lambda q, j=j, col=col: q.reduce_sum(
                out=rst.h[:, 5:6], in_=t8[:, :], axis=AX.X),
                reads=[t8], writes=[rst])
            kb.op("vector", lambda q, j=j, col=col: q.tensor_scalar(
                out=misc.h[:, col * 16 + j:col * 16 + j + 1], in0=rst.h[:, 5:6],
                scalar1=-1.0, scalar2=None, op0=ALU.add),
                reads=[rst], writes=[misc])
    kb.dma("sync", o_misc.ap(0, [[96, 128], [1, 96]]), misc.h[:, :],
           reads=[misc], writes=[o_misc])
    if standalone:
        return kb.finalize()


def stage_l0a_post(inputs, prev):
    xf = np.ascontiguousarray(inputs["L0_attn_x"].reshape(16384, D))
    W_outT = np.ascontiguousarray(inputs["L0_attn_W_out"].T)
    lngb = np.ascontiguousarray(np.broadcast_to(inputs["L0_attn_ln_g"], (128, D)))
    lnbb = np.ascontiguousarray(np.broadcast_to(inputs["L0_attn_ln_b"], (128, D)))
    wrt = np.ascontiguousarray(
        inputs["L0_ffn_w_router"].reshape(8, 128, 8).transpose(1, 0, 2).reshape(128, 64))
    iota8b = np.ascontiguousarray(
        np.broadcast_to(np.arange(8, dtype=np.float32), (128, 8)))
    L128 = np.tril(np.ones((128, 128), np.float32)).T.copy()
    L16s = np.triu(np.ones((16, 16), np.float32), 1)
    SEL16 = np.zeros((16, 16 * 128), np.float32)
    for j in range(16):
        SEL16[j, j * 128:(j + 1) * 128] = 1.0
    COLSEL = np.zeros((128, 16 * 16), np.float32)
    for j in range(16):
        COLSEL[:, j * 16 + j] = 1.0
    maps = []
    for c in range(NCORE):
        b, h = divmod(c, 2)
        yg2 = np.concatenate([prev["o_yg"][2 * b][:, h * TC:(h + 1) * TC],
                              prev["o_yg"][2 * b + 1][:, h * TC:(h + 1) * TC]], 0)
        maps.append(dict(
            yg2=np.ascontiguousarray(yg2), W_outT=W_outT,
            x_ln=prev["o_xln"][c], x0s=np.ascontiguousarray(xf[c * TC:(c + 1) * TC]),
            lngb=lngb, lnbb=lnbb, wrt=wrt, iota8b=iota8b, L128=L128, L16s=L16s,
            SEL16=SEL16, COLSEL=COLSEL))
    return maps





BIGSLOT = 1000000.0


def build_moe_route2(kb=None, ext=None):
    standalone = kb is None
    if standalone:
        kb = KB("moe_route2")

    def _inp(name, shape, dtype=F32):
        return ext[name] if (ext and name in ext) else kb.inp(name, shape, dtype)

    def _out(name, shape, dtype=F32):
        return ext[name] if (ext and name in ext) else kb.out(name, shape, dtype)
    cnt_all = _inp("cnt_all", [8, 8])
    maskprev = _inp("maskprev", [8, 1])
    misc_in = _inp("misc_in", [128, 96])
    tidc = _inp("tidc", [128, 16])
    iota8b = _inp("iota8b", [128, 8])

    o_stg = _out("o_stg", [EXP * CAP, 1])
    o_weff = _out("o_weff", [128, 32])
    o_gidx = _out("o_gidx", [128, 32], I32)

    cbd = kb.dram("cbd", [1, 8])

    ca = kb.sbuf("ca", [8, 8])
    kb.dma("sync", ca[:, :], cnt_all.ap(0, [[8, 8], [1, 8]]), reads=[cnt_all], writes=[ca])
    mp = kb.sbuf("mp", [8, 1])
    kb.dma("sync", mp[:, :], maskprev.ap(0, [[1, 8], [1, 1]]), reads=[maskprev], writes=[mp])
    mi = kb.sbuf("mi", [128, 96])
    kb.dma("sync", mi[:, :], misc_in.ap(0, [[96, 128], [1, 96]]), reads=[misc_in], writes=[mi])
    tid = kb.sbuf("tid", [128, 16])
    kb.dma("sync", tid[:, :], tidc.ap(0, [[16, 128], [1, 16]]), reads=[tidc], writes=[tid])
    io8 = kb.sbuf("io8", [128, 8])
    kb.dma("sync", io8[:, :], iota8b.ap(0, [[8, 128], [1, 8]]), reads=[io8, iota8b], writes=[io8])

    psc = kb.psum("psc", [1, 8])
    matmul_group(kb, psc, psc.h[0:1, 0:8], [(mp[:, :], ca[:, :])], [mp, ca])
    cb1 = kb.sbuf("cb1", [1, 8])
    kb.op("vector", lambda q: q.tensor_copy(out=cb1[:, :], in_=psc.h[0:1, 0:8]),
          reads=[psc], writes=[cb1])
    kb.dma("sync", cbd.ap(0, [[8, 1], [1, 8]]), cb1[:, :], reads=[cb1], writes=[cbd])
    cbbc = kb.sbuf("cbbc", [128, 8])
    kb.dma("sync", cbbc[:, :], cbd.ap(0, [[0, 128], [1, 8]]), reads=[cbd], writes=[cbbc])

    zt = kb.sbuf("zt", [128, 256])
    kb.op("vector", lambda q: q.memset(zt.h[:, :], 0.0), reads=[], writes=[zt])
    kb.dma("gpsimd", o_stg.ap(0, [[256, 128], [1, 256]]), zt[:, :],
           reads=[zt], writes=[o_stg])

    oh = kb.sbuf("oh", [128, 8])
    t8 = kb.sbuf("t8", [128, 8])
    sc = kb.sbuf("sc", [128, 8])
    weff = kb.sbuf("weff", [128, 32])
    gidx = kb.sbuf("gidx", [128, 32], I32)
    gfl = kb.sbuf("gfl", [128, 1])
    for k in range(2):
        for j in range(16):
            ecol = mi.h[:, k * 16 + j:k * 16 + j + 1]
            wcol = mi.h[:, (2 + k) * 16 + j:(2 + k) * 16 + j + 1]
            lrcol = mi.h[:, (4 + k) * 16 + j:(4 + k) * 16 + j + 1]
            kb.op("vector", lambda q, ecol=ecol: q.tensor_scalar(
                out=oh[:, :], in0=io8[:, :], scalar1=ecol, scalar2=None,
                op0=ALU.is_equal), reads=[mi, io8], writes=[oh])
            kb.op("vector", lambda q: q.tensor_tensor(
                out=t8[:, :], in0=oh[:, :], in1=cbbc[:, :], op=ALU.mult),
                reads=[oh, cbbc], writes=[t8])
            kb.op("vector", lambda q: q.reduce_sum(
                out=sc.h[:, 0:1], in_=t8[:, :], axis=AX.X), reads=[t8], writes=[sc])
            kb.op("vector", lambda q, lrcol=lrcol: q.tensor_tensor(
                out=sc.h[:, 1:2], in0=sc.h[:, 0:1], in1=lrcol, op=ALU.add),
                reads=[sc, mi], writes=[sc])
            kb.op("vector", lambda q: q.tensor_scalar(
                out=sc.h[:, 2:3], in0=sc.h[:, 1:2], scalar1=float(CAP), scalar2=None,
                op0=ALU.is_lt), reads=[sc], writes=[sc])
            kb.op("vector", lambda q, wcol=wcol, k=k, j=j: q.tensor_tensor(
                out=weff.h[:, k * 16 + j:k * 16 + j + 1], in0=sc.h[:, 2:3],
                in1=wcol, op=ALU.mult), reads=[sc, mi], writes=[weff])
            kb.op("vector", lambda q, ecol=ecol: q.tensor_scalar(
                out=sc.h[:, 3:4], in0=ecol, scalar1=float(CAP), scalar2=None,
                op0=ALU.mult), reads=[mi], writes=[sc])
            kb.op("vector", lambda q: q.tensor_tensor(
                out=sc.h[:, 4:5], in0=sc.h[:, 3:4], in1=sc.h[:, 1:2], op=ALU.add),
                reads=[sc], writes=[sc])
            kb.op("vector", lambda q: q.tensor_scalar(
                out=sc.h[:, 5:6], in0=sc.h[:, 2:3], scalar1=-BIGSLOT, scalar2=BIGSLOT,
                op0=ALU.mult, op1=ALU.add), reads=[sc], writes=[sc])
            kb.op("vector", lambda q, k=k, j=j: q.tensor_tensor(
                out=sc.h[:, 6:7], in0=sc.h[:, 4:5], in1=sc.h[:, 5:6], op=ALU.add),
                reads=[sc], writes=[sc])
            kb.op("vector", lambda q, k=k, j=j: q.tensor_copy(
                out=gidx.h[:, k * 16 + j:k * 16 + j + 1], in_=sc.h[:, 6:7]),
                reads=[sc], writes=[gidx])
            kb.op("vector", lambda q, j=j: q.tensor_copy(
                out=gfl[:, :], in_=tid.h[:, j:j + 1]), reads=[tid], writes=[gfl])
            kb.idma(o_stg.ap(0, [[1, EXP * CAP], [1, 1]]), gfl[:, :],
                    out_off=bass.IndirectOffsetOnAxis(
                        gidx.h[:, k * 16 + j:k * 16 + j + 1], 0),
                    bounds=EXP * CAP - 1,
                    reads=[gfl, gidx], writes=[o_stg], stream="d_scat")
    kb.dma("sync", o_weff.ap(0, [[32, 128], [1, 32]]), weff[:, :],
           reads=[weff], writes=[o_weff])
    kb.dma("sync", o_gidx.ap(0, [[32, 128], [1, 32]]), gidx[:, :],
           reads=[gidx], writes=[o_gidx])
    if standalone:
        return kb.finalize()


def stage_moe_route2(prev):
    cnt_all = np.concatenate([prev["o_cnt"][c] for c in range(8)], 0)
    maps = []
    for c in range(NCORE):
        mp = np.zeros((8, 1), np.float32)
        mp[:c] = 1.0
        tid = np.arange(c * TC, (c + 1) * TC, dtype=np.float32)
        tidc = np.ascontiguousarray(tid.reshape(16, 128).T)
        iota8b = np.ascontiguousarray(
            np.broadcast_to(np.arange(8, dtype=np.float32), (128, 8)))
        maps.append(dict(cnt_all=np.ascontiguousarray(cnt_all), maskprev=mp,
                         misc_in=prev["o_misc"][c], tidc=tidc, iota8b=iota8b))
    return maps



def build_moe_expert(kb=None, ext=None):
    standalone = kb is None
    if standalone:
        kb = KB("moe_expert")

    def _inp(name, shape, dtype=F32):
        return ext[name] if (ext and name in ext) else kb.inp(name, shape, dtype)

    def _out(name, shape, dtype=F32):
        return ext[name] if (ext and name in ext) else kb.out(name, shape, dtype)
    stg_all = _inp("stg_all", [8, EXP * CAP])
    yfull = _inp("yfull", [16384, D], BF16)
    w1e = _inp("w1e", [D, 4096])
    w2e = _inp("w2e", [4096, D])
    eid = _inp("eid", [1, 1])

    o_ye = _out("o_ye", [CAP, D], BF16)

    stgi = kb.dram("stgi", [EXP * CAP, 1], I32)
    xb_dram = kb.dram("xb_dram", [CAP, D], BF16)
    yT_dram = kb.dram("yT_dram", [D, CAP], BF16)

    acc = kb.sbuf("acc", [128, 256])
    tmp = kb.sbuf("tmp", [128, 256])
    kb.dma("sync", acc[:, :], stg_all.ap(0, [[256, 128], [1, 256]]),
           reads=[stg_all], writes=[acc])
    for r in range(1, 8):
        kb.dma("sync", tmp[:, :], stg_all.ap(r * EXP * CAP, [[256, 128], [1, 256]]),
               reads=[stg_all], writes=[tmp], stream="d_tmp")
        kb.op("vector", lambda q: q.tensor_tensor(
            out=acc[:, :], in0=acc[:, :], in1=tmp[:, :], op=ALU.add),
            reads=[acc, tmp], writes=[acc])
    acci = kb.sbuf("acci", [128, 256], I32)
    kb.op("vector", lambda q: q.tensor_copy(out=acci[:, :], in_=acc[:, :]),
          reads=[acc], writes=[acci])
    kb.dma("sync", stgi.ap(0, [[256, 128], [1, 256]]), acci[:, :],
           reads=[acci], writes=[stgi])

    idxt = [kb.sbuf(f"idxt{i}", [128, 1], I32) for i in range(4)]
    gt = [kb.sbuf(f"gt{i}", [128, D], BF16) for i in range(2)]
    eidv = kb.sbuf("eidv", [1, 1])
    kb.dma("sync", eidv[:, :], eid.ap(0, [[1, 1], [1, 1]]), reads=[eid], writes=[eidv])
    for js in range(32):
        it = idxt[js % 4]
        g = gt[js % 2]
        kb.dma("sync", it[:, :], stgi.ap(js * 128, [[1, 128], [1, 1]]),
               reads=[stgi], writes=[it], stream=f"d_idx{js % 4}")
        kb.idma(g[:, :], yfull.ap(0, [[D, 16384], [1, D]]),
                in_off=bass.IndirectOffsetOnAxis(it[:, :], 0),
                bounds=16383, reads=[yfull, it], writes=[g])
        kb.dma("gpsimd", xb_dram.ap(js * 128 * D, [[D, 128], [1, D]]), g[:, :],
               reads=[g], writes=[xb_dram], stream=f"d_xbd{js % 2}")

    w1b = kb.sbuf("w1b", [128, 8 * 4096], BF16)
    kb.dma("gpsimd", bass.AP(w1b.h, 0, [[8 * 4096, 128], [4096, 8], [1, 4096]]),
           w1e.ap(0, [[4096, 128], [128 * 4096, 8], [1, 4096]]),
           reads=[w1e], writes=[w1b])
    w2b = kb.sbuf("w2b", [128, 32 * D], BF16)
    kb.dma("gpsimd", bass.AP(w2b.h, 0, [[32 * D, 128], [D, 32], [1, D]]),
           w2e.ap(0, [[D, 128], [128 * D, 32], [1, D]]),
           reads=[w2e], writes=[w2b])

    ps = [kb.psum(f"ps{i}", [128, 512]) for i in range(4)]
    xbT = [kb.sbuf(f"xbT{kd}", [128, 512], BF16) for kd in range(8)]
    hT = kb.sbuf("hT", [128, 32 * 512], BF16)
    yto = [kb.sbuf(f"yto{i}", [128, 512], BF16) for i in range(2)]
    for s0 in range(8):
        for kd in range(8):
            kb.dma("sync", xbT[kd][:, :],
                   xb_dram.ap(s0 * 512 * D + kd * 128, [[D, 512], [1, 128]]),
                   reads=[xb_dram], writes=[xbT[kd]], transpose=True)
        for mf in range(32):
            p = ps[mf % 4]
            pairs = [(w1b.ap(kd * 4096 + mf * 128, [[8 * 4096, 128], [1, 128]]),
                      xbT[kd][:, :]) for kd in range(8)]
            matmul_group(kb, p, p.h[:, 0:512], pairs, [w1b] + xbT)
            kb.op("scalar", lambda q, p=p, mf=mf: q.activation(
                out=hT[:, mf * 512:(mf + 1) * 512], in_=p.h[:, 0:512], func=AF.Gelu),
                reads=[p], writes=[hT])
        for mo in range(8):
            p = ps[mo % 4]
            y = yto[mo % 2]
            pairs = [(w2b.ap(kf * D + mo * 128, [[32 * D, 128], [1, 128]]),
                      hT[:, kf * 512:(kf + 1) * 512]) for kf in range(32)]
            matmul_group(kb, p, p.h[:, 0:512], pairs, [w2b, hT])
            kb.op("vector", lambda q, p=p, y=y: q.tensor_copy(
                out=y[:, :], in_=p.h[:, 0:512]), reads=[p], writes=[y])
            kb.dma("sync", yT_dram.ap(mo * 128 * CAP + s0 * 512, [[CAP, 128], [1, 512]]),
                   y[:, :], reads=[y], writes=[yT_dram], stream=f"d_yT{mo % 2}")

    ytk = [kb.sbuf(f"ytk{i}", [128, D], BF16) for i in range(2)]
    for js in range(32):
        y = ytk[js % 2]
        kb.dma("sync", y[:, :], yT_dram.ap(js * 128, [[CAP, D], [1, 128]]),
               reads=[yT_dram], writes=[y], transpose=True)
        kb.dma("gpsimd", o_ye.ap(js * 128 * D, [[D, 128], [1, D]]), y[:, :],
               reads=[y], writes=[o_ye], stream=f"d_oye{js % 2}")
    if standalone:
        return kb.finalize()


def stage_moe_expert(inputs, prev):
    stg = np.stack([np.asarray(prev["o_stg"][c]).reshape(-1) for c in range(8)], 0)
    yfull = np.concatenate([np.asarray(prev["o_yln2"][c]) for c in range(8)], 0)
    w1 = inputs["L0_ffn_w1"]
    w2 = inputs["L0_ffn_w2"]
    maps = []
    for e in range(NCORE):
        stg_rot = np.ascontiguousarray(
            np.roll(stg, -e * CAP, axis=1))
        maps.append(dict(
            stg_all=stg_rot, yfull=np.ascontiguousarray(yfull),
            w1e=np.ascontiguousarray(w1[e]), w2e=np.ascontiguousarray(w2[e]),
            eid=np.array([[float(e)]], np.float32)))
    return maps



def build_moe_combine(kb=None, ext=None):
    standalone = kb is None
    if standalone:
        kb = KB("moe_combine")

    def _inp(name, shape, dtype=F32):
        return ext[name] if (ext and name in ext) else kb.inp(name, shape, dtype)

    def _out(name, shape, dtype=F32):
        return ext[name] if (ext and name in ext) else kb.out(name, shape, dtype)
    yall = _inp("yall", [EXP * CAP, D], BF16)
    x1s = _inp("x1s", [TC, D])
    weff_in = _inp("weff_in", [128, 32])
    gidx_in = _inp("gidx_in", [128, 32], I32)
    biasb = _inp("biasb", [128, D])

    o_x2 = _out("o_x2", [TC, D])
    o_yln3 = _out("o_yln3", [TC, D], BF16)

    wf = kb.sbuf("wf", [128, 32])
    kb.dma("sync", wf[:, :], weff_in.ap(0, [[32, 128], [1, 32]]),
           reads=[weff_in], writes=[wf])
    gx = kb.sbuf("gx", [128, 32], I32)
    kb.dma("sync", gx[:, :], gidx_in.ap(0, [[32, 128], [1, 32]]),
           reads=[gidx_in], writes=[gx])
    bb = kb.sbuf("bb", [128, D])
    kb.dma("sync", bb[:, :], biasb.ap(0, [[D, 128], [1, D]]), reads=[biasb], writes=[bb])

    g1 = [kb.sbuf(f"g1_{i}", [128, D], BF16) for i in range(2)]
    g2 = [kb.sbuf(f"g2_{i}", [128, D], BF16) for i in range(2)]
    x1t = [kb.sbuf(f"x1t{i}", [128, D]) for i in range(2)]
    acc = [kb.sbuf(f"acc{i}", [128, D]) for i in range(2)]
    yl = [kb.sbuf(f"yl{i}", [128, D], BF16) for i in range(2)]
    sq = [kb.sbuf("sq", [128, D])]
    stat = kb.sbuf("stat", [128, 8])
    for j in range(16):
        ga, gb2, x1b, ac, ylb = g1[j % 2], g2[j % 2], x1t[j % 2], acc[j % 2], yl[j % 2]
        kb.dma("sync", x1b[:, :], x1s.ap(j * 128 * D, [[D, 128], [1, D]]),
               reads=[x1s], writes=[x1b], stream=f"d_x1{j % 2}")
        kb.op("vector", lambda q, ga=ga: q.memset(ga.h[:, :], 0.0),
              reads=[], writes=[ga])
        kb.op("vector", lambda q, gb2=gb2: q.memset(gb2.h[:, :], 0.0),
              reads=[], writes=[gb2])
        kb.idma(ga[:, :], yall.ap(0, [[D, EXP * CAP], [1, D]]),
                in_off=bass.IndirectOffsetOnAxis(gx.h[:, j:j + 1], 0),
                bounds=EXP * CAP - 1, reads=[yall, gx], writes=[ga])
        kb.idma(gb2[:, :], yall.ap(0, [[D, EXP * CAP], [1, D]]),
                in_off=bass.IndirectOffsetOnAxis(gx.h[:, 16 + j:16 + j + 1], 0),
                bounds=EXP * CAP - 1, reads=[yall, gx], writes=[gb2])
        kb.op("vector", lambda q, ga=ga, ac=ac, j=j: q.tensor_scalar(
            out=ac[:, :], in0=ga[:, :], scalar1=wf.h[:, j:j + 1], scalar2=None,
            op0=ALU.mult), reads=[ga, wf], writes=[ac])
        kb.op("vector", lambda q, gb2=gb2, ac=ac, j=j: q.scalar_tensor_tensor(
            out=ac[:, :], in0=gb2[:, :], scalar=wf.h[:, 16 + j:16 + j + 1],
            in1=ac[:, :], op0=ALU.mult, op1=ALU.add),
            reads=[gb2, wf, ac], writes=[ac])
        kb.op("vector", lambda q, x1b=x1b, ac=ac: q.tensor_tensor(
            out=ac[:, :], in0=ac[:, :], in1=x1b[:, :], op=ALU.add),
            reads=[ac, x1b], writes=[ac])
        kb.op("vector", lambda q, ac=ac: q.tensor_tensor(
            out=ac[:, :], in0=ac[:, :], in1=bb[:, :], op=ALU.add),
            reads=[ac, bb], writes=[ac])
        kb.dma("sync", o_x2.ap(j * 128 * D, [[D, 128], [1, D]]), ac[:, :],
               reads=[ac], writes=[o_x2], stream=f"d_ox2{j % 2}")
        mu_ap, rstd_ap = e_layernorm(kb, ac, ac[:, :], stat, sq)
        kb.op("vector", lambda q, ac=ac, ylb=ylb, mu_ap=mu_ap, rstd_ap=rstd_ap:
              q.tensor_scalar(out=ylb[:, :], in0=ac[:, :], scalar1=mu_ap,
                              scalar2=rstd_ap, op0=ALU.subtract, op1=ALU.mult),
              reads=[ac, stat], writes=[ylb])
        kb.dma("sync", o_yln3.ap(j * 128 * D, [[D, 128], [1, D]]), ylb[:, :],
               reads=[ylb], writes=[o_yln3], stream=f"d_oyl{j % 2}")
    if standalone:
        return kb.finalize()


def stage_moe_combine(inputs, prev):
    yall = np.concatenate([np.asarray(prev["o_ye"][c]) for c in range(8)], 0)
    biasb = np.ascontiguousarray(np.broadcast_to(
        np.asarray(inputs["L0_ffn_bias"]).reshape(D), (128, D)))
    maps = []
    for c in range(NCORE):
        maps.append(dict(yall=yall, x1s=prev["o_x1"][c],
                         weff_in=prev["o_weff"][c], gidx_in=prev["o_gidx"][c],
                         biasb=biasb))
    return maps





FP4_MID = (0.25, 0.75, 1.25, 1.75, 2.5, 3.5, 5.0)
FP4_DG = (0.5, 0.5, 0.5, 0.5, 1.0, 1.0, 2.0)


def e_fp4(kb, x_buf, x_ap, P, W, qs, out_buf, out_ap):
    nb = W // 16
    ax, qa, tt, rs = qs["ax"], qs["qa"], qs["tt"], qs["rs"]
    kb.op("scalar", lambda q: q.activation(out=ax.h[0:P, 0:W], in_=x_ap, func=AF.Abs),
          reads=[x_buf], writes=[ax])
    ax3 = bass.AP(ax.h, 0, [[ax.h.shape[1], P], [16, nb], [1, 16]])
    kb.op("vector", lambda q: q.tensor_reduce(
        out=rs.h[0:P, 0:nb], in_=ax3, axis=AX.X, op=ALU.max),
        reads=[ax], writes=[rs])
    kb.op("vector", lambda q: q.tensor_scalar(
        out=rs.h[0:P, nb:2 * nb], in0=rs.h[0:P, 0:nb], scalar1=1.0 / 6.0,
        scalar2=1e-12, op0=ALU.mult, op1=ALU.max), reads=[rs], writes=[rs])
    kb.op("vector", lambda q: q.reciprocal(
        out=rs.h[0:P, 2 * nb:3 * nb], in_=rs.h[0:P, nb:2 * nb]),
        reads=[rs], writes=[rs])
    rsw = rs.h.shape[1]
    rrep = bass.AP(rs.h, 2 * nb, [[rsw, P], [1, nb], [0, 16]])
    kb.op("vector", lambda q: q.tensor_tensor(
        out=ax3, in0=ax3, in1=rrep, op=ALU.mult), reads=[ax, rs], writes=[ax])
    kb.op("vector", lambda q: q.tensor_scalar(
        out=qa.h[0:P, 0:W], in0=ax.h[0:P, 0:W], scalar1=FP4_MID[0],
        scalar2=FP4_DG[0], op0=ALU.is_gt, op1=ALU.mult), reads=[ax], writes=[qa])
    for j in range(1, 7):
        kb.op("vector", lambda q, j=j: q.tensor_scalar(
            out=tt.h[0:P, 0:W], in0=ax.h[0:P, 0:W], scalar1=FP4_MID[j],
            scalar2=FP4_DG[j], op0=ALU.is_gt, op1=ALU.mult), reads=[ax], writes=[tt])
        kb.op("vector", lambda q: q.tensor_tensor(
            out=qa.h[0:P, 0:W], in0=qa.h[0:P, 0:W], in1=tt.h[0:P, 0:W],
            op=ALU.add), reads=[qa, tt], writes=[qa])
    kb.op("scalar", lambda q: q.activation(out=ax.h[0:P, 0:W], in_=x_ap, func=AF.Sign),
          reads=[x_buf], writes=[ax])
    srep = bass.AP(rs.h, nb, [[rsw, P], [1, nb], [0, 16]])
    kb.op("vector", lambda q: q.tensor_tensor(
        out=ax3, in0=ax3, in1=srep, op=ALU.mult), reads=[ax, rs], writes=[ax])
    kb.op("vector", lambda q: q.tensor_tensor(
        out=out_ap, in0=qa.h[0:P, 0:W], in1=ax.h[0:P, 0:W], op=ALU.mult),
        reads=[qa, ax], writes=[out_buf])



def build_l1a_pre(kb=None, ext=None):
    standalone = kb is None
    if standalone:
        kb = KB("l1a_pre")

    def _inp(name, shape, dtype=F32):
        return ext[name] if (ext and name in ext) else kb.inp(name, shape, dtype)

    def _out(name, shape, dtype=F32):
        return ext[name] if (ext and name in ext) else kb.out(name, shape, dtype)
    yln3 = _inp("yln3", [TC, D], BF16)
    W_in1 = _inp("W_in1", [D, 2 * DI])
    o_xi = _out("o_xi", [DI, TC], BF16)
    o_sz1 = _out("o_sz1", [DI, TC], BF16)

    ylT = [kb.sbuf(f"ylT{k}", [128, TC], BF16) for k in range(8)]
    for kd in range(8):
        kb.dma("sync", ylT[kd][:, :], yln3.ap(kd * 128, [[D, TC], [1, 128]]),
               reads=[yln3], writes=[ylT[kd]], transpose=True)
    wbuf = kb.sbuf("wbuf", [128, 8 * DI], BF16)
    kb.dma("gpsimd", bass.AP(wbuf.h, 0, [[8 * DI, 128], [DI, 8], [1, DI]]),
           W_in1.ap(0, [[2 * DI, 128], [128 * 2 * DI, 8], [1, DI]]),
           reads=[W_in1], writes=[wbuf])
    ps = [kb.psum(f"ps{i}", [128, 512]) for i in range(4)]
    ot = [kb.sbuf(f"ot{i}", [128, 512], BF16) for i in range(2)]
    for m in range(16):
        for ci, (t0, nt) in enumerate(CHUNKS4):
            p = ps[ci % 4]
            o = ot[ci % 2]
            pairs = [(wbuf.ap(kd * DI + m * 128, [[8 * DI, 128], [1, 128]]),
                      ylT[kd][:, t0:t0 + nt]) for kd in range(8)]
            matmul_group(kb, p, p.h[:, 0:nt], pairs, [wbuf] + ylT)
            kb.op("vector", lambda q, p=p, o=o, nt=nt: q.tensor_copy(
                out=o[:, 0:nt], in_=p.h[:, 0:nt]), reads=[p], writes=[o])
            kb.dma("sync", o_xi.ap(m * 128 * TC + t0, [[TC, 128], [1, nt]]),
                   o[:, 0:nt], reads=[o], writes=[o_xi], stream=f"d_oxi{ci % 2}")
    kb.dma("gpsimd", bass.AP(wbuf.h, 0, [[8 * DI, 128], [DI, 8], [1, DI]]),
           W_in1.ap(DI, [[2 * DI, 128], [128 * 2 * DI, 8], [1, DI]]),
           reads=[W_in1], writes=[wbuf])
    for m in range(16):
        for ci, (t0, nt) in enumerate(CHUNKS4):
            p = ps[ci % 4]
            o = ot[ci % 2]
            pairs = [(wbuf.ap(kd * DI + m * 128, [[8 * DI, 128], [1, 128]]),
                      ylT[kd][:, t0:t0 + nt]) for kd in range(8)]
            matmul_group(kb, p, p.h[:, 0:nt], pairs, [wbuf] + ylT)
            kb.op("scalar", lambda q, p=p, o=o, nt=nt: q.activation(
                out=o[:, 0:nt], in_=p.h[:, 0:nt], func=AF.Silu),
                reads=[p], writes=[o])
            kb.dma("sync", o_sz1.ap(m * 128 * TC + t0, [[TC, 128], [1, nt]]),
                   o[:, 0:nt], reads=[o], writes=[o_sz1], stream=f"d_osz{ci % 2}")
    if standalone:
        return kb.finalize()


def stage_l1a_pre(inputs, prev):
    W_in1 = np.ascontiguousarray(inputs["L1_attn_W_in"])
    maps = []
    for c in range(NCORE):
        maps.append(dict(yln3=prev["o_yln3"][c], W_in1=W_in1))
    return maps



def build_l1a_mid(kb=None, ext=None):
    standalone = kb is None
    if standalone:
        kb = KB("l1a_mid")

    def _inp(name, shape, dtype=F32):
        return ext[name] if (ext and name in ext) else kb.inp(name, shape, dtype)

    def _out(name, shape, dtype=F32):
        return ext[name] if (ext and name in ext) else kb.out(name, shape, dtype)
    xih = _inp("xih", [DI, TC + 3], BF16)
    cw4 = _inp("cw4", [128, 64])
    cbv = _inp("cbv", [128, 16])
    W_xproj = _inp("W_xproj", [DI, 2 * NS])
    W_dt = _inp("W_dt", [DI, DI])
    bdt = _inp("bdt", [128, 16])

    o_xc1 = _out("o_xc1", [DI, TC], BF16)
    o_bt = _out("o_bt", [NS, TC])
    o_delta = _out("o_delta", [DI, TC])

    cwb = kb.sbuf("cwb", [128, 64])
    kb.dma("sync", cwb[:, :], cw4.ap(0, [[64, 128], [1, 64]]), reads=[cw4], writes=[cwb])
    cbb = kb.sbuf("cbb", [128, 16])
    kb.dma("sync", cbb[:, :], cbv.ap(0, [[16, 128], [1, 16]]), reads=[cbv], writes=[cbb])
    bdtb = kb.sbuf("bdtb", [128, 16])
    kb.dma("sync", bdtb[:, :], bdt.ap(0, [[16, 128], [1, 16]]), reads=[bdt], writes=[bdtb])

    xc = [kb.sbuf(f"xct{m}", [128, TC], BF16) for m in range(16)]
    xihb = [kb.sbuf(f"xihb{m}", [128, TC + 3], BF16, side="right") for m in range(16)]
    convb = [kb.sbuf(f"convb{i}", [128, TC]) for i in range(2)]
    for m in range(16):
        kb.dma("sync", xihb[m][:, :],
               xih.ap(m * 128 * (TC + 3), [[TC + 3, 128], [1, TC + 3]]),
               reads=[xih], writes=[xihb[m]])
        cvb = convb[m % 2]
        kb.op("vector", lambda q, m=m, cvb=cvb: q.tensor_scalar(
            out=cvb[:, :], in0=xihb[m][:, 0:TC],
            scalar1=cwb.h[:, m * 4:m * 4 + 1], scalar2=None, op0=ALU.mult),
            reads=[xihb[m], cwb], writes=[cvb])
        for tap in range(1, 4):
            kb.op("vector", lambda q, m=m, cvb=cvb, tap=tap: q.scalar_tensor_tensor(
                out=cvb[:, :], in0=xihb[m][:, tap:tap + TC],
                scalar=cwb.h[:, m * 4 + tap:m * 4 + tap + 1],
                in1=cvb[:, :], op0=ALU.mult, op1=ALU.add),
                reads=[xihb[m], cwb, cvb], writes=[cvb])
        kb.op("scalar", lambda q, m=m, cvb=cvb: q.activation(
            out=xc[m][:, :], in_=cvb[:, :], func=AF.Silu,
            bias=cbb.h[:, m:m + 1]), reads=[cvb, cbb], writes=[xc[m]])
        kb.dma("sync", o_xc1.ap(m * 128 * TC, [[TC, 128], [1, TC]]),
               xc[m][:, :], reads=[xc[m]], writes=[o_xc1], stream="d_oxc")
    kb.free(*xihb)

    wxp = kb.sbuf("wxp", [128, 16 * 2 * NS], BF16)
    kb.dma("gpsimd", bass.AP(wxp.h, 0, [[16 * 2 * NS, 128], [2 * NS, 16], [1, 2 * NS]]),
           W_xproj.ap(0, [[2 * NS, 128], [128 * 2 * NS, 16], [1, 2 * NS]]),
           reads=[W_xproj], writes=[wxp])
    ps32 = kb.psum("ps32", [32, 512])
    spb = [kb.sbuf(f"spb{i}", [32, 512]) for i in range(2)]
    for ci, (t0, nt) in enumerate(CHUNKS4):
        pairs = [(wxp.ap(kd * 2 * NS, [[16 * 2 * NS, 128], [1, 2 * NS]]),
                  xc[kd][:, t0:t0 + nt]) for kd in range(16)]
        matmul_group(kb, ps32, ps32.h[:, 0:nt], pairs, [wxp] + xc)
        s = spb[ci % 2]
        kb.op("vector", lambda q, s=s, nt=nt: q.tensor_copy(
            out=s[:, 0:nt], in_=ps32.h[:, 0:nt]), reads=[ps32], writes=[s])
        kb.dma("sync", o_bt.ap(t0, [[TC, NS], [1, nt]]),
               s.h[NS:2 * NS, 0:nt], reads=[s], writes=[o_bt], stream=f"d_obt{ci % 2}")

    wdt = kb.sbuf("wdt", [128, 16 * DI], BF16)
    kb.dma("gpsimd", bass.AP(wdt.h, 0, [[16 * DI, 128], [DI, 16], [1, DI]]),
           W_dt.ap(0, [[DI, 128], [128 * DI, 16], [1, DI]]),
           reads=[W_dt], writes=[wdt])
    ps = [kb.psum(f"ps{i}", [128, 512]) for i in range(2)]
    t1 = kb.sbuf("t1", [128, 512])
    t2 = kb.sbuf("t2", [128, 512])
    t3 = kb.sbuf("t3", [128, 512])
    dto = [kb.sbuf(f"dto{i}", [128, 512]) for i in range(2)]
    for m in range(16):
        for ci, (t0, nt) in enumerate(CHUNKS4):
            p = ps[ci % 2]
            d = dto[ci % 2]
            pairs = [(wdt.ap(kd * DI + m * 128, [[16 * DI, 128], [1, 128]]),
                      xc[kd][:, t0:t0 + nt]) for kd in range(16)]
            matmul_group(kb, p, p.h[:, 0:nt], pairs, [wdt] + xc)
            kb.op("scalar", lambda q, p=p, m=m: q.activation(
                out=t1[:, :], in_=p.h[:, 0:512], func=AF.Abs,
                bias=bdtb.h[:, m:m + 1]), reads=[p, bdtb], writes=[t1])
            kb.op("scalar", lambda q: q.activation(
                out=t2[:, :], in_=t1[:, :], func=AF.Exp, scale=-1.0),
                reads=[t1], writes=[t2])
            kb.op("scalar", lambda q: q.activation(
                out=t1[:, :], in_=t2[:, :], func=AF.Ln, bias=1.0),
                reads=[t2], writes=[t1])
            kb.op("scalar", lambda q, p=p, m=m: q.activation(
                out=t3[:, :], in_=p.h[:, 0:512], func=AF.Relu,
                bias=bdtb.h[:, m:m + 1]), reads=[p, bdtb], writes=[t3])
            kb.op("vector", lambda q, d=d: q.tensor_tensor(
                out=d[:, :], in0=t1[:, :], in1=t3[:, :], op=ALU.add),
                reads=[t1, t3], writes=[d])
            kb.dma("sync", o_delta.ap(m * 128 * TC + t0, [[TC, 128], [1, nt]]),
                   d[:, :], reads=[d], writes=[o_delta], stream=f"d_odel{ci % 2}")
    if standalone:
        return kb.finalize()


def stage_l1a_mid(inputs, prev):
    cw4 = np.ascontiguousarray(
        inputs["L1_attn_conv_w"].reshape(16, 128, 4).transpose(1, 0, 2).reshape(128, 64))
    cbv = np.ascontiguousarray(inputs["L1_attn_conv_b"].reshape(16, 128).T)
    bdt = np.ascontiguousarray(inputs["L1_attn_b_dt"].reshape(16, 128).T)
    W_xproj = np.ascontiguousarray(inputs["L1_attn_W_xproj"])
    W_dt = np.ascontiguousarray(inputs["L1_attn_W_dt"])
    maps = []
    for c in range(NCORE):
        b, h = divmod(c, 2)
        xi = np.asarray(prev["o_xi"][c])
        if h == 1:
            halo = np.asarray(prev["o_xi"][c - 1])[:, -3:]
        else:
            halo = np.zeros((DI, 3), xi.dtype)
        xih = np.ascontiguousarray(np.concatenate([halo, xi], axis=1))
        maps.append(dict(xih=xih, cw4=cw4, cbv=cbv, W_xproj=W_xproj,
                         W_dt=W_dt, bdt=bdt))
    return maps



def build_l1a_scan(kb=None, ext=None):
    standalone = kb is None
    if standalone:
        kb = KB("l1a_scan")

    def _inp(name, shape, dtype=F32):
        return ext[name] if (ext and name in ext) else kb.inp(name, shape, dtype)

    def _out(name, shape, dtype=F32):
        return ext[name] if (ext and name in ext) else kb.out(name, shape, dtype)
    delta2 = _inp("delta2", [D, TFULL])
    xc2 = _inp("xc2", [D, TFULL], BF16)
    BT2 = _inp("BT2", [NS, TFULL])
    Alog = _inp("Alog", [128, 8 * NS])
    Dps = _inp("Dps", [128, 8])
    o_y = _out("o_y", [D, TFULL], BF16)

    alg = kb.sbuf("alg", [128, 8 * NS])
    kb.dma("sync", alg[:, :], Alog.ap(0, [[8 * NS, 128], [1, 8 * NS]]),
           reads=[Alog], writes=[alg])
    dpb = kb.sbuf("dpb", [128, 8])
    kb.dma("sync", dpb[:, :], Dps.ap(0, [[8, 128], [1, 8]]), reads=[Dps], writes=[dpb])
    aa = kb.sbuf("aa", [128, 8 * NS])
    kb.op("scalar", lambda q: q.activation(out=aa[:, :], in_=alg[:, :], func=AF.Exp),
          reads=[alg], writes=[aa])
    kb.op("vector", lambda q: q.tensor_scalar(
        out=aa[:, :], in0=aa[:, :], scalar1=-1.0, scalar2=None, op0=ALU.mult),
        reads=[aa], writes=[aa])

    TCH = 2048
    bbc = [kb.sbuf(f"bbc{n}", [128, TCH], BF16) for n in range(NS)]
    scar = [kb.sbuf(f"scar{db}", [128, NS]) for db in range(8)]
    for db in range(8):
        kb.op("vector", lambda q, db=db: q.memset(scar[db].h[:, :], 0.0),
              reads=[], writes=[scar[db]])

    dl = [kb.sbuf(f"dl{i}", [128, TCH]) for i in range(2)]
    xcb = [kb.sbuf(f"xcb{i}", [128, TCH], BF16) for i in range(2)]
    dx = [kb.sbuf(f"dx{i}", [128, TCH], BF16) for i in range(2)]
    ab = [kb.sbuf(f"ab{i}", [128, TCH]) for i in range(2)]
    ub = [kb.sbuf(f"ub{i}", [128, TCH], BF16) for i in range(2)]
    sb = [kb.sbuf(f"sb{i}", [128, TCH], BF16) for i in range(2)]
    sm = kb.sbuf("sm", [128, TCH], BF16)
    ya = [kb.sbuf(f"ya{i}", [128, TCH]) for i in range(2)]
    for tc in range(2):
        for n in range(NS):
            kb.dma("gpsimd", bbc[n][:, :],
                   BT2.ap(n * TFULL + tc * TCH, [[0, 128], [1, TCH]]),
                   reads=[BT2], writes=[bbc[n]])
        for db in range(8):
            d = dl[db % 2]
            x = xcb[db % 2]
            dxb = dx[db % 2]
            y = ya[db % 2]
            kb.dma("sync", d[:, :],
                   delta2.ap(db * 128 * TFULL + tc * TCH, [[TFULL, 128], [1, TCH]]),
                   reads=[delta2], writes=[d], stream=f"d_dl{db % 2}")
            kb.dma("sync", x[:, :],
                   xc2.ap(db * 128 * TFULL + tc * TCH, [[TFULL, 128], [1, TCH]]),
                   reads=[xc2], writes=[x], stream=f"d_xc{db % 2}")
            kb.op("vector", lambda q, d=d, x=x, dxb=dxb: q.tensor_tensor(
                out=dxb[:, :], in0=d[:, :], in1=x[:, :], op=ALU.mult),
                reads=[d, x], writes=[dxb])
            kb.op("vector", lambda q, x=x, y=y, db=db: q.tensor_scalar(
                out=y[:, :], in0=x[:, :], scalar1=dpb.h[:, db:db + 1], scalar2=None,
                op0=ALU.mult), reads=[x, dpb], writes=[y])
            for n in range(NS):
                a = ab[n % 2]
                u = ub[n % 2]
                s = sb[n % 2]
                kb.op("scalar", lambda q, d=d, a=a, db=db, n=n: q.activation(
                    out=a[:, :], in_=d[:, :], func=AF.Exp,
                    scale=aa.h[:, db * NS + n:db * NS + n + 1]),
                    reads=[d, aa], writes=[a])
                kb.op("vector", lambda q, dxb=dxb, u=u, n=n: q.tensor_tensor(
                    out=u[:, :], in0=dxb[:, :], in1=bbc[n][:, :], op=ALU.mult),
                    reads=[dxb, bbc[n]], writes=[u])
                if tc == 0:
                    kb.op("vector", lambda q, a=a, u=u, s=s: q.tensor_tensor_scan(
                        out=s[:, :], data0=a[:, :], data1=u[:, :], initial=0.0,
                        op0=ALU.mult, op1=ALU.add), reads=[a, u], writes=[s])
                else:
                    kb.op("vector", lambda q, a=a, u=u, s=s, db=db, n=n:
                          q.tensor_tensor_scan(
                              out=s[:, :], data0=a[:, :], data1=u[:, :],
                              initial=scar[db].h[:, n:n + 1],
                              op0=ALU.mult, op1=ALU.add),
                          reads=[a, u, scar[db]], writes=[s])
                if tc == 0:
                    kb.op("vector", lambda q, s=s, db=db, n=n: q.tensor_copy(
                        out=scar[db].h[:, n:n + 1], in_=s[:, TCH - 1:TCH]),
                        reads=[s], writes=[scar[db]])
                kb.op("vector", lambda q, s=s, n=n: q.tensor_tensor(
                    out=sm[:, :], in0=s[:, :], in1=bbc[n][:, :], op=ALU.mult),
                    reads=[s, bbc[n]], writes=[sm])
                kb.op("vector", lambda q, y=y: q.tensor_tensor(
                    out=y[:, :], in0=y[:, :], in1=sm[:, :], op=ALU.add),
                    reads=[y, sm], writes=[y])
            kb.dma("gpsimd", o_y.ap(db * 128 * TFULL + tc * TCH, [[TFULL, 128], [1, TCH]]),
                   y[:, :], reads=[y], writes=[o_y], stream=f"d_oy{db % 2}")
    if standalone:
        return kb.finalize()


def stage_l1a_scan(inputs, prev):
    A_log = np.asarray(inputs["L1_attn_A_log"])
    Dp = np.asarray(inputs["L1_attn_Dp"])
    maps = []
    for c in range(NCORE):
        b, h = divmod(c, 2)
        hs = slice(h * D, (h + 1) * D)
        delta2 = np.concatenate([np.asarray(prev["o_delta"][2 * b])[hs],
                                 np.asarray(prev["o_delta"][2 * b + 1])[hs]], 1)
        xc2 = np.concatenate([np.asarray(prev["o_xc1"][2 * b])[hs],
                              np.asarray(prev["o_xc1"][2 * b + 1])[hs]], 1)
        BT2 = np.concatenate([np.asarray(prev["o_bt"][2 * b]),
                              np.asarray(prev["o_bt"][2 * b + 1])], 1)
        Al = A_log[hs].reshape(8, 128, NS).transpose(1, 0, 2).reshape(128, 8 * NS)
        Dps = Dp[hs].reshape(8, 128).T
        maps.append(dict(delta2=np.ascontiguousarray(delta2),
                         xc2=np.ascontiguousarray(xc2),
                         BT2=np.ascontiguousarray(BT2),
                         Alog=np.ascontiguousarray(Al),
                         Dps=np.ascontiguousarray(Dps)))
    return maps



def build_l1a_post(kb=None, ext=None):
    standalone = kb is None
    if standalone:
        kb = KB("l1a_post")

    def _inp(name, shape, dtype=F32):
        return ext[name] if (ext and name in ext) else kb.inp(name, shape, dtype)

    def _out(name, shape, dtype=F32):
        return ext[name] if (ext and name in ext) else kb.out(name, shape, dtype)
    y2 = _inp("y2", [DI, TC], BF16)
    sz1 = _inp("sz1", [DI, TC], BF16)
    W_out1 = _inp("W_out1", [DI, D])
    x2s = _inp("x2s", [TC, D])
    wg_s = _inp("wg_s", [352, D])
    wu_s = _inp("wu_s", [352, D])
    wd_s = _inp("wd_s", [128, 2816])

    o_x3 = _out("o_x3", [TC, D])
    o_xq = _out("o_xq", [TC, D], BF16)
    o_wgq = _out("o_wgq", [352, D], BF16)
    o_wuq = _out("o_wuq", [352, D], BF16)
    o_wdq = _out("o_wdq", [128, 2816], BF16)

    ygt = [kb.sbuf(f"ygt{k}", [128, TC], BF16) for k in range(16)]
    szb = [kb.sbuf(f"szb{i}", [128, TC], BF16) for i in range(2)]
    for k in range(16):
        s = szb[k % 2]
        kb.dma("sync", ygt[k][:, :], y2.ap(k * 128 * TC, [[TC, 128], [1, TC]]),
               reads=[y2], writes=[ygt[k]])
        kb.dma("sync", s[:, :], sz1.ap(k * 128 * TC, [[TC, 128], [1, TC]]),
               reads=[sz1], writes=[s], stream=f"d_sz{k % 2}")
        kb.op("vector", lambda q, k=k, s=s: q.tensor_tensor(
            out=ygt[k][:, :], in0=ygt[k][:, :], in1=s[:, :], op=ALU.mult),
            reads=[ygt[k], s], writes=[ygt[k]])
    wo1 = kb.sbuf("wo1", [128, 16 * D], BF16)
    kb.dma("gpsimd", bass.AP(wo1.h, 0, [[16 * D, 128], [D, 16], [1, D]]),
           W_out1.ap(0, [[D, 128], [128 * D, 16], [1, D]]),
           reads=[W_out1], writes=[wo1])

    ps = [kb.psum(f"ps{i}", [128, 512]) for i in range(2)]
    x2t = [kb.sbuf(f"x2t{i}", [128, D]) for i in range(2)]
    x3t = [kb.sbuf(f"x3t{i}", [128, D]) for i in range(2)]
    xl4 = [kb.sbuf(f"xl4{i}", [128, D]) for i in range(2)]
    xqt = [kb.sbuf(f"xqt{i}", [128, D], BF16) for i in range(2)]
    sq = [kb.sbuf("sq", [128, D])]
    stat = kb.sbuf("stat", [128, 8])
    qs = dict(ax=kb.sbuf("q_ax", [128, 2816]), qa=kb.sbuf("q_qa", [128, 2816]),
              tt=kb.sbuf("q_tt", [128, 2816]), rs=kb.sbuf("q_rs", [128, 3 * 176]))
    for j in range(16):
        x2b, x3b, xlb, xqb = x2t[j % 2], x3t[j % 2], xl4[j % 2], xqt[j % 2]
        kb.dma("sync", x2b[:, :], x2s.ap(j * 128 * D, [[D, 128], [1, D]]),
               reads=[x2s], writes=[x2b], stream=f"d_x2{j % 2}")
        for ci in range(2):
            p = ps[ci]
            pairs = [(ygt[k][:, j * 128:(j + 1) * 128],
                      wo1.ap(k * D + ci * 512, [[16 * D, 128], [1, 512]]))
                     for k in range(16)]
            matmul_group(kb, p, p.h[:, 0:512], pairs, ygt + [wo1])
            kb.op("vector", lambda q, p=p, x3b=x3b, x2b=x2b, ci=ci: q.tensor_tensor(
                out=x3b[:, ci * 512:(ci + 1) * 512],
                in0=x2b[:, ci * 512:(ci + 1) * 512],
                in1=p.h[:, 0:512], op=ALU.add), reads=[p, x2b], writes=[x3b])
        kb.dma("sync", o_x3.ap(j * 128 * D, [[D, 128], [1, D]]), x3b[:, :],
               reads=[x3b], writes=[o_x3], stream=f"d_ox3{j % 2}")
        mu_ap, rstd_ap = e_layernorm(kb, x3b, x3b[:, :], stat, sq)
        kb.op("vector", lambda q, x3b=x3b, xlb=xlb, mu_ap=mu_ap, rstd_ap=rstd_ap:
              q.tensor_scalar(out=xlb[:, :], in0=x3b[:, :], scalar1=mu_ap,
                              scalar2=rstd_ap, op0=ALU.subtract, op1=ALU.mult),
              reads=[x3b, stat], writes=[xlb])
        e_fp4(kb, xlb, xlb[:, :], 128, D, qs, xqb, xqb[:, :])
        kb.dma("sync", o_xq.ap(j * 128 * D, [[D, 128], [1, D]]), xqb[:, :],
               reads=[xqb], writes=[o_xq], stream=f"d_oxq{j % 2}")

    wqt = kb.sbuf("wqt", [128, 2816])
    wqo = kb.sbuf("wqo", [128, 2816], BF16)
    for (win, wout, rows, wdth) in ((wg_s, o_wgq, 352, D), (wu_s, o_wuq, 352, D),
                                    (wd_s, o_wdq, 128, 2816)):
        r0 = 0
        while r0 < rows:
            pr = min(128, rows - r0)
            kb.dma("sync", wqt.h[0:pr, 0:wdth],
                   win.ap(r0 * wdth, [[wdth, pr], [1, wdth]]),
                   reads=[win], writes=[wqt], stream="d_wq")
            e_fp4(kb, wqt, wqt.h[0:pr, 0:wdth], pr, wdth, qs, wqo,
                  wqo.h[0:pr, 0:wdth])
            kb.dma("sync", wout.ap(r0 * wdth, [[wdth, pr], [1, wdth]]),
                   wqo.h[0:pr, 0:wdth], reads=[wqo], writes=[wout], stream="d_wqo")
            r0 += pr
    if standalone:
        return kb.finalize()


def stage_l1a_post(inputs, prev):
    W_out1 = np.ascontiguousarray(inputs["L1_attn_W_out"])
    wg = np.asarray(inputs["L1_ffn_w_gate"])
    wu = np.asarray(inputs["L1_ffn_w_up"])
    wd = np.asarray(inputs["L1_ffn_w_down"])
    maps = []
    for c in range(NCORE):
        b, h = divmod(c, 2)
        y2 = np.concatenate([np.asarray(prev["o_y"][2 * b])[:, h * TC:(h + 1) * TC],
                             np.asarray(prev["o_y"][2 * b + 1])[:, h * TC:(h + 1) * TC]], 0)
        maps.append(dict(
            y2=np.ascontiguousarray(y2), sz1=prev["o_sz1"][c], W_out1=W_out1,
            x2s=prev["o_x2"][c],
            wg_s=np.ascontiguousarray(wg[c * 352:(c + 1) * 352]),
            wu_s=np.ascontiguousarray(wu[c * 352:(c + 1) * 352]),
            wd_s=np.ascontiguousarray(wd[c * 128:(c + 1) * 128])))
    return maps



FFI = 2816


def build_l1_ffn(kb=None, ext=None):
    standalone = kb is None
    if standalone:
        kb = KB("l1_ffn")

    def _inp(name, shape, dtype=F32):
        return ext[name] if (ext and name in ext) else kb.inp(name, shape, dtype)

    def _out(name, shape, dtype=F32):
        return ext[name] if (ext and name in ext) else kb.out(name, shape, dtype)
    xq = _inp("xq", [TC, D], BF16)
    wgq = _inp("wgq", [FFI, D], BF16)
    wuq = _inp("wuq", [FFI, D], BF16)
    wdq = _inp("wdq", [D, FFI], BF16)
    x3s = _inp("x3s", [TC, D])
    o_x4 = _out("o_x4", [TC, D])

    hq_dram = kb.dram("hq_dram", [TC, FFI], BF16)

    xqT = [kb.sbuf(f"xqT{k}", [128, TC], BF16, side="right") for k in range(8)]
    for kd in range(8):
        kb.dma("sync", xqT[kd][:, :], xq.ap(kd * 128, [[D, TC], [1, 128]]),
               reads=[xq], writes=[xqT[kd]], transpose=True)
    wgT = [kb.sbuf(f"wgT{k}", [128, FFI], BF16, side="right") for k in range(8)]
    wuT = [kb.sbuf(f"wuT{k}", [128, FFI], BF16, side="right") for k in range(8)]
    for kd in range(8):
        kb.dma("sync", wgT[kd][:, :], wgq.ap(kd * 128, [[D, FFI], [1, 128]]),
               reads=[wgq], writes=[wgT[kd]], transpose=True)
        kb.dma("sync", wuT[kd][:, :], wuq.ap(kd * 128, [[D, FFI], [1, 128]]),
               reads=[wuq], writes=[wuT[kd]], transpose=True)

    ps = [kb.psum(f"ps{i}", [128, 512]) for i in range(4)]
    sg = kb.sbuf("sg", [128, FFI], BF16)
    upb = kb.sbuf("upb", [128, FFI], BF16)
    ht = kb.sbuf("ht", [128, FFI])
    hqb = [kb.sbuf(f"hqb{i}", [128, FFI], BF16) for i in range(2)]
    qs = dict(ax=kb.sbuf("q_ax", [128, FFI]), qa=kb.sbuf("q_qa", [128, FFI]),
              tt=kb.sbuf("q_tt", [128, FFI]), rs=kb.sbuf("q_rs", [128, 3 * 176]))
    FCH = [(0, 512), (512, 512), (1024, 512), (1536, 512), (2048, 512), (2560, 256)]
    for j in range(16):
        hqo = hqb[j % 2]
        for (f0, nf) in FCH:
            p = ps[0]
            pairs = [(xqT[kd][:, j * 128:(j + 1) * 128], wgT[kd][:, f0:f0 + nf])
                     for kd in range(8)]
            matmul_group(kb, p, p.h[:, 0:nf], pairs, xqT + wgT)
            kb.op("scalar", lambda q, p=p, f0=f0, nf=nf: q.activation(
                out=sg[:, f0:f0 + nf], in_=p.h[:, 0:nf], func=AF.Silu),
                reads=[p], writes=[sg])
            p2 = ps[1]
            pairs = [(xqT[kd][:, j * 128:(j + 1) * 128], wuT[kd][:, f0:f0 + nf])
                     for kd in range(8)]
            matmul_group(kb, p2, p2.h[:, 0:nf], pairs, xqT + wuT)
            kb.op("vector", lambda q, p2=p2, f0=f0, nf=nf: q.tensor_copy(
                out=upb[:, f0:f0 + nf], in_=p2.h[:, 0:nf]), reads=[p2], writes=[upb])
        kb.op("vector", lambda q: q.tensor_tensor(
            out=ht[:, :], in0=sg[:, :], in1=upb[:, :], op=ALU.mult),
            reads=[sg, upb], writes=[ht])
        e_fp4(kb, ht, ht[:, :], 128, FFI, qs, hqo, hqo[:, :])
        kb.dma("sync", hq_dram.ap(j * 128 * FFI, [[FFI, 128], [1, FFI]]),
               hqo[:, :], reads=[hqo], writes=[hq_dram], stream=f"d_hq{j % 2}")

    kb.free(*xqT, *wgT, *wuT)
    kb.free(sg, upb, ht, hqb[0], hqb[1], qs["ax"], qs["qa"], qs["tt"], qs["rs"])
    hqT = [kb.sbuf(f"hqT{k}", [128, TC], BF16, side="right") for k in range(22)]
    wdT = [kb.sbuf(f"wdT{k}", [128, D], BF16, side="right") for k in range(22)]
    for kf in range(22):
        kb.dma("sync", hqT[kf][:, :], hq_dram.ap(kf * 128, [[FFI, TC], [1, 128]]),
               reads=[hq_dram], writes=[hqT[kf]], transpose=True)
        kb.dma("sync", wdT[kf][:, :], wdq.ap(kf * 128, [[FFI, D], [1, 128]]),
               reads=[wdq], writes=[wdT[kf]], transpose=True)
    x3t = [kb.sbuf(f"x3t{i}", [128, D]) for i in range(2)]
    for j in range(16):
        x3b = x3t[j % 2]
        kb.dma("sync", x3b[:, :], x3s.ap(j * 128 * D, [[D, 128], [1, D]]),
               reads=[x3s], writes=[x3b], stream=f"d_x3{j % 2}")
        for ci in range(2):
            p = ps[2 + ci]
            pairs = [(hqT[kf][:, j * 128:(j + 1) * 128],
                      wdT[kf][:, ci * 512:(ci + 1) * 512]) for kf in range(22)]
            matmul_group(kb, p, p.h[:, 0:512], pairs, hqT + wdT)
            kb.op("vector", lambda q, p=p, x3b=x3b, ci=ci: q.tensor_tensor(
                out=x3b[:, ci * 512:(ci + 1) * 512],
                in0=x3b[:, ci * 512:(ci + 1) * 512],
                in1=p.h[:, 0:512], op=ALU.add), reads=[p], writes=[x3b])
        kb.dma("sync", o_x4.ap(j * 128 * D, [[D, 128], [1, D]]), x3b[:, :],
               reads=[x3b], writes=[o_x4], stream=f"d_ox4{j % 2}")
    if standalone:
        return kb.finalize()


def stage_l1_ffn(inputs, prev):
    wgq = np.concatenate([np.asarray(prev["o_wgq"][c]) for c in range(8)], 0)
    wuq = np.concatenate([np.asarray(prev["o_wuq"][c]) for c in range(8)], 0)
    wdq = np.concatenate([np.asarray(prev["o_wdq"][c]) for c in range(8)], 0)
    maps = []
    for c in range(NCORE):
        maps.append(dict(xq=prev["o_xq"][c], wgq=wgq, wuq=wuq, wdq=wdq,
                         x3s=prev["o_x3"][c]))
    return maps


def kernel(**inputs):
    inputs = {k: np.asarray(v) for k, v in inputs.items()}
    prev = {}

    def runl(build, maps):
        nc = build()
        res = run_bass_kernel_spmd(nc, maps, core_ids=list(range(NCORE)))
        for k in res.results[0]:
            prev[k] = [res.results[c][k] for c in range(NCORE)]

    runl(build_l0a_pre, stage_l0a_pre(inputs))
    runl(build_l0a_scan, stage_l0a_scan(prev))
    runl(build_l0a_post, stage_l0a_post(inputs, prev))
    runl(build_moe_route2, stage_moe_route2(prev))
    runl(build_moe_expert, stage_moe_expert(inputs, prev))
    runl(build_moe_combine, stage_moe_combine(inputs, prev))
    runl(build_l1a_pre, stage_l1a_pre(inputs, prev))
    runl(build_l1a_mid, stage_l1a_mid(inputs, prev))
    runl(build_l1a_scan, stage_l1a_scan(inputs, prev))
    runl(build_l1a_post, stage_l1a_post(inputs, prev))
    runl(build_l1_ffn, stage_l1_ffn(inputs, prev))
    out = np.concatenate([np.asarray(prev["o_x4"][c], np.float32) for c in range(NCORE)], 0)
    return np.ascontiguousarray(out.reshape(4, 4096, 1024))
